# version 09
# baseline (speedup 1.0000x reference)
from collections import defaultdict
from contextlib import ExitStack

import numpy as np

import concourse.bass as bass
import concourse.mybir as mybir
from concourse.bass_utils import run_bass_kernel_spmd

F32 = mybir.dt.float32
BF16 = mybir.dt.bfloat16
AX = mybir.AxisListType.X
OP = mybir.AluOpType
AF = mybir.ActivationFunctionType

NCORES = 8
T = 1024
TT = 8
TH = 2
D = 2048
DC = 16
H = 16
CH = 4
E = 8
HID = 1024
HC = 8
OC = 16
SW = 129

ENGINES = ("tensor", "vector", "scalar", "sync", "gpsimd")


class Sched:

    def __init__(self):
        self.streams = {e: [] for e in ENGINES}
        self.cnt = defaultdict(int)

    def op(self, eng, fn, waits=(), inc=None, n=1, tag=""):
        waits = tuple(w for w in waits if w is not None)
        val = None
        if inc is not None:
            self.cnt[inc] += n
            val = self.cnt[inc]
        self.streams[eng].append((fn, waits, inc, n, tag))
        return (inc, val) if inc else None

    def tok(self, sem):
        return (sem, self.cnt[sem])

    def check(self):
        pos = {e: 0 for e in ENGINES}
        val = defaultdict(int)
        while True:
            progressed = False
            for e in ENGINES:
                while pos[e] < len(self.streams[e]):
                    fn, waits, inc, n, tag = self.streams[e][pos[e]]
                    if any(val[s] < w for s, w in waits):
                        break
                    if inc is not None:
                        val[inc] += n
                    pos[e] += 1
                    progressed = True
            if all(pos[e] == len(self.streams[e]) for e in ENGINES):
                return
            if not progressed:
                stuck = {
                    e: (pos[e], len(self.streams[e]),
                        self.streams[e][pos[e]][1], self.streams[e][pos[e]][4])
                    for e in ENGINES if pos[e] < len(self.streams[e])
                }
                raise RuntimeError(
                    f"schedule deadlock; sems={dict(val)}; stuck={stuck}")

    def emit(self, block, sems):
        water = {e: defaultdict(int) for e in ENGINES}

        def mk(eng_name):
            def body(eng):
                for fn, waits, inc, n, tag in self.streams[eng_name]:
                    for s, w in waits:
                        if w > water[eng_name][s]:
                            eng.wait_ge(sems[s], w)
                            water[eng_name][s] = w
                    ins = fn(eng)
                    if inc is not None:
                        assert ins is not None, f"{tag}: inc op returned None"
                        ins.then_inc(sems[inc], n)
            return body
        return mk


def build_program(upto=99, dump=None, feats=()):
    nc = bass.Bass("TRN2", target_bir_lowering=False,
                   detect_race_conditions=False)

    x_in = nc.dram_tensor("x", [T, D], F32, kind="ExternalInput").ap()
    wqkvT = nc.dram_tensor("wqkvT", [D, 3 * D], F32, kind="ExternalInput").ap()
    woutT = nc.dram_tensor("woutT", [D, D], F32, kind="ExternalInput").ap()
    gwT = nc.dram_tensor("gwT", [D, E], F32, kind="ExternalInput").ap()
    w1T = nc.dram_tensor("w1T", [E, D, HID], F32, kind="ExternalInput").ap()
    w3T = nc.dram_tensor("w3T", [E, D, HID], F32, kind="ExternalInput").ap()
    w2T = nc.dram_tensor("w2T", [E, HID, D], F32, kind="ExternalInput").ap()
    sw1T = nc.dram_tensor("sw1T", [D, HID], F32, kind="ExternalInput").ap()
    sw3T = nc.dram_tensor("sw3T", [D, HID], F32, kind="ExternalInput").ap()
    sw2T = nc.dram_tensor("sw2T", [HID, D], F32, kind="ExternalInput").ap()
    ebias_in = nc.dram_tensor("ebias", [128, E], F32, kind="ExternalInput").ap()
    pmask_in = nc.dram_tensor("pmask", [128, E], F32, kind="ExternalInput").ap()
    cosf_in = nc.dram_tensor("cosf", [128, T], F32, kind="ExternalInput").ap()
    sinf_in = nc.dram_tensor("sinf", [128, T], F32, kind="ExternalInput").ap()
    identf_in = nc.dram_tensor("identf", [128, 128], F32, kind="ExternalInput").ap()
    protT_in = nc.dram_tensor("protT", [128, 128], F32, kind="ExternalInput").ap()
    trilm_in = nc.dram_tensor("trilm", [128, 512], F32, kind="ExternalInput").ap()
    ones_in = nc.dram_tensor("onesin", [128, 128], F32, kind="ExternalInput").ap()
    out_t = nc.dram_tensor("out", [T, D], F32, kind="ExternalOutput").ap()

    agg_in = nc.dram_tensor("agg_in", [H * 128, SW], F32)
    agg_out = nc.dram_tensor("agg_out", [NCORES, H * 128, SW], F32)

    LIMIT = 212608
    arena = nc.alloc_sbuf_tensor("arena", [128, LIMIT], mybir.dt.uint8)
    base = nc.lookup_mloc(arena).addr
    nxt = [0]
    placed = []

    def at(name, shape, dtype, off):
        return nc.alloc_sbuf_tensor_at(name, shape, dtype, offset=base + off)

    def seq(name, shape, dtype):
        off = (nxt[0] + 31) // 32 * 32
        per = int(np.prod(shape[1:])) * mybir.dt.size(dtype)
        nxt[0] = off + per
        assert nxt[0] <= LIMIT, f"SBUF overflow at {name}: {nxt[0]}"
        placed.append((name, off, per))
        return at(name, shape, dtype, off), off

    xn_fm, _ = seq("xn_fm", [128, DC * T], BF16)

    regA, offA = seq("regA", [128, 16384], F32)
    q_fm = at("q_fm", [128, H * T], BF16, offA)
    den_i = at("den_i", [1, H * T], BF16, offA + 32768)
    y_acc = regA

    regB, offB = seq("regB", [128, 8192], F32)
    n_acc = at("n_acc", [128, H * T], BF16, offB)
    logit_sb = at("logit_sb", [E, T], F32, offB)
    score_tm = at("score_tm", [128, TT * E], F32, offB + 4096)
    g_scr = at("g_scr", [128, 3 * E], F32, offB + 4352)
    x_rd = [at(f"x_rd{i}", [128, D], F32, offB + i * 8192) for i in range(2)]
    out_st = [at(f"out_st{i}", [128, D], F32, offB + 16384 + i * 8192)
              for i in range(2)]

    regC, offC = seq("regC", [128, 16384], BF16)
    g_raw = at("g_raw", [128, H * CH * SW], BF16, offC)
    s_agg = at("s_agg", [128, H * SW], F32, offC + 16512)
    lnst = at("lnst", [128, 256], F32, offC + 16512)
    cosf = at("cosf_sb", [128, T], BF16, offC + 24768)
    sinf = at("sinf_sb", [128, T], BF16, offC + 26816)
    xng = regC

    regD, offD = seq("regD", [128, 8192], BF16)
    h_e = regD
    x_t0 = at("x_t0", [128, D], F32, offD)
    xn_scr = at("xn_scr", [128, D], F32, offD + 8192)

    regE, offE = seq("regE", [128, 8256], mybir.dt.uint8)
    x_t1 = at("x_t1", [128, D], F32, offE)
    qr_s = at("qr_s", [128, 512], BF16, offD + 7168)
    rp_s = at("rp_s", [128, 512], BF16, offD + 8192)
    rq_s = at("rq_s", [128, 512], BF16, offD + 9216)
    ex_s = at("ex_s", [128, 512], BF16, offD + 10240)
    rl_s = at("rl_s", [128, 512], BF16, offD + 11264)
    _abase = [offE, offD]
    k_tm = [at(f"k_tm{i}", [128, T], BF16, _abase[i]) for i in range(2)]
    v_tm = [at(f"v_tm{i}", [128, T], BF16, _abase[i] + 2048) for i in range(2)]
    v_fs = [at(f"v_fs{i}", [128, T], BF16, _abase[i] + 4096) for i in range(2)]
    att_s = [at(f"att_s{i}", [128, 512], BF16, _abase[i] + 4096)
             for i in range(2)]
    k_fm = [at(f"k_fm{i}", [128, T], BF16, _abase[i] + 6144) for i in range(2)]
    rank_h = [at(f"rank_h{i}", [128, 8 * SW], F32, offE + i * 4128)
              for i in range(2)]
    sb4 = [at(f"s_bf4{i}", [128, 4 * SW], BF16, offC + 28864 + i * 1056)
           for i in range(2)]
    dn_b = at("dn_b", [1, 1024], BF16, offC + 24768)
    rc_b = at("rc_b", [1, 1024], BF16, offC + 26816)
    gbc = at("gbc", [128, T], BF16, offE)

    _, offW = seq("warena", [128, 8192], BF16)
    wslot = [at(f"wslot{i}", [128, DC * 128], BF16, offW + i * 4096)
             for i in range(4)]

    _, offS = seq("regS", [128, 3072], mybir.dt.uint8)
    trilm = at("trilm", [128, 512], BF16, offS)
    xnf32 = [at(f"xnf32_{i}", [128, 128], F32, offS + 1024 + i * 512)
             for i in range(2)]
    s1_scr = at("s1_scr", [128, 512], BF16, offS)
    t3_scr = at("t3_scr", [128, 512], BF16, offS + 1024)
    at_b = at("at_b", [128, 1024], BF16, offS)
    gwT_sb = at("gwT_sb", [128, DC * E], F32, offS + 2048)
    identf, _ = seq("identf", [128, 128], F32)
    identb, _ = seq("identb", [128, 128], BF16)
    protT, _ = seq("protT", [128, 128], BF16)
    ones_b, _ = seq("ones_b", [128, 1], BF16)
    ones_r, _ = seq("ones_r", [1, 128], BF16)
    ones_f, _ = seq("ones_f", [1, 128], F32)
    ebias_sb, _ = seq("ebias_sb", [128, E], F32)
    pmask_sb, _ = seq("pmask_sb", [128, E], F32)
    g_tm2, _ = seq("g_tm2", [128, TT * E], F32)
    g_stage = at("g_stage", [1, 512], BF16, offS + 2048)
    mu_sc, _ = seq("mu_sc", [128, 8], F32)

    pd = [nc.alloc_psum_tensor(f"pd{i}", [128, 1024], F32) for i in range(4)]
    pb = [pd[i // 2].ap()[:, (i % 2) * 512:(i % 2) * 512 + 512]
          for i in range(8)]

    S = Sched()
    SEMS = ("LX0", "LX1", "LW0", "LW1", "LW2", "LW3", "LC", "LG", "LA",
            "RK0", "RK1", "MM", "EV", "SC", "CC", "ST", "ST1")
    dump_map = {}


    bank_free = [[] for _ in range(8)]
    bank_data = [None] * 8

    def dma(q, sem, out_ap, in_ap, waits=(), tag="", slow=False):
        waits = list(waits)
        if S.cnt[sem]:
            waits.append((sem, S.cnt[sem]))
        def f(eng, o=out_ap, i=in_ap, slow=slow):
            return eng.dma_start(out=o, in_=i,
                                 allow_slow_non_contiguous=slow)
        return S.op(q, f, waits=waits, inc=sem, n=16, tag=tag)

    def mmg(items, b, waits=(), tag=""):
        w = list(waits) + bank_free[b]
        bank_free[b] = []
        n = len(items)
        tok = None
        for j, (o, l, r) in enumerate(items):
            def f(eng, o=o, l=l, r=r, j=j, n=n):
                return eng.matmul(o, l, r, start=(j == 0), stop=(j == n - 1))
            tok = S.op("tensor", f, waits=(w if j == 0 else ()),
                       inc=("MM" if j == n - 1 else None), tag=tag)
        bank_data[b] = tok
        return tok

    def trans(out_ap, in_ap, ident_ap, b, waits=(), tag=""):
        w = list(waits) + bank_free[b]
        bank_free[b] = []
        def f(eng, o=out_ap, i=in_ap, idn=ident_ap):
            return eng.transpose(o, i, idn)
        tok = S.op("tensor", f, waits=w, inc="MM", tag=tag)
        bank_data[b] = tok
        return tok

    def evac(eng_name, fn, b, waits=(), last=True, tag=""):
        w = list(waits)
        if bank_data[b] is not None:
            w.append(bank_data[b])
        sem = {"vector": "EV", "scalar": "SC"}[eng_name]
        tok = S.op(eng_name, fn, waits=w, inc=sem, tag=tag)
        bank_free[b].append(tok)
        return tok

    def vop(fn, waits=(), inc="EV", tag=""):
        return S.op("vector", fn, waits=waits, inc=inc, tag=tag)

    def sop(fn, waits=(), inc="SC", tag=""):
        return S.op("scalar", fn, waits=waits, inc=inc, tag=tag)

    class VChain:

        def __init__(self, tok=None):
            self.tok = tok

        def v(self, fn, waits=(), tag=""):
            self.tok = vop(fn, waits=list(waits) + [self.tok], tag=tag)
            return self.tok

        def s(self, fn, waits=(), tag=""):
            self.tok = sop(fn, waits=list(waits) + [self.tok], tag=tag)
            return self.tok

    dump_map.update(xn=(xn_fm, BF16), q=(q_fm, BF16), nacc=(n_acc, BF16),
                    sagg=(s_agg, F32), yacc=(y_acc, F32), graw=(g_raw, BF16),
                    xnscr=(xn_scr, F32), xt0=(x_t0, F32), mu=(mu_sc, F32),
                    )

    def _finish():
        if dump is not None:
            buf, dt_ = dump_map[dump]
            ncol = buf.shape[1]
            view3 = out_t.rearrange("(o p) d -> p o d", p=128)
            w = [("EV", S.cnt["EV"]), ("SC", S.cnt["SC"]),
                 ("MM", S.cnt["MM"])]
            nfull, rem = divmod(ncol, D)
            if nfull:
                dma("gpsimd", "ST", view3[:, 0:nfull, :],
                    buf.ap()[:, 0:nfull * D].rearrange("p (o d) -> p o d", d=D),
                    waits=w, tag="dump")
            if rem:
                dma("gpsimd", "ST", view3[:, nfull:nfull + 1, 0:rem],
                    buf.ap()[:, nfull * D:ncol].rearrange(
                        "p (o d) -> p o d", o=1),
                    waits=w, tag="dumpr")
        S.op("sync", lambda eng: None, waits=[S.tok("ST")], tag="fin")
        S.check()
        return nc, S, SEMS

    dma("sync", "LC", identf.ap(), identf_in, tag="identf")
    dma("sync", "LC", ebias_sb.ap(), ebias_in, tag="ebias")
    dma("sync", "LC", pmask_sb.ap(), pmask_in, tag="pmask")
    dma("sync", "LC", ones_f.ap(), ones_in[0:1, :], tag="onesf")
    dma("sync", "LC", gwT_sb.ap().rearrange("p (c e) -> p c e", c=DC),
        gwT.rearrange("(c p) e -> p c e", p=128), tag="gwT")
    dma("gpsimd", "LG", identb.ap(), identf_in, tag="identb")
    dma("gpsimd", "LG", protT.ap(), protT_in, tag="protT")
    dma("gpsimd", "LG", trilm.ap(), trilm_in, tag="trilm")
    dma("gpsimd", "LG", ones_b.ap(), ones_in[:, 0:1], tag="onesb", slow=True)
    dma("gpsimd", "LG", ones_r.ap(), ones_in[0:1, :], tag="onesr")
    dma("gpsimd", "LG", cosf.ap(), cosf_in, tag="cosf")
    dma("gpsimd", "LG", sinf.ap(), sinf_in, tag="sinf")
    tok_const_g = S.tok("LG")
    tok_const_s = S.tok("LC")



    x_bufs = [x_t0, x_t1]
    x_tok = [None, None]
    x_free = [None, None]
    for tt in range(2):
        b = tt % 2
        x_tok[b] = dma("sync", f"LX{b}", x_bufs[b].ap(),
                       x_in[tt * 128:(tt + 1) * 128, :], tag=f"xld{tt}")

    gmm_tok = [None, None]
    evx_tok = [None, None]
    prev_tr_tok = None
    NT = 1 if "t1" in feats else TT
    stats_tok = [None] * max(NT, 1)

    def phase_a(tt):
        b = tt % 2
        xb = x_bufs[b]
        this_x_tok = x_tok[b]
        tok = None
        for qq in range(4):
            tok = vop(lambda eng, xb=xb, tt=tt, qq=qq: eng.bn_stats(
                lnst.ap()[:, tt * 24 + qq * 6: tt * 24 + (qq + 1) * 6],
                xb.ap()[:, qq * 512:(qq + 1) * 512]),
                waits=[this_x_tok] if qq == 0 else (), tag=f"lnA{tt}_{qq}")
        stats_tok[tt] = vop(lambda eng, tt=tt: eng.bn_aggr(
            lnst.ap()[:, 192 + 2 * tt: 194 + 2 * tt],
            lnst.ap()[:, tt * 24: (tt + 1) * 24]),
            waits=[tok], tag=f"lnAg{tt}")
        x_free[b] = stats_tok[tt]

    def phase_b(t0, n):
        mv = lnst.ap()[:, 192 + 2 * t0:192 + 2 * (t0 + n)].rearrange(
            "p (t two) -> p t two", two=2)
        var8 = mv[:, :, 1:2]
        ve8 = lnst.ap()[:, 208 + t0:208 + t0 + n]
        rc8 = lnst.ap()[:, 216 + t0:216 + t0 + n]
        rs8 = lnst.ap()[:, 224 + t0:224 + t0 + n]
        n1 = lnst.ap()[:, 232 + t0:232 + t0 + n]
        n2 = lnst.ap()[:, 240 + t0:240 + t0 + n]
        cB = VChain(stats_tok[t0 + n - 1])
        cB.v(lambda eng: eng.tensor_scalar_add(ve8, var8, 1e-5), tag=f"lnBve{t0}")
        cB.v(lambda eng: eng.reciprocal(rc8, ve8), tag=f"lnBrc{t0}")
        cB.s(lambda eng: eng.activation(rs8, rc8, AF.Sqrt), tag=f"lnBsq{t0}")
        cB.v(lambda eng: eng.tensor_tensor(n1, rs8, rs8, OP.mult), tag=f"lnBn1{t0}")
        cB.v(lambda eng: eng.tensor_tensor(n2, ve8, n1, OP.mult), tag=f"lnBn2{t0}")
        cB.v(lambda eng: eng.tensor_scalar(n2, n2, -0.5, 1.5, OP.mult, OP.add),
             tag=f"lnBn3{t0}")
        return cB.v(lambda eng: eng.tensor_tensor(rs8, rs8, n2, OP.mult),
                    tag=f"lnBn4{t0}")

    n0 = min(4, NT)
    for tt in range(n0):
        b = tt % 2
        if tt >= 2:
            x_tok[b] = dma("sync", f"LX{b}", x_bufs[b].ap(),
                           x_in[tt * 128:(tt + 1) * 128, :],
                           waits=[x_free[b]], tag=f"xldA{tt}")
        phase_a(tt)
    tokB = {0: phase_b(0, n0)}
    for tt in range(NT):
        tok_rstd = tokB[tt // 4]
        b = tt % 2
        xb = x_bufs[b]
        this_x_tok = dma("sync", f"LX{b}", xb.ap(),
                         x_in[tt * 128:(tt + 1) * 128, :],
                         waits=[x_free[b]], tag=f"xldC{tt}")
        x_free[b] = vop(lambda eng, xb=xb, tt=tt: eng.tensor_scalar(
            xn_scr.ap(), xb.ap(), lnst.ap()[:, 192 + 2 * tt:193 + 2 * tt],
            lnst.ap()[:, 224 + tt:225 + tt], OP.subtract, OP.mult),
            waits=[this_x_tok, tok_rstd, prev_tr_tok], tag=f"lnXn{tt}")
        xn_ready = x_free[b]

        pend = None
        for dc in range(DC if "notrans" not in feats else 0):
            tb = dc % 2
            xf = dc % 2
            ttok = trans(pb[tb][:, 0:128],
                         xn_scr.ap()[:, dc * 128:(dc + 1) * 128],
                         identf.ap(), tb,
                         waits=[xn_ready, tok_const_s], tag=f"trx{tt}_{dc}")
            sc_tok = evac("scalar",
                          lambda eng, tb=tb, xf=xf: eng.copy(
                              xnf32[xf].ap(), pb[tb][:, 0:128]),
                          tb, waits=[gmm_tok[xf], evx_tok[xf]],
                          tag=f"evf{tt}_{dc}")
            evx_tok[xf] = vop(
                lambda eng, tt=tt, dc=dc, xf=xf: eng.tensor_copy(
                    xn_fm.ap()[:, dc * T + tt * 128: dc * T + (tt + 1) * 128],
                    xnf32[xf].ap()),
                waits=[sc_tok], tag=f"evx{tt}_{dc}")
            if pend is not None and "nogate" not in feats:
                pdc, pxf, ptok = pend
                gb = 6 + (tt // 4)
                col = (tt % 4) * 128
                gmm_tok[pxf] = S.op(
                    "tensor",
                    lambda eng, pdc=pdc, pxf=pxf, gb=gb, col=col: eng.matmul(
                        pb[gb][0:E, col:col + 128],
                        gwT_sb.ap()[:, pdc * E:(pdc + 1) * E],
                        xnf32[pxf].ap(),
                        start=(pdc == 0), stop=(pdc == DC - 1)),
                    waits=[ptok], inc=("MM" if pdc == DC - 1 else None),
                    tag=f"gmm{tt}_{pdc}")
            pend = (dc, xf, sc_tok)
        if "nogate" not in feats and "notrans" not in feats:
          pdc, pxf, ptok = pend
          gb = 6 + (tt // 4)
          col = (tt % 4) * 128
          gmm_tok[pxf] = S.op(
            "tensor",
            lambda eng, pdc=pdc, pxf=pxf, gb=gb, col=col: eng.matmul(
                pb[gb][0:E, col:col + 128],
                gwT_sb.ap()[:, pdc * E:(pdc + 1) * E],
                xnf32[pxf].ap(),
                start=(pdc == 0), stop=(pdc == DC - 1)),
            waits=[ptok], inc=("MM" if pdc == DC - 1 else None),
            tag=f"gmm{tt}_{pdc}")
        if tt == 3 and "nogate" not in feats:
            bank_data[6] = gmm_tok[1]
        if tt == 7 and "nogate" not in feats:
            bank_data[7] = gmm_tok[1]
        prev_tr_tok = S.tok("MM")
        if NT > 4 and tt < 4:
            bA = (tt + 4) % 2
            x_tok[bA] = dma("sync", f"LX{bA}", x_bufs[bA].ap(),
                            x_in[(tt + 4) * 128:(tt + 5) * 128, :],
                            waits=[x_free[bA]], tag=f"xldA{tt + 4}")
            phase_a(tt + 4)
            if tt == 3:
                tokB[1] = phase_b(4, 4)
    tok_xn = {0: None, 1: None}

    tok_xn_all = S.tok("EV")
    tok_msagg = vop(lambda eng: eng.memset(s_agg.ap(), 0.0),
                    waits=[S.tok("MM")], tag="msagg")

    if "nogate" in feats or "notrans" in feats:
        return _finish()
    e0 = evac("vector", lambda eng: eng.tensor_copy(
        logit_sb.ap()[:, 0:512], pb[6][0:E, 0:512]), 6, tag="evg0")
    tok_logits = evac("vector", lambda eng: eng.tensor_copy(
        logit_sb.ap()[:, 512:1024], pb[7][0:E, 0:512]), 7, tag="evg1")

    for tt in range(TT):
        tb = tt % 2
        trans(pb[tb][0:128, 0:E],
              logit_sb.ap()[:, tt * 128:(tt + 1) * 128],
              identf.ap()[0:E, 0:E], tb, waits=[tok_logits], tag=f"trg{tt}")
        evac("vector", lambda eng, tt=tt, tb=tb: eng.tensor_copy(
            score_tm.ap()[:, tt * E:(tt + 1) * E], pb[tb][0:128, 0:E]),
            tb, tag=f"evsc{tt}")
    tok_scores = S.tok("EV")
    tok_sig = sop(lambda eng: eng.activation(
        score_tm.ap(), score_tm.ap(), AF.Sigmoid),
        waits=[tok_scores], tag="sig")
    for tt in range(TT):
        sc_ap = score_tm.ap()[:, tt * E:(tt + 1) * E]
        y_ap = g_scr.ap()[:, 0:E]
        m1_ap = g_scr.ap()[:, E:2 * E]
        y2_ap = g_scr.ap()[:, 2 * E:3 * E]
        mx1 = mu_sc.ap()[:, 0:1]
        mx2 = mu_sc.ap()[:, 1:2]
        c = VChain()
        c.v(lambda eng, sc_ap=sc_ap, y_ap=y_ap: eng.tensor_add(
            y_ap, sc_ap, ebias_sb.ap()), waits=[tok_sig], tag=f"gy{tt}")
        c.v(lambda eng, y_ap=y_ap, mx1=mx1: eng.reduce_max(mx1, y_ap, axis=AX),
            tag=f"gm1{tt}")
        c.v(lambda eng, y_ap=y_ap, m1_ap=m1_ap, mx1=mx1:
            eng.tensor_single_scalar(m1_ap, y_ap, mx1, OP.is_equal),
            tag=f"geq{tt}")
        c.v(lambda eng, m1_ap=m1_ap, y_ap=y_ap, y2_ap=y2_ap:
            eng.scalar_tensor_tensor(y2_ap, m1_ap, -1e30, y_ap,
                                     OP.mult, OP.add), tag=f"gy2{tt}")
        c.v(lambda eng, y2_ap=y2_ap, mx2=mx2: eng.reduce_max(
            mx2, y2_ap, axis=AX), tag=f"gm2{tt}")
        c.v(lambda eng, y_ap=y_ap, m1_ap=m1_ap, mx2=mx2:
            eng.tensor_single_scalar(m1_ap, y_ap, mx2, OP.is_ge),
            tag=f"gmask{tt}")
        c.v(lambda eng, sc_ap=sc_ap, m1_ap=m1_ap, tt=tt: eng.tensor_tensor(
            g_tm2.ap()[:, tt * E:(tt + 1) * E], sc_ap, m1_ap, OP.mult),
            tag=f"gg{tt}")
    tok_gtm = S.tok("EV")

    slot_free = [None] * 4
    slot_rr = [0]

    def wload(panel_ap, n_dc, tag=""):
        s = slot_rr[0]
        slot_rr[0] = (s + 1) % 4
        tok = dma("gpsimd", f"LW{s}",
                  wslot[s].ap()[:, 0:n_dc * 128].rearrange(
                      "p (c n) -> p c n", c=n_dc),
                  panel_ap, waits=[slot_free[s]], tag=f"wl{tag}")
        return s, tok

    def wpanel(src2d, c0, c1, p=128):
        return src2d[:, c0:c1].rearrange("(c p) n -> p c n", p=p)

    rope_prev = {"q2": None, "ex": None}

    def rope_elu(mb, src_cols_th, dst_ap, tag):
        th = src_cols_th
        t0 = th * 512
        qtok = evac("scalar", lambda eng, mb=mb: eng.copy(
            qr_s.ap(), pb[mb][:, 0:512]), mb,
            waits=[rope_prev["q2"]], tag=f"{tag}qr")
        rb = mb
        mmg([(pb[rb][:, 0:512], protT.ap(), qr_s.ap())], rb,
            waits=[qtok], tag=f"{tag}rot")
        ch = VChain()
        ch.v(lambda eng, t0=t0: eng.tensor_tensor(
            rp_s.ap(), qr_s.ap(), cosf.ap()[:, t0:t0 + 512], OP.mult),
            waits=[qtok, tok_const_g, rope_prev["ex"]], tag=f"{tag}t1")
        ch.tok = evac("vector", lambda eng, t0=t0, rb=rb, _p=ch.tok:
                      eng.tensor_tensor(
                          rq_s.ap(), pb[rb][:, 0:512],
                          sinf.ap()[:, t0:t0 + 512], OP.mult),
                      rb, waits=[ch.tok], tag=f"{tag}t2")
        t_qp = ch.v(lambda eng: eng.tensor_add(
            rp_s.ap(), rp_s.ap(), rq_s.ap()), tag=f"{tag}qp")
        t_ex = sop(lambda eng: eng.activation(ex_s.ap(), rp_s.ap(), AF.Exp),
                   waits=[t_qp], tag=f"{tag}ex")
        ch.v(lambda eng: eng.tensor_scalar_max(rl_s.ap(), rp_s.ap(), 0.0),
             tag=f"{tag}rl")
        t_q2 = ch.v(lambda eng, dst_ap=dst_ap: eng.scalar_tensor_tensor(
            dst_ap, ex_s.ap(), 1.0, rl_s.ap(), OP.min, OP.add),
            waits=[t_ex], tag=f"{tag}q2")
        rope_prev["q2"] = t_q2
        rope_prev["ex"] = t_ex
        return t_q2

    if upto < 2:
        return _finish()
    for h in range(H):
        s, wtok = wload(wpanel(wqkvT, h * 128, (h + 1) * 128), DC, tag=f"q{h}")
        last_mm = None
        for th in range(TH):
            mb = 2 + th
            items = [(pb[mb][:, 0:512],
                      wslot[s].ap()[:, dc * 128:(dc + 1) * 128],
                      xn_fm.ap()[:, dc * T + th * 512: dc * T + (th + 1) * 512])
                     for dc in range(DC)]
            last_mm = mmg(items, mb, waits=[wtok, tok_xn_all],
                          tag=f"qmm{h}_{th}")
            rope_elu(mb, th, q_fm.ap()[:, h * T + th * 512: h * T + (th + 1) * 512],
                     tag=f"q{h}_{th}_")
        slot_free[s] = last_mm
    tok_q = S.tok("EV")

    if upto < 3:
        return _finish()
    agg_prev = {}
    for h in range(H):
        par = h % 2
        kf, kt, vt, vf, ats = k_fm[par], k_tm[par], v_tm[par], v_fs[par], att_s[par]
        kcol = 2048 + h * 128
        vcol = 4096 + h * 128
        sk, ktok = wload(wpanel(wqkvT, kcol, kcol + 128), DC, tag=f"k{h}")
        sv, vtok = wload(wpanel(wqkvT, vcol, vcol + 128), DC, tag=f"v{h}")
        last_kmm = None
        for th in range(TH):
            mb = th
            items = [(pb[mb][:, 0:512],
                      wslot[sk].ap()[:, dc * 128:(dc + 1) * 128],
                      xn_fm.ap()[:, dc * T + th * 512: dc * T + (th + 1) * 512])
                     for dc in range(DC)]
            last_kmm = mmg(items, mb, waits=[ktok, tok_xn_all],
                           tag=f"kmm{h}_{th}")
            rope_elu(mb, th, kf.ap()[:, th * 512:(th + 1) * 512],
                     tag=f"k{h}_{th}_")
        slot_free[sk] = last_kmm
        tok_k = S.tok("EV")

        last_vmm = None
        for th in range(TH):
            mb = th
            items = [(pb[mb][:, 0:512],
                      wslot[sv].ap()[:, dc * 128:(dc + 1) * 128],
                      xn_fm.ap()[:, dc * T + th * 512: dc * T + (th + 1) * 512])
                     for dc in range(DC)]
            last_vmm = mmg(items, mb, waits=[vtok, tok_xn_all],
                           tag=f"vmm{h}_{th}")
            evac("scalar", lambda eng, th=th, mb=mb, vf=vf: eng.copy(
                vf.ap()[:, th * 512:(th + 1) * 512], pb[mb][:, 0:512]),
                mb, tag=f"evv{h}_{th}")
        slot_free[sv] = last_vmm
        tok_vfs = S.tok("EV")

        for src_buf, dst_buf, wtk, nm in ((kf, kt, tok_k, "k"),
                                          (vf, vt, tok_vfs, "v")):
            for g in range(2):
                tb = g % 2
                for i in range(4):
                    tt = g * 4 + i
                    trans(pb[tb].bitcast(BF16)[:, i * 128:(i + 1) * 128],
                          src_buf.ap()[:, tt * 128:(tt + 1) * 128],
                          identb.ap(), tb,
                          waits=[wtk] if i == 0 else (),
                          tag=f"tr{nm}{h}_{tt}")
                evac("scalar",
                     lambda eng, g=g, tb=tb, dst_buf=dst_buf: eng.copy(
                         dst_buf.ap()[:, g * 512:(g + 1) * 512],
                         pb[tb].bitcast(BF16)[:, 0:512]),
                     tb, tag=f"ev{nm}t{h}_{g}")
        tok_kv = S.tok("EV")

        for c in range(CH):
            ab = 2 + (c % 2)
            nb = 4 + (c % 2)
            gbk = 6 + (c % 2)

            for ms in range(2):
                mmg([(pb[ab][:, ms * 256:(ms + 1) * 256],
                      kf.ap()[:, c * 256 + ms * 128: c * 256 + (ms + 1) * 128],
                      q_fm.ap()[:, h * T + c * 256: h * T + (c + 1) * 256])],
                    ab, waits=[tok_kv, tok_q], tag=f"att{h}_{c}_{ms}")
            t_att = evac("vector", lambda eng, ab=ab, ats=ats: eng.tensor_tensor(
                ats.ap(), pb[ab][:, 0:512], trilm.ap(), OP.mult),
                ab, tag=f"evatt{h}_{c}")

            nitems = [(pb[nb][:, 0:256],
                       vt.ap()[:, (2 * c + ms) * 128:(2 * c + ms + 1) * 128],
                       ats.ap()[:, ms * 256:(ms + 1) * 256])
                      for ms in range(2)]
            mmg(nitems, nb, waits=[t_att], tag=f"num{h}_{c}")
            t_num_mm = bank_data[nb]
            ditems = [(pb[nb][0:1, 256:512],
                       ones_b.ap(),
                       ats.ap()[:, ms * 256:(ms + 1) * 256])
                      for ms in range(2)]
            for j, (o, l, r) in enumerate(ditems):
                S.op("tensor",
                     lambda eng, o=o, l=l, r=r, j=j: eng.matmul(
                         o, l, r, start=(j == 0), stop=(j == 1)),
                     inc=("MM" if j == 1 else None), tag=f"den{h}_{c}_{j}")
            t_den_mm = S.tok("MM")
            bank_data[nb] = t_den_mm

            evac("scalar", lambda eng, h=h, c=c, nb=nb: eng.copy(
                n_acc.ap()[:, h * T + c * 256: h * T + (c + 1) * 256],
                pb[nb][:, 0:256]), nb, waits=[t_num_mm],
                tag=f"evnum{h}_{c}")
            evac("scalar", lambda eng, h=h, c=c, nb=nb: eng.copy(
                den_i.ap()[0:1, h * T + c * 256: h * T + (c + 1) * 256],
                pb[nb][0:1, 256:512]), nb, tag=f"evden{h}_{c}")

            gitems = [(pb[gbk][:, 0:128],
                       kt.ap()[:, (2 * c + ms) * 128:(2 * c + ms + 1) * 128],
                       vt.ap()[:, (2 * c + ms) * 128:(2 * c + ms + 1) * 128])
                      for ms in range(2)]
            mmg(gitems, gbk, waits=[tok_kv], tag=f"G{h}_{c}")
            zitems = [(pb[gbk][:, 128:129],
                       kt.ap()[:, (2 * c + ms) * 128:(2 * c + ms + 1) * 128],
                       ones_b.ap())
                      for ms in range(2)]
            for j, (o, l, r) in enumerate(zitems):
                S.op("tensor",
                     lambda eng, o=o, l=l, r=r, j=j: eng.matmul(
                         o, l, r, start=(j == 0), stop=(j == 1)),
                     inc=("MM" if j == 1 else None), tag=f"z{h}_{c}_{j}")
            bank_data[gbk] = S.tok("MM")

            agg_prev[h] = vop(lambda eng, h=h, gbk=gbk: eng.scalar_tensor_tensor(
                s_agg.ap()[:, h * SW:(h + 1) * SW],
                pb[gbk][:, 0:129], 1.0,
                s_agg.ap()[:, h * SW:(h + 1) * SW], OP.mult, OP.add),
                waits=[bank_data[gbk], tok_msagg, agg_prev.get(h)],
                tag=f"agg{h}_{c}")
            t_g = evac("vector", lambda eng, h=h, c=c, gbk=gbk: eng.tensor_copy(
                g_raw.ap()[:, (h * CH + c) * SW:(h * CH + c + 1) * SW],
                pb[gbk][:, 0:129]), gbk, tag=f"evG{h}_{c}")
    tok_p3_v = S.tok("EV")
    tok_p3_sc = S.tok("SC")
    tok_p3_mm = S.tok("MM")

    if upto < 4:
        return _finish()
    _pre = {hc: (wload(wpanel(sw1T, hc * 128, (hc + 1) * 128), DC,
                       tag=f"su1_{hc}p"),
                 wload(wpanel(sw3T, hc * 128, (hc + 1) * 128), DC,
                       tag=f"su3_{hc}p"))
            for hc in (0, 1)}
    t_aggd = dma("sync", "LA", agg_in.ap().rearrange("(h p) s -> p h s", p=128),
                 s_agg.ap().rearrange("p (h s) -> p h s", h=H),
                 waits=[agg_prev[H - 1]], tag="aggout")
    def f_cc(eng):
        return eng.collective_compute(
            "AllGather", OP.bypass, replica_groups=[list(range(NCORES))],
            ins=[agg_in.ap().opt()], outs=[agg_out.ap().opt()])
    tok_cc = S.op("gpsimd", f_cc, waits=[t_aggd], inc="CC", tag="cc")

    uh_prev = {"h": tok_p3_v}

    def u_phase(hcp, xsrc, xtok, m1src, m3src, tagp, preload=None):
        group = 0 if hcp % 2 == 0 else 4
        for hc in (2 * hcp, 2 * hcp + 1):
            if preload is not None and hc in preload:
                (s1s, w1tok), (s3s, w3tok) = preload[hc]
            else:
                s1s, w1tok = wload(wpanel(m1src, hc * 128, (hc + 1) * 128), DC,
                                   tag=f"{tagp}u1_{hc}")
                s3s, w3tok = wload(wpanel(m3src, hc * 128, (hc + 1) * 128), DC,
                                   tag=f"{tagp}u3_{hc}")
            for th in range(TH):
                b1 = group + th
                b3 = group + 2 + th
                items1 = [(pb[b1][:, 0:512],
                           wslot[s1s].ap()[:, dc * 128:(dc + 1) * 128],
                           xsrc.ap()[:, dc * T + th * 512: dc * T + (th + 1) * 512])
                          for dc in range(DC)]
                mmg(items1, b1, waits=[w1tok, xtok], tag=f"{tagp}u1mm{hc}_{th}")
                items3 = [(pb[b3][:, 0:512],
                           wslot[s3s].ap()[:, dc * 128:(dc + 1) * 128],
                           xsrc.ap()[:, dc * T + th * 512: dc * T + (th + 1) * 512])
                          for dc in range(DC)]
                mmg(items3, b3, waits=[w3tok, xtok], tag=f"{tagp}u3mm{hc}_{th}")
                t_s1 = evac("scalar", lambda eng, b1=b1: eng.activation(
                    s1_scr.ap(), pb[b1][:, 0:512], AF.Silu), b1,
                    waits=[uh_prev["h"]], tag=f"{tagp}s1{hc}_{th}")
                t_t3 = evac("scalar", lambda eng, b3=b3: eng.copy(
                    t3_scr.ap(), pb[b3][:, 0:512]), b3,
                    tag=f"{tagp}t3{hc}_{th}")
                uh_prev["h"] = vop(lambda eng, hc=hc, th=th: eng.tensor_tensor(
                    h_e.ap()[:, hc * T + th * 512: hc * T + (th + 1) * 512],
                    s1_scr.ap(), t3_scr.ap(), OP.mult),
                    waits=[t_s1, t_t3], tag=f"{tagp}h{hc}_{th}")
            slot_free[s1s] = S.tok("MM")
            slot_free[s3s] = S.tok("MM")

    t_clr = vop(lambda eng: eng.memset(s_agg.ap(), 0.0),
                waits=[t_aggd], tag="clr_sagg")
    rk = {"i": 0, "acc": [t_clr, t_clr], "free": [t_clr, t_clr]}

    def plan_rank_batch(n):
        for _ in range(n):
            i = rk["i"]
            if i >= 2 * NCORES:
                return
            rk["i"] += 1
            r, hh = i // 2, i % 2
            rtok = dma("sync", "RK0",
                       rank_h[hh].ap().rearrange("p (h s) -> p h s", h=8),
                       agg_out[r][hh * 1024:(hh + 1) * 1024].rearrange(
                           "(h p) s -> p h s", p=128),
                       waits=[tok_cc, rk["free"][hh]], tag=f"rk{r}_{hh}")
            rk["acc"][hh] = vop(lambda eng, r=r, hh=hh:
                                eng.scalar_tensor_tensor(
                s_agg.ap()[:, hh * 8 * SW:(hh + 1) * 8 * SW],
                rank_h[hh].ap(), pmask_sb.ap()[:, r:r + 1],
                s_agg.ap()[:, hh * 8 * SW:(hh + 1) * 8 * SW],
                OP.mult, OP.add),
                waits=[rtok, rk["acc"][hh]], tag=f"rka{r}_{hh}")
            rk["free"][hh] = rk["acc"][hh]

    for hcp in range(HC // 2):
        u_phase(hcp, xn_fm, tok_xn_all, sw1T, sw3T, "su",
                preload=_pre if hcp == 0 else None)
        plan_rank_batch(4)
    tok_hsh = S.tok("EV")

    acc_half = rk["acc"]
    prev_cast_mm = [None, None]
    prev_num_rd = [None, None]
    prev_den_rd = [None, None]
    prev_dbc = [None, None]
    init_free = [list(bank_free[i]) for i in range(8)]
    for i in range(8):
        bank_free[i] = []
    ones1 = ones_b.ap()[0:1, 0:1]

    def plan_chainA(h):
        p = h % 2
        ca = VChain()
        cast_toks = []
        for c in range(CH):
            w = [acc_half[h // 8], prev_cast_mm[p]] if c == 0 else []
            cast_toks.append(ca.v(
                lambda eng, h=h, c=c, p=p: eng.tensor_copy(
                    sb4[p].ap()[:, c * SW:(c + 1) * SW],
                    s_agg.ap()[:, h * SW:(h + 1) * SW]),
                waits=w, tag=f"cs{h}_{c}"))
            if c < CH - 1:
                ca.v(lambda eng, h=h, c=c: eng.tensor_tensor(
                    s_agg.ap()[:, h * SW:(h + 1) * SW],
                    s_agg.ap()[:, h * SW:(h + 1) * SW],
                    g_raw.ap()[:, (h * CH + c) * SW:(h * CH + c + 1) * SW],
                    OP.add), tag=f"sr{h}_{c}")
        return cast_toks

    all_casts = {0: plan_chainA(0)}
    for h in range(H):
        p = h % 2
        pdN = pd[p].ap()
        pdD = pd[2 + p].ap()
        q_h = q_fm.ap()[:, h * T:(h + 1) * T]
        cast_toks = all_casts[h]
        for c in range(CH):
            wN = [cast_toks[c], prev_num_rd[p]] + (init_free[2 * p] +
                 init_free[2 * p + 1] if c == 0 and h < 2 else [])
            S.op("tensor", lambda eng, h=h, c=c, p=p, pdN=pdN, q_h=q_h:
                 eng.matmul(pdN[:, c * 256:(c + 1) * 256],
                            sb4[p].ap()[:, c * SW:c * SW + 128],
                            q_h[:, c * 256:(c + 1) * 256],
                            start=True, stop=True),
                 waits=wN, inc="MM", tag=f"xn{h}_{c}")
            wD = [prev_den_rd[p], tok_p3_sc] + (init_free[4 + 2 * p] +
                 init_free[5 + 2 * p] if c == 0 and h < 2 else [])
            S.op("tensor", lambda eng, h=h, c=c, pdD=pdD: eng.matmul(
                pdD[0:1, c * 256:(c + 1) * 256], ones1,
                den_i.ap()[0:1, h * T + c * 256: h * T + (c + 1) * 256],
                start=True, stop=False),
                waits=wD, tag=f"xdi{h}_{c}")
            S.op("tensor", lambda eng, h=h, c=c, p=p, pdD=pdD, q_h=q_h:
                 eng.matmul(pdD[0:1, c * 256:(c + 1) * 256],
                            sb4[p].ap()[:, c * SW + 128:c * SW + 129],
                            q_h[:, c * 256:(c + 1) * 256],
                            start=False, stop=True),
                 inc="MM", tag=f"xdz{h}_{c}")
        t_mms = S.tok("MM")
        prev_cast_mm[p] = t_mms
        if h + 1 < H:
            all_casts[h + 1] = plan_chainA(h + 1)
        t_mx = vop(lambda eng, pdD=pdD: eng.tensor_scalar_max(
            dn_b.ap(), pdD[0:1, 0:1024], 1e-6),
            waits=[t_mms, prev_dbc[p]], tag=f"mx{h}")
        def f_rc(eng):
            with eng.bass.allow_low_precision(
                    reason="bf16 den/recip: within calibrated tolerance"):
                return eng.reciprocal(rc_b.ap(), dn_b.ap())
        t_rc = vop(f_rc, waits=[t_mx], tag=f"rc{h}")
        dbc_toks = []
        for half in range(2):
            tok = S.op("tensor", lambda eng, half=half, pdD=pdD: eng.matmul(
                pdD[:, half * 512:(half + 1) * 512], ones_r.ap(),
                rc_b.ap()[0:1, half * 512:(half + 1) * 512],
                start=True, stop=True),
                waits=[t_rc], inc="MM", tag=f"dbc{h}_{half}")
            dbc_toks.append(tok)
        prev_dbc[p] = dbc_toks[-1]
        t_nt = vop(lambda eng, h=h, pdN=pdN: eng.scalar_tensor_tensor(
            at_b.ap(), pdN[:, 0:1024], 1.0,
            n_acc.ap()[:, h * T:(h + 1) * T], OP.mult, OP.add),
            waits=[t_mms], tag=f"nt{h}")
        prev_num_rd[p] = t_nt
        t_ao = vop(lambda eng, h=h, pdD=pdD: eng.tensor_tensor(
            n_acc.ap()[:, h * T:(h + 1) * T], at_b.ap(), pdD[:, 0:1024],
            OP.mult), waits=[dbc_toks[-1], t_nt], tag=f"ao{h}")
        prev_den_rd[p] = t_ao
    for p in range(2):
        bank_free[2 * p].append(prev_num_rd[p])
        bank_free[2 * p + 1].append(prev_num_rd[p])
        bank_free[4 + 2 * p].append(prev_den_rd[p])
        bank_free[5 + 2 * p].append(prev_den_rd[p])
    tok_attn = S.tok("EV")

    if upto < 5:
        return _finish()
    y_banks = [0, 1, 2, 3]
    for oc in range(OC):
        s, wtok = wload(wpanel(woutT, oc * 128, (oc + 1) * 128), DC,
                        tag=f"wo{oc}")
        last_mm = None
        for th in range(TH):
            yb = y_banks[(oc * TH + th) % 4]
            items = [(pb[yb][:, 0:512],
                      wslot[s].ap()[:, hh * 128:(hh + 1) * 128],
                      n_acc.ap()[:, hh * T + th * 512: hh * T + (th + 1) * 512])
                     for hh in range(H)]
            last_mm = mmg(items, yb, waits=[wtok, tok_attn], tag=f"womm{oc}_{th}")
            evac("vector", lambda eng, oc=oc, th=th, yb=yb: eng.tensor_copy(
                y_acc.ap()[:, oc * T + th * 512: oc * T + (th + 1) * 512],
                pb[yb][:, 0:512]), yb, tag=f"evwo{oc}_{th}")
        slot_free[s] = last_mm
    tok_y0 = S.tok("EV")

    if upto < 6:
        return _finish()
    for oc in range(OC):
        s, wtok = wload(wpanel(sw2T, oc * 128, (oc + 1) * 128), HC,
                        tag=f"sw2_{oc}")
        last_mm = None
        for th in range(TH):
            yb = y_banks[(oc * TH + th) % 4]
            items = [(pb[yb][:, 0:512],
                      wslot[s].ap()[:, hc * 128:(hc + 1) * 128],
                      h_e.ap()[:, hc * T + th * 512: hc * T + (th + 1) * 512])
                     for hc in range(HC)]
            last_mm = mmg(items, yb, waits=[wtok, tok_hsh, tok_y0],
                          tag=f"symm{oc}_{th}")
            evac("vector", lambda eng, oc=oc, th=th, yb=yb: eng.tensor_tensor(
                y_acc.ap()[:, oc * T + th * 512: oc * T + (th + 1) * 512],
                y_acc.ap()[:, oc * T + th * 512: oc * T + (th + 1) * 512],
                pb[yb][:, 0:512], OP.add), yb, tag=f"evsy{oc}_{th}")
        slot_free[s] = last_mm

    if upto < 7:
        return _finish()
    prev_gbc_gate = [tok_attn]

    def plan_gbc_xng(e, extra_wait):
        gb_toks = []
        for th in range(TH):
            gband = 6 + th
            for i, tt in enumerate(range(th * 4, th * 4 + 4)):
                trans(pb[gband][0:1, i * 128:(i + 1) * 128],
                      g_tm2.ap()[:, tt * E + e: tt * E + e + 1],
                      identf.ap(), gband, waits=[tok_gtm], tag=f"gtr{e}_{tt}")
            t_gs = evac("vector", lambda eng, gband=gband: eng.tensor_copy(
                g_stage.ap(), pb[gband][0:1, 0:512]), gband,
                waits=[prev_gbc_gate[0]], tag=f"gst{e}_{th}")
            bb = 6 + th
            mmg([(pb[bb][:, 0:512], ones_r.ap(), g_stage.ap())], bb,
                waits=[t_gs], tag=f"gbb{e}_{th}")
            gb_toks.append(evac("vector",
                                lambda eng, th=th, bb=bb: eng.tensor_copy(
                                    gbc.ap()[:, th * 512:(th + 1) * 512],
                                    pb[bb][:, 0:512]), bb,
                                waits=[prev_gbc_gate[0]], tag=f"gbe{e}_{th}"))
        xng_tok = None
        for dc in range(DC):
            xng_tok = vop(lambda eng, dc=dc: eng.tensor_tensor(
                xng.ap()[:, dc * T:(dc + 1) * T],
                xn_fm.ap()[:, dc * T:(dc + 1) * T], gbc.ap(), OP.mult),
                waits=(gb_toks + [extra_wait]) if dc == 0 else (),
                tag=f"xng{e}_{dc}")
        prev_gbc_gate[0] = xng_tok
        return xng_tok

    xng_tok = plan_gbc_xng(0, None)
    for e in range(E):
        for hcp in range(HC // 2):
            u_phase(hcp, xng, xng_tok, w1T[e], w3T[e], f"e{e}")
        tok_umm = S.tok("MM")
        tok_he = S.tok("EV")
        if e + 1 < E:
            xng_tok = plan_gbc_xng(e + 1, tok_umm)
        for oc in range(OC):
            s, wtok = wload(wpanel(w2T[e], oc * 128, (oc + 1) * 128), HC,
                            tag=f"e{e}w2_{oc}")
            last_mm = None
            for th in range(TH):
                yb = y_banks[(oc * TH + th) % 4]
                items = [(pb[yb][:, 0:512],
                          wslot[s].ap()[:, hc * 128:(hc + 1) * 128],
                          h_e.ap()[:, hc * T + th * 512: hc * T + (th + 1) * 512])
                         for hc in range(HC)]
                last_mm = mmg(items, yb, waits=[wtok, tok_he],
                              tag=f"e{e}ymm{oc}_{th}")
                evac("vector", lambda eng, oc=oc, th=th, yb=yb: eng.tensor_tensor(
                    y_acc.ap()[:, oc * T + th * 512: oc * T + (th + 1) * 512],
                    y_acc.ap()[:, oc * T + th * 512: oc * T + (th + 1) * 512],
                    pb[yb][:, 0:512], OP.add), yb, tag=f"e{e}ev{oc}_{th}")
            slot_free[s] = last_mm
    tok_yfull = S.tok("EV")

    if upto < 8:
        return _finish()
    tok_mm_p8 = S.tok("MM")
    st_tok = [None, None]
    xr_tok = [None, None]
    xr_free = [None, None]
    for tt in range(TT):
        b = tt % 2
        xr_tok[b] = dma("scalar", f"LX{b}", x_rd[b].ap(),
                        x_in[tt * 128:(tt + 1) * 128, :],
                        waits=[xr_free[b], tok_mm_p8], tag=f"xrd{tt}")
        last_add = None
        for g in range(4):
            tb = g % 4
            for i in range(4):
                oc = g * 4 + i
                trans(pb[tb][:, i * 128:(i + 1) * 128],
                      y_acc.ap()[:, oc * T + tt * 128: oc * T + (tt + 1) * 128],
                      identf.ap(), tb, waits=[tok_yfull], tag=f"try{tt}_{oc}")
            last_add = evac("vector",
                            lambda eng, g=g, tb=tb, b=b: eng.tensor_tensor(
                                out_st[b].ap()[:, g * 512:(g + 1) * 512],
                                pb[tb][:, 0:512],
                                x_rd[b].ap()[:, g * 512:(g + 1) * 512],
                                OP.add),
                            tb, waits=[xr_tok[b], st_tok[b]],
                            tag=f"oadd{tt}_{g}")
        xr_free[b] = last_add
        st_tok[b] = dma("sync", "ST" if b == 0 else "ST1",
                        out_t[tt * 128:(tt + 1) * 128, :],
                        out_st[b].ap(), waits=[last_add], tag=f"ost{tt}")

    S.op("sync", lambda eng: None, waits=[S.tok("ST"), S.tok("ST1")],
         tag="fin")

    S.check()
    return nc, S, SEMS


_CACHE = {}


def _host_constants():
    inv = 1.0 / (10000.0 ** (np.arange(0, 128, 2, dtype=np.float64) / 128.0))
    invf = np.concatenate([inv, inv])
    ident = np.eye(128, dtype=np.float32)
    protT = np.zeros((128, 128), dtype=np.float32)
    for i in range(64):
        protT[2 * i + 1, 2 * i] = -1.0
        protT[2 * i, 2 * i + 1] = 1.0
    trilm = np.zeros((128, 512), dtype=np.float32)
    for s_ in range(2):
        for p in range(128):
            for l in range(256):
                if s_ * 128 + p <= l:
                    trilm[p, s_ * 256 + l] = 1.0
    ones = np.ones((128, 128), dtype=np.float32)
    return invf, ident, protT, trilm, ones


def kernel(L0_attn_x, L0_attn_W_qkv, L0_attn_W_out, L0_ffn_expert_bias,
           L0_ffn_gate_w, L0_ffn_w1, L0_ffn_w2, L0_ffn_w3, L0_ffn_sw1,
           L0_ffn_sw2, L0_ffn_sw3):
    x = np.ascontiguousarray(np.asarray(L0_attn_x, dtype=np.float32)
                             .reshape(NCORES * T, D))
    wqkvT = np.ascontiguousarray(np.asarray(L0_attn_W_qkv, np.float32).T)
    woutT = np.ascontiguousarray(np.asarray(L0_attn_W_out, np.float32).T)
    gwT = np.ascontiguousarray(np.asarray(L0_ffn_gate_w, np.float32).T)
    w1T = np.ascontiguousarray(np.asarray(L0_ffn_w1, np.float32)
                               .transpose(0, 2, 1))
    w3T = np.ascontiguousarray(np.asarray(L0_ffn_w3, np.float32)
                               .transpose(0, 2, 1))
    w2T = np.ascontiguousarray(np.asarray(L0_ffn_w2, np.float32)
                               .transpose(0, 2, 1))
    sw1T = np.ascontiguousarray(np.asarray(L0_ffn_sw1, np.float32).T)
    sw3T = np.ascontiguousarray(np.asarray(L0_ffn_sw3, np.float32).T)
    sw2T = np.ascontiguousarray(np.asarray(L0_ffn_sw2, np.float32).T)
    ebias = np.ascontiguousarray(
        np.broadcast_to(np.asarray(L0_ffn_expert_bias, np.float32), (128, E)))

    invf, ident, protT, trilm, ones = _host_constants()

    if "nc" not in _CACHE:
        nc, S, SEMS = build_program()
        _CACHE["nc"] = nc
        _CACHE["S"] = S
        _CACHE["SEMS"] = SEMS
    nc, S, SEMS = _CACHE["nc"], _CACHE["S"], _CACHE["SEMS"]

    in_maps = []
    for c in range(NCORES):
        pos = np.arange(c * T, (c + 1) * T, dtype=np.float64)
        ang = pos[None, :] * invf[:, None]
        cosf = np.ascontiguousarray(np.cos(ang).astype(np.float32))
        sinf = np.ascontiguousarray(np.sin(ang).astype(np.float32))
        pmask = np.zeros((128, E), np.float32)
        pmask[:, :c] = 1.0
        in_maps.append({
            "x": x[c * T:(c + 1) * T],
            "wqkvT": wqkvT, "woutT": woutT, "gwT": gwT,
            "w1T": w1T, "w3T": w3T, "w2T": w2T,
            "sw1T": sw1T, "sw3T": sw3T, "sw2T": sw2T,
            "ebias": ebias, "pmask": pmask,
            "cosf": cosf, "sinf": sinf,
            "identf": ident, "protT": protT, "trilm": trilm, "onesin": ones,
        })

    res = _run(nc, S, SEMS, in_maps)
    out = np.concatenate([res[c]["out"] for c in range(NCORES)], axis=0)
    return out.reshape(1, NCORES * T, D).astype(np.float32)


def _run(nc, S, SEMS, in_maps):
    if "emitted" not in _CACHE:
        with ExitStack() as ctx:
            sems = {s: ctx.enter_context(nc.semaphore(s)) for s in SEMS}
            block = ctx.enter_context(nc.Block())
            mk = S.emit(block, sems)
            block.tensor(mk("tensor"))
            block.vector(mk("vector"))
            block.scalar(mk("scalar"))
            block.sync(mk("sync"))
            block.gpsimd(mk("gpsimd"))
        _CACHE["emitted"] = True
    res = run_bass_kernel_spmd(nc, in_maps, core_ids=list(range(NCORES)))
    return res.results


# version 10
# speedup vs baseline: 1.0065x; 1.0065x over previous
from collections import defaultdict
from contextlib import ExitStack

import numpy as np

import concourse.bass as bass
import concourse.mybir as mybir
from concourse.bass_utils import run_bass_kernel_spmd

F32 = mybir.dt.float32
BF16 = mybir.dt.bfloat16
AX = mybir.AxisListType.X
OP = mybir.AluOpType
AF = mybir.ActivationFunctionType

NCORES = 8
T = 1024
TT = 8
TH = 2
D = 2048
DC = 16
H = 16
CH = 4
E = 8
HID = 1024
HC = 8
OC = 16
SW = 129

ENGINES = ("tensor", "vector", "scalar", "sync", "gpsimd")


class Sched:

    def __init__(self):
        self.streams = {e: [] for e in ENGINES}
        self.cnt = defaultdict(int)

    def op(self, eng, fn, waits=(), inc=None, n=1, tag=""):
        waits = tuple(w for w in waits if w is not None)
        val = None
        if inc is not None:
            self.cnt[inc] += n
            val = self.cnt[inc]
        self.streams[eng].append((fn, waits, inc, n, tag))
        return (inc, val) if inc else None

    def tok(self, sem):
        return (sem, self.cnt[sem])

    def check(self):
        pos = {e: 0 for e in ENGINES}
        val = defaultdict(int)
        while True:
            progressed = False
            for e in ENGINES:
                while pos[e] < len(self.streams[e]):
                    fn, waits, inc, n, tag = self.streams[e][pos[e]]
                    if any(val[s] < w for s, w in waits):
                        break
                    if inc is not None:
                        val[inc] += n
                    pos[e] += 1
                    progressed = True
            if all(pos[e] == len(self.streams[e]) for e in ENGINES):
                return
            if not progressed:
                stuck = {
                    e: (pos[e], len(self.streams[e]),
                        self.streams[e][pos[e]][1], self.streams[e][pos[e]][4])
                    for e in ENGINES if pos[e] < len(self.streams[e])
                }
                raise RuntimeError(
                    f"schedule deadlock; sems={dict(val)}; stuck={stuck}")

    def emit(self, block, sems):
        water = {e: defaultdict(int) for e in ENGINES}

        def mk(eng_name):
            def body(eng):
                for fn, waits, inc, n, tag in self.streams[eng_name]:
                    for s, w in waits:
                        if w > water[eng_name][s]:
                            eng.wait_ge(sems[s], w)
                            water[eng_name][s] = w
                    ins = fn(eng)
                    if inc is not None:
                        assert ins is not None, f"{tag}: inc op returned None"
                        ins.then_inc(sems[inc], n)
            return body
        return mk


def build_program(upto=99, dump=None, feats=()):
    nc = bass.Bass("TRN2", target_bir_lowering=False,
                   detect_race_conditions=False)

    x_in = nc.dram_tensor("x", [T, D], F32, kind="ExternalInput").ap()
    wqkvT = nc.dram_tensor("wqkvT", [D, 3 * D], F32, kind="ExternalInput").ap()
    woutT = nc.dram_tensor("woutT", [D, D], F32, kind="ExternalInput").ap()
    gwT = nc.dram_tensor("gwT", [D, E], F32, kind="ExternalInput").ap()
    w1T = nc.dram_tensor("w1T", [E, D, HID], F32, kind="ExternalInput").ap()
    w3T = nc.dram_tensor("w3T", [E, D, HID], F32, kind="ExternalInput").ap()
    w2T = nc.dram_tensor("w2T", [E, HID, D], F32, kind="ExternalInput").ap()
    sw1T = nc.dram_tensor("sw1T", [D, HID], F32, kind="ExternalInput").ap()
    sw3T = nc.dram_tensor("sw3T", [D, HID], F32, kind="ExternalInput").ap()
    sw2T = nc.dram_tensor("sw2T", [HID, D], F32, kind="ExternalInput").ap()
    ebias_in = nc.dram_tensor("ebias", [128, E], F32, kind="ExternalInput").ap()
    pmask_in = nc.dram_tensor("pmask", [128, E], F32, kind="ExternalInput").ap()
    cosf_in = nc.dram_tensor("cosf", [128, T], F32, kind="ExternalInput").ap()
    sinf_in = nc.dram_tensor("sinf", [128, T], F32, kind="ExternalInput").ap()
    identf_in = nc.dram_tensor("identf", [128, 128], F32, kind="ExternalInput").ap()
    protT_in = nc.dram_tensor("protT", [128, 128], F32, kind="ExternalInput").ap()
    trilm_in = nc.dram_tensor("trilm", [128, 512], F32, kind="ExternalInput").ap()
    ones_in = nc.dram_tensor("onesin", [128, 128], F32, kind="ExternalInput").ap()
    out_t = nc.dram_tensor("out", [T, D], F32, kind="ExternalOutput").ap()

    agg_in = nc.dram_tensor("agg_in", [H * 128, SW], F32)
    agg_out = nc.dram_tensor("agg_out", [NCORES, H * 128, SW], F32)

    LIMIT = 212608
    arena = nc.alloc_sbuf_tensor("arena", [128, LIMIT], mybir.dt.uint8)
    base = nc.lookup_mloc(arena).addr
    nxt = [0]
    placed = []

    def at(name, shape, dtype, off):
        return nc.alloc_sbuf_tensor_at(name, shape, dtype, offset=base + off)

    def seq(name, shape, dtype):
        off = (nxt[0] + 31) // 32 * 32
        per = int(np.prod(shape[1:])) * mybir.dt.size(dtype)
        nxt[0] = off + per
        assert nxt[0] <= LIMIT, f"SBUF overflow at {name}: {nxt[0]}"
        placed.append((name, off, per))
        return at(name, shape, dtype, off), off

    xn_fm, _ = seq("xn_fm", [128, DC * T], BF16)

    regA, offA = seq("regA", [128, 16384], F32)
    q_fm = at("q_fm", [128, H * T], BF16, offA)
    den_i = at("den_i", [1, H * T], BF16, offA + 32768)
    y_acc = regA

    regB, offB = seq("regB", [128, 8192], F32)
    n_acc = at("n_acc", [128, H * T], BF16, offB)
    logit_sb = at("logit_sb", [E, T], F32, offB)
    score_tm = at("score_tm", [128, TT * E], F32, offB + 4096)
    g_scr = at("g_scr", [128, 3 * E], F32, offB + 4352)
    x_rd = [at(f"x_rd{i}", [128, D], F32, offB + i * 8192) for i in range(2)]
    out_st = [at(f"out_st{i}", [128, D], F32, offB + 16384 + i * 8192)
              for i in range(2)]

    regC, offC = seq("regC", [128, 16384], BF16)
    g_raw = at("g_raw", [128, H * CH * SW], BF16, offC)
    s_agg = at("s_agg", [128, H * SW], F32, offC + 16512)
    lnst = at("lnst", [128, 256], F32, offC + 16512)
    cosf = at("cosf_sb", [128, T], BF16, offC + 24768)
    sinf = at("sinf_sb", [128, T], BF16, offC + 26816)
    xng = regC

    regD, offD = seq("regD", [128, 8192], BF16)
    h_e = regD
    x_t0 = at("x_t0", [128, D], F32, offD)
    xn_scr = at("xn_scr", [128, D], F32, offD + 8192)

    regE, offE = seq("regE", [128, 8256], mybir.dt.uint8)
    x_t1 = at("x_t1", [128, D], F32, offE)
    qr_s = at("qr_s", [128, 512], BF16, offD + 7168)
    rp_s = at("rp_s", [128, 512], BF16, offD + 8192)
    rq_s = at("rq_s", [128, 512], BF16, offD + 9216)
    ex_s = at("ex_s", [128, 512], BF16, offD + 10240)
    rl_s = at("rl_s", [128, 512], BF16, offD + 11264)
    _abase = [offE, offD]
    k_tm = [at(f"k_tm{i}", [128, T], BF16, _abase[i]) for i in range(2)]
    v_tm = [at(f"v_tm{i}", [128, T], BF16, _abase[i] + 2048) for i in range(2)]
    v_fs = [at(f"v_fs{i}", [128, T], BF16, _abase[i] + 4096) for i in range(2)]
    att_s = [at(f"att_s{i}", [128, 512], BF16, _abase[i] + 4096)
             for i in range(2)]
    k_fm = [at(f"k_fm{i}", [128, T], BF16, _abase[i] + 6144) for i in range(2)]
    rank_h = [at(f"rank_h{i}", [128, 8 * SW], F32, offE + i * 4128)
              for i in range(2)]
    sb4 = [at(f"s_bf4{i}", [128, 4 * SW], BF16, offC + 28864 + i * 1056)
           for i in range(2)]
    dn_b = at("dn_b", [1, 1024], BF16, offC + 24768)
    rc_b = at("rc_b", [1, 1024], BF16, offC + 26816)
    gbc = at("gbc", [128, T], BF16, offE)

    _, offW = seq("warena", [128, 8192], BF16)
    wslot = [at(f"wslot{i}", [128, DC * 128], BF16, offW + i * 4096)
             for i in range(4)]

    _, offS = seq("regS", [128, 3072], mybir.dt.uint8)
    trilm = at("trilm", [128, 512], BF16, offS)
    xnf32 = [at(f"xnf32_{i}", [128, 128], F32, offS + 1024 + i * 512)
             for i in range(2)]
    s1_scr = at("s1_scr", [128, 512], BF16, offS)
    t3_scr = at("t3_scr", [128, 512], BF16, offS + 1024)
    at_b = [at("at_b0", [128, 1024], BF16, offS),
            at("at_b1", [128, 1024], BF16, offE)]
    gwT_sb = at("gwT_sb", [128, DC * E], F32, offS + 2048)
    identf, _ = seq("identf", [128, 128], F32)
    identb, _ = seq("identb", [128, 128], BF16)
    protT, _ = seq("protT", [128, 128], BF16)
    ones_b, _ = seq("ones_b", [128, 1], BF16)
    ones_r, _ = seq("ones_r", [1, 128], BF16)
    ones_f, _ = seq("ones_f", [1, 128], F32)
    ebias_sb, _ = seq("ebias_sb", [128, E], F32)
    pmask_sb, _ = seq("pmask_sb", [128, E], F32)
    g_tm2, _ = seq("g_tm2", [128, TT * E], F32)
    g_stage = at("g_stage", [1, 512], BF16, offS + 2048)
    mu_sc, _ = seq("mu_sc", [128, 8], F32)

    pd = [nc.alloc_psum_tensor(f"pd{i}", [128, 1024], F32) for i in range(4)]
    pb = [pd[i // 2].ap()[:, (i % 2) * 512:(i % 2) * 512 + 512]
          for i in range(8)]

    S = Sched()
    SEMS = ("LX0", "LX1", "LW0", "LW1", "LW2", "LW3", "LC", "LG", "LA",
            "RK0", "RK1", "MM", "EV", "SC", "CC", "ST", "ST1")
    dump_map = {}


    bank_free = [[] for _ in range(8)]
    bank_data = [None] * 8

    def dma(q, sem, out_ap, in_ap, waits=(), tag="", slow=False):
        waits = list(waits)
        if S.cnt[sem]:
            waits.append((sem, S.cnt[sem]))
        def f(eng, o=out_ap, i=in_ap, slow=slow):
            return eng.dma_start(out=o, in_=i,
                                 allow_slow_non_contiguous=slow)
        return S.op(q, f, waits=waits, inc=sem, n=16, tag=tag)

    def mmg(items, b, waits=(), tag=""):
        w = list(waits) + bank_free[b]
        bank_free[b] = []
        n = len(items)
        tok = None
        for j, (o, l, r) in enumerate(items):
            def f(eng, o=o, l=l, r=r, j=j, n=n):
                return eng.matmul(o, l, r, start=(j == 0), stop=(j == n - 1))
            tok = S.op("tensor", f, waits=(w if j == 0 else ()),
                       inc=("MM" if j == n - 1 else None), tag=tag)
        bank_data[b] = tok
        return tok

    def trans(out_ap, in_ap, ident_ap, b, waits=(), tag=""):
        w = list(waits) + bank_free[b]
        bank_free[b] = []
        def f(eng, o=out_ap, i=in_ap, idn=ident_ap):
            return eng.transpose(o, i, idn)
        tok = S.op("tensor", f, waits=w, inc="MM", tag=tag)
        bank_data[b] = tok
        return tok

    def evac(eng_name, fn, b, waits=(), last=True, tag=""):
        w = list(waits)
        if bank_data[b] is not None:
            w.append(bank_data[b])
        sem = {"vector": "EV", "scalar": "SC"}[eng_name]
        tok = S.op(eng_name, fn, waits=w, inc=sem, tag=tag)
        bank_free[b].append(tok)
        return tok

    def vop(fn, waits=(), inc="EV", tag=""):
        return S.op("vector", fn, waits=waits, inc=inc, tag=tag)

    def sop(fn, waits=(), inc="SC", tag=""):
        return S.op("scalar", fn, waits=waits, inc=inc, tag=tag)

    class VChain:

        def __init__(self, tok=None):
            self.tok = tok

        def v(self, fn, waits=(), tag=""):
            self.tok = vop(fn, waits=list(waits) + [self.tok], tag=tag)
            return self.tok

        def s(self, fn, waits=(), tag=""):
            self.tok = sop(fn, waits=list(waits) + [self.tok], tag=tag)
            return self.tok

    dump_map.update(xn=(xn_fm, BF16), q=(q_fm, BF16), nacc=(n_acc, BF16),
                    sagg=(s_agg, F32), yacc=(y_acc, F32), graw=(g_raw, BF16),
                    xnscr=(xn_scr, F32), xt0=(x_t0, F32), mu=(mu_sc, F32),
                    )

    def _finish():
        if dump is not None:
            buf, dt_ = dump_map[dump]
            ncol = buf.shape[1]
            view3 = out_t.rearrange("(o p) d -> p o d", p=128)
            w = [("EV", S.cnt["EV"]), ("SC", S.cnt["SC"]),
                 ("MM", S.cnt["MM"])]
            nfull, rem = divmod(ncol, D)
            if nfull:
                dma("gpsimd", "ST", view3[:, 0:nfull, :],
                    buf.ap()[:, 0:nfull * D].rearrange("p (o d) -> p o d", d=D),
                    waits=w, tag="dump")
            if rem:
                dma("gpsimd", "ST", view3[:, nfull:nfull + 1, 0:rem],
                    buf.ap()[:, nfull * D:ncol].rearrange(
                        "p (o d) -> p o d", o=1),
                    waits=w, tag="dumpr")
        S.op("sync", lambda eng: None, waits=[S.tok("ST")], tag="fin")
        S.check()
        return nc, S, SEMS

    dma("sync", "LC", identf.ap(), identf_in, tag="identf")
    dma("sync", "LC", ebias_sb.ap(), ebias_in, tag="ebias")
    dma("sync", "LC", pmask_sb.ap(), pmask_in, tag="pmask")
    dma("sync", "LC", ones_f.ap(), ones_in[0:1, :], tag="onesf")
    dma("sync", "LC", gwT_sb.ap().rearrange("p (c e) -> p c e", c=DC),
        gwT.rearrange("(c p) e -> p c e", p=128), tag="gwT")
    dma("gpsimd", "LG", identb.ap(), identf_in, tag="identb")
    dma("gpsimd", "LG", protT.ap(), protT_in, tag="protT")
    dma("gpsimd", "LG", trilm.ap(), trilm_in, tag="trilm")
    dma("gpsimd", "LG", ones_b.ap(), ones_in[:, 0:1], tag="onesb", slow=True)
    dma("gpsimd", "LG", ones_r.ap(), ones_in[0:1, :], tag="onesr")
    dma("gpsimd", "LG", cosf.ap(), cosf_in, tag="cosf")
    dma("gpsimd", "LG", sinf.ap(), sinf_in, tag="sinf")
    tok_const_g = S.tok("LG")
    tok_const_s = S.tok("LC")



    x_bufs = [x_t0, x_t1]
    x_tok = [None, None]
    x_free = [None, None]
    for tt in range(2):
        b = tt % 2
        x_tok[b] = dma("sync", f"LX{b}", x_bufs[b].ap(),
                       x_in[tt * 128:(tt + 1) * 128, :], tag=f"xld{tt}")

    gmm_tok = [None, None]
    evx_tok = [None, None]
    prev_tr_tok = None
    NT = 1 if "t1" in feats else TT
    stats_tok = [None] * max(NT, 1)

    def phase_a(tt):
        b = tt % 2
        xb = x_bufs[b]
        this_x_tok = x_tok[b]
        tok = None
        for qq in range(4):
            tok = vop(lambda eng, xb=xb, tt=tt, qq=qq: eng.bn_stats(
                lnst.ap()[:, tt * 24 + qq * 6: tt * 24 + (qq + 1) * 6],
                xb.ap()[:, qq * 512:(qq + 1) * 512]),
                waits=[this_x_tok] if qq == 0 else (), tag=f"lnA{tt}_{qq}")
        stats_tok[tt] = vop(lambda eng, tt=tt: eng.bn_aggr(
            lnst.ap()[:, 192 + 2 * tt: 194 + 2 * tt],
            lnst.ap()[:, tt * 24: (tt + 1) * 24]),
            waits=[tok], tag=f"lnAg{tt}")
        x_free[b] = stats_tok[tt]

    def phase_b(t0, n):
        mv = lnst.ap()[:, 192 + 2 * t0:192 + 2 * (t0 + n)].rearrange(
            "p (t two) -> p t two", two=2)
        var8 = mv[:, :, 1:2]
        ve8 = lnst.ap()[:, 208 + t0:208 + t0 + n]
        rc8 = lnst.ap()[:, 216 + t0:216 + t0 + n]
        rs8 = lnst.ap()[:, 224 + t0:224 + t0 + n]
        n1 = lnst.ap()[:, 232 + t0:232 + t0 + n]
        n2 = lnst.ap()[:, 240 + t0:240 + t0 + n]
        cB = VChain(stats_tok[t0 + n - 1])
        cB.v(lambda eng: eng.tensor_scalar_add(ve8, var8, 1e-5), tag=f"lnBve{t0}")
        cB.v(lambda eng: eng.reciprocal(rc8, ve8), tag=f"lnBrc{t0}")
        cB.s(lambda eng: eng.activation(rs8, rc8, AF.Sqrt), tag=f"lnBsq{t0}")
        cB.v(lambda eng: eng.tensor_tensor(n1, rs8, rs8, OP.mult), tag=f"lnBn1{t0}")
        cB.v(lambda eng: eng.tensor_tensor(n2, ve8, n1, OP.mult), tag=f"lnBn2{t0}")
        cB.v(lambda eng: eng.tensor_scalar(n2, n2, -0.5, 1.5, OP.mult, OP.add),
             tag=f"lnBn3{t0}")
        return cB.v(lambda eng: eng.tensor_tensor(rs8, rs8, n2, OP.mult),
                    tag=f"lnBn4{t0}")

    n0 = min(4, NT)
    for tt in range(n0):
        b = tt % 2
        if tt >= 2:
            x_tok[b] = dma("sync", f"LX{b}", x_bufs[b].ap(),
                           x_in[tt * 128:(tt + 1) * 128, :],
                           waits=[x_free[b]], tag=f"xldA{tt}")
        phase_a(tt)
    tokB = {0: phase_b(0, n0)}
    for tt in range(NT):
        tok_rstd = tokB[tt // 4]
        b = tt % 2
        xb = x_bufs[b]
        this_x_tok = dma("sync", f"LX{b}", xb.ap(),
                         x_in[tt * 128:(tt + 1) * 128, :],
                         waits=[x_free[b]], tag=f"xldC{tt}")
        x_free[b] = vop(lambda eng, xb=xb, tt=tt: eng.tensor_scalar(
            xn_scr.ap(), xb.ap(), lnst.ap()[:, 192 + 2 * tt:193 + 2 * tt],
            lnst.ap()[:, 224 + tt:225 + tt], OP.subtract, OP.mult),
            waits=[this_x_tok, tok_rstd, prev_tr_tok], tag=f"lnXn{tt}")
        xn_ready = x_free[b]

        pend = None
        for dc in range(DC if "notrans" not in feats else 0):
            tb = dc % 2
            xf = dc % 2
            ttok = trans(pb[tb][:, 0:128],
                         xn_scr.ap()[:, dc * 128:(dc + 1) * 128],
                         identf.ap(), tb,
                         waits=[xn_ready, tok_const_s], tag=f"trx{tt}_{dc}")
            sc_tok = evac("scalar",
                          lambda eng, tb=tb, xf=xf: eng.copy(
                              xnf32[xf].ap(), pb[tb][:, 0:128]),
                          tb, waits=[gmm_tok[xf], evx_tok[xf]],
                          tag=f"evf{tt}_{dc}")
            evx_tok[xf] = vop(
                lambda eng, tt=tt, dc=dc, xf=xf: eng.tensor_copy(
                    xn_fm.ap()[:, dc * T + tt * 128: dc * T + (tt + 1) * 128],
                    xnf32[xf].ap()),
                waits=[sc_tok], tag=f"evx{tt}_{dc}")
            if pend is not None and "nogate" not in feats:
                pdc, pxf, ptok = pend
                gb = 6 + (tt // 4)
                col = (tt % 4) * 128
                gmm_tok[pxf] = S.op(
                    "tensor",
                    lambda eng, pdc=pdc, pxf=pxf, gb=gb, col=col: eng.matmul(
                        pb[gb][0:E, col:col + 128],
                        gwT_sb.ap()[:, pdc * E:(pdc + 1) * E],
                        xnf32[pxf].ap(),
                        start=(pdc == 0), stop=(pdc == DC - 1)),
                    waits=[ptok], inc=("MM" if pdc == DC - 1 else None),
                    tag=f"gmm{tt}_{pdc}")
            pend = (dc, xf, sc_tok)
        if "nogate" not in feats and "notrans" not in feats:
          pdc, pxf, ptok = pend
          gb = 6 + (tt // 4)
          col = (tt % 4) * 128
          gmm_tok[pxf] = S.op(
            "tensor",
            lambda eng, pdc=pdc, pxf=pxf, gb=gb, col=col: eng.matmul(
                pb[gb][0:E, col:col + 128],
                gwT_sb.ap()[:, pdc * E:(pdc + 1) * E],
                xnf32[pxf].ap(),
                start=(pdc == 0), stop=(pdc == DC - 1)),
            waits=[ptok], inc=("MM" if pdc == DC - 1 else None),
            tag=f"gmm{tt}_{pdc}")
        if tt == 3 and "nogate" not in feats:
            bank_data[6] = gmm_tok[1]
        if tt == 7 and "nogate" not in feats:
            bank_data[7] = gmm_tok[1]
        prev_tr_tok = S.tok("MM")
        if NT > 4 and tt < 4:
            bA = (tt + 4) % 2
            x_tok[bA] = dma("sync", f"LX{bA}", x_bufs[bA].ap(),
                            x_in[(tt + 4) * 128:(tt + 5) * 128, :],
                            waits=[x_free[bA]], tag=f"xldA{tt + 4}")
            phase_a(tt + 4)
            if tt == 3:
                tokB[1] = phase_b(4, 4)
    tok_xn = {0: None, 1: None}

    tok_xn_all = S.tok("EV")
    tok_msagg = vop(lambda eng: eng.memset(s_agg.ap(), 0.0),
                    waits=[S.tok("MM")], tag="msagg")

    if "nogate" in feats or "notrans" in feats:
        return _finish()
    e0 = evac("vector", lambda eng: eng.tensor_copy(
        logit_sb.ap()[:, 0:512], pb[6][0:E, 0:512]), 6, tag="evg0")
    tok_logits = evac("vector", lambda eng: eng.tensor_copy(
        logit_sb.ap()[:, 512:1024], pb[7][0:E, 0:512]), 7, tag="evg1")

    for tt in range(TT):
        tb = tt % 2
        trans(pb[tb][0:128, 0:E],
              logit_sb.ap()[:, tt * 128:(tt + 1) * 128],
              identf.ap()[0:E, 0:E], tb, waits=[tok_logits], tag=f"trg{tt}")
        evac("vector", lambda eng, tt=tt, tb=tb: eng.tensor_copy(
            score_tm.ap()[:, tt * E:(tt + 1) * E], pb[tb][0:128, 0:E]),
            tb, tag=f"evsc{tt}")
    tok_scores = S.tok("EV")
    tok_sig = sop(lambda eng: eng.activation(
        score_tm.ap(), score_tm.ap(), AF.Sigmoid),
        waits=[tok_scores], tag="sig")
    for tt in range(TT):
        sc_ap = score_tm.ap()[:, tt * E:(tt + 1) * E]
        y_ap = g_scr.ap()[:, 0:E]
        m1_ap = g_scr.ap()[:, E:2 * E]
        y2_ap = g_scr.ap()[:, 2 * E:3 * E]
        mx1 = mu_sc.ap()[:, 0:1]
        mx2 = mu_sc.ap()[:, 1:2]
        c = VChain()
        c.v(lambda eng, sc_ap=sc_ap, y_ap=y_ap: eng.tensor_add(
            y_ap, sc_ap, ebias_sb.ap()), waits=[tok_sig], tag=f"gy{tt}")
        c.v(lambda eng, y_ap=y_ap, mx1=mx1: eng.reduce_max(mx1, y_ap, axis=AX),
            tag=f"gm1{tt}")
        c.v(lambda eng, y_ap=y_ap, m1_ap=m1_ap, mx1=mx1:
            eng.tensor_single_scalar(m1_ap, y_ap, mx1, OP.is_equal),
            tag=f"geq{tt}")
        c.v(lambda eng, m1_ap=m1_ap, y_ap=y_ap, y2_ap=y2_ap:
            eng.scalar_tensor_tensor(y2_ap, m1_ap, -1e30, y_ap,
                                     OP.mult, OP.add), tag=f"gy2{tt}")
        c.v(lambda eng, y2_ap=y2_ap, mx2=mx2: eng.reduce_max(
            mx2, y2_ap, axis=AX), tag=f"gm2{tt}")
        c.v(lambda eng, y_ap=y_ap, m1_ap=m1_ap, mx2=mx2:
            eng.tensor_single_scalar(m1_ap, y_ap, mx2, OP.is_ge),
            tag=f"gmask{tt}")
        c.v(lambda eng, sc_ap=sc_ap, m1_ap=m1_ap, tt=tt: eng.tensor_tensor(
            g_tm2.ap()[:, tt * E:(tt + 1) * E], sc_ap, m1_ap, OP.mult),
            tag=f"gg{tt}")
    tok_gtm = S.tok("EV")

    slot_free = [None] * 4
    slot_rr = [0]

    def wload(panel_ap, n_dc, tag=""):
        s = slot_rr[0]
        slot_rr[0] = (s + 1) % 4
        tok = dma("gpsimd", f"LW{s}",
                  wslot[s].ap()[:, 0:n_dc * 128].rearrange(
                      "p (c n) -> p c n", c=n_dc),
                  panel_ap, waits=[slot_free[s]], tag=f"wl{tag}")
        return s, tok

    def wpanel(src2d, c0, c1, p=128):
        return src2d[:, c0:c1].rearrange("(c p) n -> p c n", p=p)

    rope_prev = {"q2": None, "ex": None}

    def rope_elu(mb, src_cols_th, dst_ap, tag):
        th = src_cols_th
        t0 = th * 512
        qtok = evac("scalar", lambda eng, mb=mb: eng.copy(
            qr_s.ap(), pb[mb][:, 0:512]), mb,
            waits=[rope_prev["q2"]], tag=f"{tag}qr")
        rb = mb
        mmg([(pb[rb][:, 0:512], protT.ap(), qr_s.ap())], rb,
            waits=[qtok], tag=f"{tag}rot")
        ch = VChain()
        ch.v(lambda eng, t0=t0: eng.tensor_tensor(
            rp_s.ap(), qr_s.ap(), cosf.ap()[:, t0:t0 + 512], OP.mult),
            waits=[qtok, tok_const_g, rope_prev["ex"]], tag=f"{tag}t1")
        ch.tok = evac("vector", lambda eng, t0=t0, rb=rb, _p=ch.tok:
                      eng.tensor_tensor(
                          rq_s.ap(), pb[rb][:, 0:512],
                          sinf.ap()[:, t0:t0 + 512], OP.mult),
                      rb, waits=[ch.tok], tag=f"{tag}t2")
        t_qp = ch.v(lambda eng: eng.tensor_add(
            rp_s.ap(), rp_s.ap(), rq_s.ap()), tag=f"{tag}qp")
        t_ex = sop(lambda eng: eng.activation(ex_s.ap(), rp_s.ap(), AF.Exp),
                   waits=[t_qp], tag=f"{tag}ex")
        ch.v(lambda eng: eng.tensor_scalar_max(rl_s.ap(), rp_s.ap(), 0.0),
             tag=f"{tag}rl")
        t_q2 = ch.v(lambda eng, dst_ap=dst_ap: eng.scalar_tensor_tensor(
            dst_ap, ex_s.ap(), 1.0, rl_s.ap(), OP.min, OP.add),
            waits=[t_ex], tag=f"{tag}q2")
        rope_prev["q2"] = t_q2
        rope_prev["ex"] = t_ex
        return t_q2

    if upto < 2:
        return _finish()
    for h in range(H):
        s, wtok = wload(wpanel(wqkvT, h * 128, (h + 1) * 128), DC, tag=f"q{h}")
        last_mm = None
        for th in range(TH):
            mb = 2 + th
            items = [(pb[mb][:, 0:512],
                      wslot[s].ap()[:, dc * 128:(dc + 1) * 128],
                      xn_fm.ap()[:, dc * T + th * 512: dc * T + (th + 1) * 512])
                     for dc in range(DC)]
            last_mm = mmg(items, mb, waits=[wtok, tok_xn_all],
                          tag=f"qmm{h}_{th}")
            rope_elu(mb, th, q_fm.ap()[:, h * T + th * 512: h * T + (th + 1) * 512],
                     tag=f"q{h}_{th}_")
        slot_free[s] = last_mm
    tok_q = S.tok("EV")

    if upto < 3:
        return _finish()
    agg_prev = {}
    for h in range(H):
        par = h % 2
        kf, kt, vt, vf, ats = k_fm[par], k_tm[par], v_tm[par], v_fs[par], att_s[par]
        kcol = 2048 + h * 128
        vcol = 4096 + h * 128
        sk, ktok = wload(wpanel(wqkvT, kcol, kcol + 128), DC, tag=f"k{h}")
        sv, vtok = wload(wpanel(wqkvT, vcol, vcol + 128), DC, tag=f"v{h}")
        last_kmm = None
        for th in range(TH):
            mb = th
            items = [(pb[mb][:, 0:512],
                      wslot[sk].ap()[:, dc * 128:(dc + 1) * 128],
                      xn_fm.ap()[:, dc * T + th * 512: dc * T + (th + 1) * 512])
                     for dc in range(DC)]
            last_kmm = mmg(items, mb, waits=[ktok, tok_xn_all],
                           tag=f"kmm{h}_{th}")
            rope_elu(mb, th, kf.ap()[:, th * 512:(th + 1) * 512],
                     tag=f"k{h}_{th}_")
        slot_free[sk] = last_kmm
        tok_k = S.tok("EV")

        last_vmm = None
        for th in range(TH):
            mb = th
            items = [(pb[mb][:, 0:512],
                      wslot[sv].ap()[:, dc * 128:(dc + 1) * 128],
                      xn_fm.ap()[:, dc * T + th * 512: dc * T + (th + 1) * 512])
                     for dc in range(DC)]
            last_vmm = mmg(items, mb, waits=[vtok, tok_xn_all],
                           tag=f"vmm{h}_{th}")
            evac("scalar", lambda eng, th=th, mb=mb, vf=vf: eng.copy(
                vf.ap()[:, th * 512:(th + 1) * 512], pb[mb][:, 0:512]),
                mb, tag=f"evv{h}_{th}")
        slot_free[sv] = last_vmm
        tok_vfs = S.tok("EV")

        for src_buf, dst_buf, wtk, nm in ((kf, kt, tok_k, "k"),
                                          (vf, vt, tok_vfs, "v")):
            for g in range(2):
                tb = g % 2
                for i in range(4):
                    tt = g * 4 + i
                    trans(pb[tb].bitcast(BF16)[:, i * 128:(i + 1) * 128],
                          src_buf.ap()[:, tt * 128:(tt + 1) * 128],
                          identb.ap(), tb,
                          waits=[wtk] if i == 0 else (),
                          tag=f"tr{nm}{h}_{tt}")
                evac("scalar",
                     lambda eng, g=g, tb=tb, dst_buf=dst_buf: eng.copy(
                         dst_buf.ap()[:, g * 512:(g + 1) * 512],
                         pb[tb].bitcast(BF16)[:, 0:512]),
                     tb, tag=f"ev{nm}t{h}_{g}")
        tok_kv = S.tok("EV")

        for c in range(CH):
            ab = 2 + (c % 2)
            nb = 4 + (c % 2)
            gbk = 6 + (c % 2)

            for ms in range(2):
                mmg([(pb[ab][:, ms * 256:(ms + 1) * 256],
                      kf.ap()[:, c * 256 + ms * 128: c * 256 + (ms + 1) * 128],
                      q_fm.ap()[:, h * T + c * 256: h * T + (c + 1) * 256])],
                    ab, waits=[tok_kv, tok_q], tag=f"att{h}_{c}_{ms}")
            t_att = evac("vector", lambda eng, ab=ab, ats=ats: eng.tensor_tensor(
                ats.ap(), pb[ab][:, 0:512], trilm.ap(), OP.mult),
                ab, tag=f"evatt{h}_{c}")

            nitems = [(pb[nb][:, 0:256],
                       vt.ap()[:, (2 * c + ms) * 128:(2 * c + ms + 1) * 128],
                       ats.ap()[:, ms * 256:(ms + 1) * 256])
                      for ms in range(2)]
            mmg(nitems, nb, waits=[t_att], tag=f"num{h}_{c}")
            t_num_mm = bank_data[nb]
            ditems = [(pb[nb][0:1, 256:512],
                       ones_b.ap(),
                       ats.ap()[:, ms * 256:(ms + 1) * 256])
                      for ms in range(2)]
            for j, (o, l, r) in enumerate(ditems):
                S.op("tensor",
                     lambda eng, o=o, l=l, r=r, j=j: eng.matmul(
                         o, l, r, start=(j == 0), stop=(j == 1)),
                     inc=("MM" if j == 1 else None), tag=f"den{h}_{c}_{j}")
            t_den_mm = S.tok("MM")
            bank_data[nb] = t_den_mm

            evac("scalar", lambda eng, h=h, c=c, nb=nb: eng.copy(
                n_acc.ap()[:, h * T + c * 256: h * T + (c + 1) * 256],
                pb[nb][:, 0:256]), nb, waits=[t_num_mm],
                tag=f"evnum{h}_{c}")
            evac("scalar", lambda eng, h=h, c=c, nb=nb: eng.copy(
                den_i.ap()[0:1, h * T + c * 256: h * T + (c + 1) * 256],
                pb[nb][0:1, 256:512]), nb, tag=f"evden{h}_{c}")

            gitems = [(pb[gbk][:, 0:128],
                       kt.ap()[:, (2 * c + ms) * 128:(2 * c + ms + 1) * 128],
                       vt.ap()[:, (2 * c + ms) * 128:(2 * c + ms + 1) * 128])
                      for ms in range(2)]
            mmg(gitems, gbk, waits=[tok_kv], tag=f"G{h}_{c}")
            zitems = [(pb[gbk][:, 128:129],
                       kt.ap()[:, (2 * c + ms) * 128:(2 * c + ms + 1) * 128],
                       ones_b.ap())
                      for ms in range(2)]
            for j, (o, l, r) in enumerate(zitems):
                S.op("tensor",
                     lambda eng, o=o, l=l, r=r, j=j: eng.matmul(
                         o, l, r, start=(j == 0), stop=(j == 1)),
                     inc=("MM" if j == 1 else None), tag=f"z{h}_{c}_{j}")
            bank_data[gbk] = S.tok("MM")

            agg_prev[h] = vop(lambda eng, h=h, gbk=gbk: eng.scalar_tensor_tensor(
                s_agg.ap()[:, h * SW:(h + 1) * SW],
                pb[gbk][:, 0:129], 1.0,
                s_agg.ap()[:, h * SW:(h + 1) * SW], OP.mult, OP.add),
                waits=[bank_data[gbk], tok_msagg, agg_prev.get(h)],
                tag=f"agg{h}_{c}")
            t_g = evac("vector", lambda eng, h=h, c=c, gbk=gbk: eng.tensor_copy(
                g_raw.ap()[:, (h * CH + c) * SW:(h * CH + c + 1) * SW],
                pb[gbk][:, 0:129]), gbk, tag=f"evG{h}_{c}")
    tok_p3_v = S.tok("EV")
    tok_p3_sc = S.tok("SC")
    tok_p3_mm = S.tok("MM")

    if upto < 4:
        return _finish()
    _pre = {hc: (wload(wpanel(sw1T, hc * 128, (hc + 1) * 128), DC,
                       tag=f"su1_{hc}p"),
                 wload(wpanel(sw3T, hc * 128, (hc + 1) * 128), DC,
                       tag=f"su3_{hc}p"))
            for hc in (0, 1)}
    uh_prev = {"h": tok_p3_v}

    def u_phase(hcp, xsrc, xtok, m1src, m3src, tagp, preload=None):
        group = 0 if hcp % 2 == 0 else 4
        for hc in (2 * hcp, 2 * hcp + 1):
            if preload is not None and hc in preload:
                (s1s, w1tok), (s3s, w3tok) = preload[hc]
            else:
                s1s, w1tok = wload(wpanel(m1src, hc * 128, (hc + 1) * 128), DC,
                                   tag=f"{tagp}u1_{hc}")
                s3s, w3tok = wload(wpanel(m3src, hc * 128, (hc + 1) * 128), DC,
                                   tag=f"{tagp}u3_{hc}")
            for th in range(TH):
                b1 = group + th
                b3 = group + 2 + th
                items1 = [(pb[b1][:, 0:512],
                           wslot[s1s].ap()[:, dc * 128:(dc + 1) * 128],
                           xsrc.ap()[:, dc * T + th * 512: dc * T + (th + 1) * 512])
                          for dc in range(DC)]
                mmg(items1, b1, waits=[w1tok, xtok], tag=f"{tagp}u1mm{hc}_{th}")
                items3 = [(pb[b3][:, 0:512],
                           wslot[s3s].ap()[:, dc * 128:(dc + 1) * 128],
                           xsrc.ap()[:, dc * T + th * 512: dc * T + (th + 1) * 512])
                          for dc in range(DC)]
                mmg(items3, b3, waits=[w3tok, xtok], tag=f"{tagp}u3mm{hc}_{th}")
                t_s1 = evac("scalar", lambda eng, b1=b1: eng.activation(
                    s1_scr.ap(), pb[b1][:, 0:512], AF.Silu), b1,
                    waits=[uh_prev["h"]], tag=f"{tagp}s1{hc}_{th}")
                t_t3 = evac("scalar", lambda eng, b3=b3: eng.copy(
                    t3_scr.ap(), pb[b3][:, 0:512]), b3,
                    tag=f"{tagp}t3{hc}_{th}")
                uh_prev["h"] = vop(lambda eng, hc=hc, th=th: eng.tensor_tensor(
                    h_e.ap()[:, hc * T + th * 512: hc * T + (th + 1) * 512],
                    s1_scr.ap(), t3_scr.ap(), OP.mult),
                    waits=[t_s1, t_t3], tag=f"{tagp}h{hc}_{th}")
            slot_free[s1s] = S.tok("MM")
            slot_free[s3s] = S.tok("MM")

    rk = {"i": 0, "acc": [None, None], "free": [None, None]}
    ccbox = {}

    def plan_rank_batch(n):
        for _ in range(n):
            i = rk["i"]
            if i >= 2 * NCORES:
                return
            rk["i"] += 1
            r, hh = i // 2, i % 2
            rtok = dma("sync", "RK0",
                       rank_h[hh].ap().rearrange("p (h s) -> p h s", h=8),
                       agg_out[r][hh * 1024:(hh + 1) * 1024].rearrange(
                           "(h p) s -> p h s", p=128),
                       waits=[ccbox["cc"], rk["free"][hh]], tag=f"rk{r}_{hh}")
            rk["acc"][hh] = vop(lambda eng, r=r, hh=hh:
                                eng.scalar_tensor_tensor(
                s_agg.ap()[:, hh * 8 * SW:(hh + 1) * 8 * SW],
                rank_h[hh].ap(), pmask_sb.ap()[:, r:r + 1],
                s_agg.ap()[:, hh * 8 * SW:(hh + 1) * 8 * SW],
                OP.mult, OP.add),
                waits=[rtok, rk["acc"][hh]], tag=f"rka{r}_{hh}")
            rk["free"][hh] = rk["acc"][hh]

    for hcp in range(HC // 2):
        u_phase(hcp, xn_fm, tok_xn_all, sw1T, sw3T, "su",
                preload=_pre if hcp == 0 else None)
        if hcp == 1:
            t_aggd = dma("sync", "LA",
                         agg_in.ap().rearrange("(h p) s -> p h s", p=128),
                         s_agg.ap().rearrange("p (h s) -> p h s", h=H),
                         waits=[agg_prev[H - 1]], tag="aggout")

            def f_cc(eng):
                return eng.collective_compute(
                    "AllGather", OP.bypass,
                    replica_groups=[list(range(NCORES))],
                    ins=[agg_in.ap().opt()], outs=[agg_out.ap().opt()])
            ccbox["cc"] = S.op("gpsimd", f_cc, waits=[t_aggd], inc="CC",
                               tag="cc")
            t_clr = vop(lambda eng: eng.memset(s_agg.ap(), 0.0),
                        waits=[t_aggd], tag="clr_sagg")
            rk["acc"] = [t_clr, t_clr]
            rk["free"] = [t_clr, t_clr]
        if hcp >= 2:
            plan_rank_batch(8)
    tok_hsh = S.tok("EV")

    acc_half = rk["acc"]
    prev_cast_mm = [None, None]
    prev_num_rd = [None, None]
    prev_den_rd = [None, None]
    prev_dbc = [None, None]
    init_free = [list(bank_free[i]) for i in range(8)]
    for i in range(8):
        bank_free[i] = []
    ones1 = ones_b.ap()[0:1, 0:1]

    def plan_chainA(h):
        p = h % 2
        ca = VChain()
        cast_toks = []
        for c in range(CH):
            w = [acc_half[h // 8], prev_cast_mm[p]] if c == 0 else []
            cast_toks.append(ca.v(
                lambda eng, h=h, c=c, p=p: eng.tensor_copy(
                    sb4[p].ap()[:, c * SW:(c + 1) * SW],
                    s_agg.ap()[:, h * SW:(h + 1) * SW]),
                waits=w, tag=f"cs{h}_{c}"))
            if c < CH - 1:
                ca.v(lambda eng, h=h, c=c: eng.tensor_tensor(
                    s_agg.ap()[:, h * SW:(h + 1) * SW],
                    s_agg.ap()[:, h * SW:(h + 1) * SW],
                    g_raw.ap()[:, (h * CH + c) * SW:(h * CH + c + 1) * SW],
                    OP.add), tag=f"sr{h}_{c}")
        return cast_toks

    all_casts = {0: plan_chainA(0)}
    pipe = {}

    def plan_dbc_ao(h):
        st = pipe.pop(h)
        p = h % 2
        pdD = pd[2 + p].ap()
        dbc_toks = []
        for half in range(2):
            tok = S.op("tensor", lambda eng, half=half, pdD=pdD: eng.matmul(
                pdD[:, half * 512:(half + 1) * 512], ones_r.ap(),
                st["rc_ap"][0:1, half * 512:(half + 1) * 512],
                start=True, stop=True),
                waits=[st["rc"]], inc="MM", tag=f"dbc{h}_{half}")
            dbc_toks.append(tok)
        prev_dbc[p] = dbc_toks[-1]
        t_ao = vop(lambda eng, h=h, p=p, pdD=pdD: eng.tensor_tensor(
            n_acc.ap()[:, h * T:(h + 1) * T], at_b[p].ap(), pdD[:, 0:1024],
            OP.mult), waits=[dbc_toks[-1], st["nt"]], tag=f"ao{h}")
        prev_den_rd[p] = t_ao
        return dbc_toks[-1]

    for h in range(H):
        p = h % 2
        pdN = pd[p].ap()
        pdD = pd[2 + p].ap()
        q_h = q_fm.ap()[:, h * T:(h + 1) * T]
        cast_toks = all_casts[h]
        for c in range(CH):
            wN = [cast_toks[c], prev_num_rd[p]] + (init_free[2 * p] +
                 init_free[2 * p + 1] if c == 0 and h < 2 else [])
            S.op("tensor", lambda eng, h=h, c=c, p=p, pdN=pdN, q_h=q_h:
                 eng.matmul(pdN[:, c * 256:(c + 1) * 256],
                            sb4[p].ap()[:, c * SW:c * SW + 128],
                            q_h[:, c * 256:(c + 1) * 256],
                            start=True, stop=True),
                 waits=wN, inc="MM", tag=f"xn{h}_{c}")
            wD = [prev_den_rd[p], tok_p3_sc] + (init_free[4 + 2 * p] +
                 init_free[5 + 2 * p] if c == 0 and h < 2 else [])
            S.op("tensor", lambda eng, h=h, c=c, pdD=pdD: eng.matmul(
                pdD[0:1, c * 256:(c + 1) * 256], ones1,
                den_i.ap()[0:1, h * T + c * 256: h * T + (c + 1) * 256],
                start=True, stop=False),
                waits=wD, tag=f"xdi{h}_{c}")
            S.op("tensor", lambda eng, h=h, c=c, p=p, pdD=pdD, q_h=q_h:
                 eng.matmul(pdD[0:1, c * 256:(c + 1) * 256],
                            sb4[p].ap()[:, c * SW + 128:c * SW + 129],
                            q_h[:, c * 256:(c + 1) * 256],
                            start=False, stop=True),
                 inc="MM", tag=f"xdz{h}_{c}")
        t_mms = S.tok("MM")
        prev_cast_mm[p] = t_mms
        if h + 1 < H:
            all_casts[h + 1] = plan_chainA(h + 1)
        if h - 1 in pipe:
            plan_dbc_ao(h - 1)
        rc_ap = rc_b.ap()[0:1, 0:1024]
        dn_ap = dn_b.ap()[0:1, 0:1024]
        t_mx = vop(lambda eng, pdD=pdD, dn_ap=dn_ap: eng.tensor_scalar_max(
            dn_ap, pdD[0:1, 0:1024], 1e-6),
            waits=[t_mms, prev_dbc[p]], tag=f"mx{h}")
        def f_rc(eng, rc_ap=rc_ap, dn_ap=dn_ap):
            with eng.bass.allow_low_precision(
                    reason="bf16 den/recip: within calibrated tolerance"):
                return eng.reciprocal(rc_ap, dn_ap)
        t_rc = vop(f_rc, waits=[t_mx, prev_dbc[1 - p]], tag=f"rc{h}")
        t_nt = vop(lambda eng, h=h, p=p, pdN=pdN: eng.scalar_tensor_tensor(
            at_b[p].ap(), pdN[:, 0:1024], 1.0,
            n_acc.ap()[:, h * T:(h + 1) * T], OP.mult, OP.add),
            waits=[t_mms], tag=f"nt{h}")
        prev_num_rd[p] = t_nt
        pipe[h] = {"rc": t_rc, "rc_ap": rc_ap, "nt": t_nt}
    plan_dbc_ao(H - 1)
    for p in range(2):
        bank_free[2 * p].append(prev_num_rd[p])
        bank_free[2 * p + 1].append(prev_num_rd[p])
        bank_free[4 + 2 * p].append(prev_den_rd[p])
        bank_free[5 + 2 * p].append(prev_den_rd[p])
    tok_attn = S.tok("EV")

    if upto < 5:
        return _finish()
    y_banks = [0, 1, 2, 3]
    for oc in range(OC):
        s, wtok = wload(wpanel(woutT, oc * 128, (oc + 1) * 128), DC,
                        tag=f"wo{oc}")
        last_mm = None
        for th in range(TH):
            yb = y_banks[(oc * TH + th) % 4]
            items = [(pb[yb][:, 0:512],
                      wslot[s].ap()[:, hh * 128:(hh + 1) * 128],
                      n_acc.ap()[:, hh * T + th * 512: hh * T + (th + 1) * 512])
                     for hh in range(H)]
            last_mm = mmg(items, yb, waits=[wtok, tok_attn], tag=f"womm{oc}_{th}")
            evac("vector", lambda eng, oc=oc, th=th, yb=yb: eng.tensor_copy(
                y_acc.ap()[:, oc * T + th * 512: oc * T + (th + 1) * 512],
                pb[yb][:, 0:512]), yb, tag=f"evwo{oc}_{th}")
        slot_free[s] = last_mm
    tok_y0 = S.tok("EV")

    if upto < 6:
        return _finish()
    for oc in range(OC):
        s, wtok = wload(wpanel(sw2T, oc * 128, (oc + 1) * 128), HC,
                        tag=f"sw2_{oc}")
        last_mm = None
        for th in range(TH):
            yb = y_banks[(oc * TH + th) % 4]
            items = [(pb[yb][:, 0:512],
                      wslot[s].ap()[:, hc * 128:(hc + 1) * 128],
                      h_e.ap()[:, hc * T + th * 512: hc * T + (th + 1) * 512])
                     for hc in range(HC)]
            last_mm = mmg(items, yb, waits=[wtok, tok_hsh, tok_y0],
                          tag=f"symm{oc}_{th}")
            evac("vector", lambda eng, oc=oc, th=th, yb=yb: eng.tensor_tensor(
                y_acc.ap()[:, oc * T + th * 512: oc * T + (th + 1) * 512],
                y_acc.ap()[:, oc * T + th * 512: oc * T + (th + 1) * 512],
                pb[yb][:, 0:512], OP.add), yb, tag=f"evsy{oc}_{th}")
        slot_free[s] = last_mm

    if upto < 7:
        return _finish()
    prev_gbc_gate = [tok_attn]

    def plan_gbc_xng(e, extra_wait):
        gb_toks = []
        for th in range(TH):
            gband = 6 + th
            for i, tt in enumerate(range(th * 4, th * 4 + 4)):
                trans(pb[gband][0:1, i * 128:(i + 1) * 128],
                      g_tm2.ap()[:, tt * E + e: tt * E + e + 1],
                      identf.ap(), gband, waits=[tok_gtm], tag=f"gtr{e}_{tt}")
            t_gs = evac("vector", lambda eng, gband=gband: eng.tensor_copy(
                g_stage.ap(), pb[gband][0:1, 0:512]), gband,
                waits=[prev_gbc_gate[0]], tag=f"gst{e}_{th}")
            bb = 6 + th
            mmg([(pb[bb][:, 0:512], ones_r.ap(), g_stage.ap())], bb,
                waits=[t_gs], tag=f"gbb{e}_{th}")
            gb_toks.append(evac("vector",
                                lambda eng, th=th, bb=bb: eng.tensor_copy(
                                    gbc.ap()[:, th * 512:(th + 1) * 512],
                                    pb[bb][:, 0:512]), bb,
                                waits=[prev_gbc_gate[0]], tag=f"gbe{e}_{th}"))
        xng_tok = None
        for dc in range(DC):
            xng_tok = vop(lambda eng, dc=dc: eng.tensor_tensor(
                xng.ap()[:, dc * T:(dc + 1) * T],
                xn_fm.ap()[:, dc * T:(dc + 1) * T], gbc.ap(), OP.mult),
                waits=(gb_toks + [extra_wait]) if dc == 0 else (),
                tag=f"xng{e}_{dc}")
        prev_gbc_gate[0] = xng_tok
        return xng_tok

    xng_tok = plan_gbc_xng(0, None)
    for e in range(E):
        for hcp in range(HC // 2):
            u_phase(hcp, xng, xng_tok, w1T[e], w3T[e], f"e{e}")
        tok_umm = S.tok("MM")
        tok_he = S.tok("EV")
        if e + 1 < E:
            xng_tok = plan_gbc_xng(e + 1, tok_umm)
        for oc in range(OC):
            s, wtok = wload(wpanel(w2T[e], oc * 128, (oc + 1) * 128), HC,
                            tag=f"e{e}w2_{oc}")
            last_mm = None
            for th in range(TH):
                yb = y_banks[(oc * TH + th) % 4]
                items = [(pb[yb][:, 0:512],
                          wslot[s].ap()[:, hc * 128:(hc + 1) * 128],
                          h_e.ap()[:, hc * T + th * 512: hc * T + (th + 1) * 512])
                         for hc in range(HC)]
                last_mm = mmg(items, yb, waits=[wtok, tok_he],
                              tag=f"e{e}ymm{oc}_{th}")
                evac("vector", lambda eng, oc=oc, th=th, yb=yb: eng.tensor_tensor(
                    y_acc.ap()[:, oc * T + th * 512: oc * T + (th + 1) * 512],
                    y_acc.ap()[:, oc * T + th * 512: oc * T + (th + 1) * 512],
                    pb[yb][:, 0:512], OP.add), yb, tag=f"e{e}ev{oc}_{th}")
            slot_free[s] = last_mm
    tok_yfull = S.tok("EV")

    if upto < 8:
        return _finish()
    tok_mm_p8 = S.tok("MM")
    st_tok = [None, None]
    xr_tok = [None, None]
    xr_free = [None, None]
    for tt in range(TT):
        b = tt % 2
        xr_tok[b] = dma("scalar", f"LX{b}", x_rd[b].ap(),
                        x_in[tt * 128:(tt + 1) * 128, :],
                        waits=[xr_free[b], tok_mm_p8], tag=f"xrd{tt}")
        last_add = None
        for g in range(4):
            tb = g % 4
            for i in range(4):
                oc = g * 4 + i
                trans(pb[tb][:, i * 128:(i + 1) * 128],
                      y_acc.ap()[:, oc * T + tt * 128: oc * T + (tt + 1) * 128],
                      identf.ap(), tb, waits=[tok_yfull], tag=f"try{tt}_{oc}")
            last_add = evac("vector",
                            lambda eng, g=g, tb=tb, b=b: eng.tensor_tensor(
                                out_st[b].ap()[:, g * 512:(g + 1) * 512],
                                pb[tb][:, 0:512],
                                x_rd[b].ap()[:, g * 512:(g + 1) * 512],
                                OP.add),
                            tb, waits=[xr_tok[b], st_tok[b]],
                            tag=f"oadd{tt}_{g}")
        xr_free[b] = last_add
        st_tok[b] = dma("sync", "ST" if b == 0 else "ST1",
                        out_t[tt * 128:(tt + 1) * 128, :],
                        out_st[b].ap(), waits=[last_add], tag=f"ost{tt}")

    S.op("sync", lambda eng: None, waits=[S.tok("ST"), S.tok("ST1")],
         tag="fin")

    S.check()
    return nc, S, SEMS


_CACHE = {}


def _host_constants():
    inv = 1.0 / (10000.0 ** (np.arange(0, 128, 2, dtype=np.float64) / 128.0))
    invf = np.concatenate([inv, inv])
    ident = np.eye(128, dtype=np.float32)
    protT = np.zeros((128, 128), dtype=np.float32)
    for i in range(64):
        protT[2 * i + 1, 2 * i] = -1.0
        protT[2 * i, 2 * i + 1] = 1.0
    trilm = np.zeros((128, 512), dtype=np.float32)
    for s_ in range(2):
        for p in range(128):
            for l in range(256):
                if s_ * 128 + p <= l:
                    trilm[p, s_ * 256 + l] = 1.0
    ones = np.ones((128, 128), dtype=np.float32)
    return invf, ident, protT, trilm, ones


def kernel(L0_attn_x, L0_attn_W_qkv, L0_attn_W_out, L0_ffn_expert_bias,
           L0_ffn_gate_w, L0_ffn_w1, L0_ffn_w2, L0_ffn_w3, L0_ffn_sw1,
           L0_ffn_sw2, L0_ffn_sw3):
    x = np.ascontiguousarray(np.asarray(L0_attn_x, dtype=np.float32)
                             .reshape(NCORES * T, D))
    wqkvT = np.ascontiguousarray(np.asarray(L0_attn_W_qkv, np.float32).T)
    woutT = np.ascontiguousarray(np.asarray(L0_attn_W_out, np.float32).T)
    gwT = np.ascontiguousarray(np.asarray(L0_ffn_gate_w, np.float32).T)
    w1T = np.ascontiguousarray(np.asarray(L0_ffn_w1, np.float32)
                               .transpose(0, 2, 1))
    w3T = np.ascontiguousarray(np.asarray(L0_ffn_w3, np.float32)
                               .transpose(0, 2, 1))
    w2T = np.ascontiguousarray(np.asarray(L0_ffn_w2, np.float32)
                               .transpose(0, 2, 1))
    sw1T = np.ascontiguousarray(np.asarray(L0_ffn_sw1, np.float32).T)
    sw3T = np.ascontiguousarray(np.asarray(L0_ffn_sw3, np.float32).T)
    sw2T = np.ascontiguousarray(np.asarray(L0_ffn_sw2, np.float32).T)
    ebias = np.ascontiguousarray(
        np.broadcast_to(np.asarray(L0_ffn_expert_bias, np.float32), (128, E)))

    invf, ident, protT, trilm, ones = _host_constants()

    if "nc" not in _CACHE:
        nc, S, SEMS = build_program()
        _CACHE["nc"] = nc
        _CACHE["S"] = S
        _CACHE["SEMS"] = SEMS
    nc, S, SEMS = _CACHE["nc"], _CACHE["S"], _CACHE["SEMS"]

    in_maps = []
    for c in range(NCORES):
        pos = np.arange(c * T, (c + 1) * T, dtype=np.float64)
        ang = pos[None, :] * invf[:, None]
        cosf = np.ascontiguousarray(np.cos(ang).astype(np.float32))
        sinf = np.ascontiguousarray(np.sin(ang).astype(np.float32))
        pmask = np.zeros((128, E), np.float32)
        pmask[:, :c] = 1.0
        in_maps.append({
            "x": x[c * T:(c + 1) * T],
            "wqkvT": wqkvT, "woutT": woutT, "gwT": gwT,
            "w1T": w1T, "w3T": w3T, "w2T": w2T,
            "sw1T": sw1T, "sw3T": sw3T, "sw2T": sw2T,
            "ebias": ebias, "pmask": pmask,
            "cosf": cosf, "sinf": sinf,
            "identf": ident, "protT": protT, "trilm": trilm, "onesin": ones,
        })

    res = _run(nc, S, SEMS, in_maps)
    out = np.concatenate([res[c]["out"] for c in range(NCORES)], axis=0)
    return out.reshape(1, NCORES * T, D).astype(np.float32)


def _run(nc, S, SEMS, in_maps):
    if "emitted" not in _CACHE:
        with ExitStack() as ctx:
            sems = {s: ctx.enter_context(nc.semaphore(s)) for s in SEMS}
            block = ctx.enter_context(nc.Block())
            mk = S.emit(block, sems)
            block.tensor(mk("tensor"))
            block.vector(mk("vector"))
            block.scalar(mk("scalar"))
            block.sync(mk("sync"))
            block.gpsimd(mk("gpsimd"))
        _CACHE["emitted"] = True
    res = run_bass_kernel_spmd(nc, in_maps, core_ids=list(range(NCORES)))
    return res.results


# version 11
# speedup vs baseline: 1.0170x; 1.0105x over previous
from collections import defaultdict
from contextlib import ExitStack

import numpy as np

import concourse.bass as bass
import concourse.mybir as mybir
from concourse.bass_utils import run_bass_kernel_spmd

F32 = mybir.dt.float32
BF16 = mybir.dt.bfloat16
AX = mybir.AxisListType.X
OP = mybir.AluOpType
AF = mybir.ActivationFunctionType

NCORES = 8
T = 1024
TT = 8
TH = 2
D = 2048
DC = 16
H = 16
CH = 4
E = 8
HID = 1024
HC = 8
OC = 16
SW = 129

ENGINES = ("tensor", "vector", "scalar", "sync", "gpsimd")


class Sched:

    def __init__(self):
        self.streams = {e: [] for e in ENGINES}
        self.cnt = defaultdict(int)

    def op(self, eng, fn, waits=(), inc=None, n=1, tag=""):
        waits = tuple(w for w in waits if w is not None)
        val = None
        if inc is not None:
            self.cnt[inc] += n
            val = self.cnt[inc]
        self.streams[eng].append((fn, waits, inc, n, tag))
        return (inc, val) if inc else None

    def tok(self, sem):
        return (sem, self.cnt[sem])

    def check(self):
        pos = {e: 0 for e in ENGINES}
        val = defaultdict(int)
        while True:
            progressed = False
            for e in ENGINES:
                while pos[e] < len(self.streams[e]):
                    fn, waits, inc, n, tag = self.streams[e][pos[e]]
                    if any(val[s] < w for s, w in waits):
                        break
                    if inc is not None:
                        val[inc] += n
                    pos[e] += 1
                    progressed = True
            if all(pos[e] == len(self.streams[e]) for e in ENGINES):
                return
            if not progressed:
                stuck = {
                    e: (pos[e], len(self.streams[e]),
                        self.streams[e][pos[e]][1], self.streams[e][pos[e]][4])
                    for e in ENGINES if pos[e] < len(self.streams[e])
                }
                raise RuntimeError(
                    f"schedule deadlock; sems={dict(val)}; stuck={stuck}")

    def emit(self, block, sems):
        water = {e: defaultdict(int) for e in ENGINES}

        def mk(eng_name):
            def body(eng):
                for fn, waits, inc, n, tag in self.streams[eng_name]:
                    for s, w in waits:
                        if w > water[eng_name][s]:
                            eng.wait_ge(sems[s], w)
                            water[eng_name][s] = w
                    ins = fn(eng)
                    if inc is not None:
                        assert ins is not None, f"{tag}: inc op returned None"
                        ins.then_inc(sems[inc], n)
            return body
        return mk


def build_program(upto=99, dump=None, feats=()):
    nc = bass.Bass("TRN2", target_bir_lowering=False,
                   detect_race_conditions=False)

    x_in = nc.dram_tensor("x", [T, D], F32, kind="ExternalInput").ap()
    wqkvT = nc.dram_tensor("wqkvT", [D, 3 * D], F32, kind="ExternalInput").ap()
    woutT = nc.dram_tensor("woutT", [D, D], F32, kind="ExternalInput").ap()
    gwT = nc.dram_tensor("gwT", [D, E], F32, kind="ExternalInput").ap()
    w1T = nc.dram_tensor("w1T", [E, D, HID], F32, kind="ExternalInput").ap()
    w3T = nc.dram_tensor("w3T", [E, D, HID], F32, kind="ExternalInput").ap()
    w2T = nc.dram_tensor("w2T", [E, HID, D], F32, kind="ExternalInput").ap()
    sw1T = nc.dram_tensor("sw1T", [D, HID], F32, kind="ExternalInput").ap()
    sw3T = nc.dram_tensor("sw3T", [D, HID], F32, kind="ExternalInput").ap()
    sw2T = nc.dram_tensor("sw2T", [HID, D], F32, kind="ExternalInput").ap()
    ebias_in = nc.dram_tensor("ebias", [128, E], F32, kind="ExternalInput").ap()
    pmask_in = nc.dram_tensor("pmask", [128, E], F32, kind="ExternalInput").ap()
    cosf_in = nc.dram_tensor("cosf", [128, T], F32, kind="ExternalInput").ap()
    sinf_in = nc.dram_tensor("sinf", [128, T], F32, kind="ExternalInput").ap()
    identf_in = nc.dram_tensor("identf", [128, 128], F32, kind="ExternalInput").ap()
    protT_in = nc.dram_tensor("protT", [128, 128], F32, kind="ExternalInput").ap()
    trilm_in = nc.dram_tensor("trilm", [128, 512], F32, kind="ExternalInput").ap()
    ones_in = nc.dram_tensor("onesin", [128, 128], F32, kind="ExternalInput").ap()
    out_t = nc.dram_tensor("out", [T, D], F32, kind="ExternalOutput").ap()

    agg_in = nc.dram_tensor("agg_in", [H * 128, SW], F32)
    agg_out = nc.dram_tensor("agg_out", [NCORES, H * 128, SW], F32,
                             addr_space="Shared")

    LIMIT = 212608
    arena = nc.alloc_sbuf_tensor("arena", [128, LIMIT], mybir.dt.uint8)
    base = nc.lookup_mloc(arena).addr
    nxt = [0]
    placed = []

    def at(name, shape, dtype, off):
        return nc.alloc_sbuf_tensor_at(name, shape, dtype, offset=base + off)

    def seq(name, shape, dtype):
        off = (nxt[0] + 31) // 32 * 32
        per = int(np.prod(shape[1:])) * mybir.dt.size(dtype)
        nxt[0] = off + per
        assert nxt[0] <= LIMIT, f"SBUF overflow at {name}: {nxt[0]}"
        placed.append((name, off, per))
        return at(name, shape, dtype, off), off

    xn_fm, _ = seq("xn_fm", [128, DC * T], BF16)

    regA, offA = seq("regA", [128, 16384], F32)
    q_fm = at("q_fm", [128, H * T], BF16, offA)
    den_i = at("den_i", [1, H * T], BF16, offA + 32768)
    y_acc = regA

    regB, offB = seq("regB", [128, 8192], F32)
    n_acc = at("n_acc", [128, H * T], BF16, offB)
    logit_sb = at("logit_sb", [E, T], F32, offB)
    score_tm = at("score_tm", [128, TT * E], F32, offB + 4096)
    g_scr = at("g_scr", [128, 3 * E], F32, offB + 4352)
    x_rd = [at(f"x_rd{i}", [128, D], F32, offB + i * 8192) for i in range(2)]
    out_st = [at(f"out_st{i}", [128, D], F32, offB + 16384 + i * 8192)
              for i in range(2)]

    regC, offC = seq("regC", [128, 16384], BF16)
    g_raw = at("g_raw", [128, H * CH * SW], BF16, offC)
    s_agg = at("s_agg", [128, H * SW], F32, offC + 16512)
    lnst = at("lnst", [128, 256], F32, offC + 16512)
    cosf = at("cosf_sb", [128, T], BF16, offC + 24768)
    sinf = at("sinf_sb", [128, T], BF16, offC + 26816)
    xng = regC

    regD, offD = seq("regD", [128, 8192], BF16)
    h_e = regD
    x_t0 = at("x_t0", [128, D], F32, offD)
    xn_scr = at("xn_scr", [128, D], F32, offD + 8192)

    regE, offE = seq("regE", [128, 8256], mybir.dt.uint8)
    x_t1 = at("x_t1", [128, D], F32, offE)
    qr_s = at("qr_s", [128, 512], BF16, offD + 7168)
    rp_s = at("rp_s", [128, 512], BF16, offD + 8192)
    rq_s = at("rq_s", [128, 512], BF16, offD + 9216)
    ex_s = at("ex_s", [128, 512], BF16, offD + 10240)
    rl_s = at("rl_s", [128, 512], BF16, offD + 11264)
    _abase = [offE, offD]
    k_tm = [at(f"k_tm{i}", [128, T], BF16, _abase[i]) for i in range(2)]
    v_tm = [at(f"v_tm{i}", [128, T], BF16, _abase[i] + 2048) for i in range(2)]
    v_fs = [at(f"v_fs{i}", [128, T], BF16, _abase[i] + 4096) for i in range(2)]
    att_s = [at(f"att_s{i}", [128, 512], BF16, _abase[i] + 4096)
             for i in range(2)]
    k_fm = [at(f"k_fm{i}", [128, T], BF16, _abase[i] + 6144) for i in range(2)]
    rank_h = [at(f"rank_h{i}", [128, 8 * SW], F32, offE + i * 4128)
              for i in range(2)]
    sb4 = [at(f"s_bf4{i}", [128, 4 * SW], BF16, offC + 28864 + i * 1056)
           for i in range(2)]
    dn_b = at("dn_b", [1, 1024], BF16, offC + 24768)
    rc_b = at("rc_b", [1, 1024], BF16, offC + 26816)
    gbc = at("gbc", [128, T], BF16, offE)

    _, offW = seq("warena", [128, 8192], BF16)
    wslot = [at(f"wslot{i}", [128, DC * 128], BF16, offW + i * 4096)
             for i in range(4)]

    _, offS = seq("regS", [128, 3072], mybir.dt.uint8)
    trilm = at("trilm", [128, 512], BF16, offS)
    xnf32 = [at(f"xnf32_{i}", [128, 128], F32, offS + 1024 + i * 512)
             for i in range(2)]
    s1_scr = at("s1_scr", [128, 512], BF16, offS)
    t3_scr = at("t3_scr", [128, 512], BF16, offS + 1024)
    at_b = [at("at_b0", [128, 1024], BF16, offE + 2048),
            at("at_b1", [128, 1024], BF16, offE)]
    rc_p = [None, None]
    gwT_sb = at("gwT_sb", [128, DC * E], F32, offS + 2048)
    identf, _ = seq("identf", [128, 128], F32)
    identb, _ = seq("identb", [128, 128], BF16)
    protT, _ = seq("protT", [128, 128], BF16)
    ones_b, _ = seq("ones_b", [128, 1], BF16)
    ones_r, _ = seq("ones_r", [1, 128], BF16)
    ones_f, _ = seq("ones_f", [1, 128], F32)
    ebias_sb, _ = seq("ebias_sb", [128, E], F32)
    pmask_sb, _ = seq("pmask_sb", [128, E], F32)
    rc_s2 = at("rc_s2", [1, 1024], BF16, offS)
    g_tm2, _ = seq("g_tm2", [128, TT * E], F32)
    g_stage = at("g_stage", [1, 512], BF16, offS + 2048)
    mu_sc, _ = seq("mu_sc", [128, 8], F32)

    pd = [nc.alloc_psum_tensor(f"pd{i}", [128, 1024], F32) for i in range(4)]
    pb = [pd[i // 2].ap()[:, (i % 2) * 512:(i % 2) * 512 + 512]
          for i in range(8)]

    S = Sched()
    SEMS = ("LX0", "LX1", "LW0", "LW1", "LW2", "LW3", "LC", "LG", "LA",
            "RK0", "RK1", "MM", "EV", "SC", "CC", "ST", "ST1")
    dump_map = {}


    bank_free = [[] for _ in range(8)]
    bank_data = [None] * 8

    def dma(q, sem, out_ap, in_ap, waits=(), tag="", slow=False):
        waits = list(waits)
        if S.cnt[sem]:
            waits.append((sem, S.cnt[sem]))
        def f(eng, o=out_ap, i=in_ap, slow=slow):
            return eng.dma_start(out=o, in_=i,
                                 allow_slow_non_contiguous=slow)
        return S.op(q, f, waits=waits, inc=sem, n=16, tag=tag)

    def mmg(items, b, waits=(), tag=""):
        w = list(waits) + bank_free[b]
        bank_free[b] = []
        n = len(items)
        tok = None
        for j, (o, l, r) in enumerate(items):
            def f(eng, o=o, l=l, r=r, j=j, n=n):
                return eng.matmul(o, l, r, start=(j == 0), stop=(j == n - 1))
            tok = S.op("tensor", f, waits=(w if j == 0 else ()),
                       inc=("MM" if j == n - 1 else None), tag=tag)
        bank_data[b] = tok
        return tok

    def trans(out_ap, in_ap, ident_ap, b, waits=(), tag=""):
        w = list(waits) + bank_free[b]
        bank_free[b] = []
        def f(eng, o=out_ap, i=in_ap, idn=ident_ap):
            return eng.transpose(o, i, idn)
        tok = S.op("tensor", f, waits=w, inc="MM", tag=tag)
        bank_data[b] = tok
        return tok

    def evac(eng_name, fn, b, waits=(), last=True, tag=""):
        w = list(waits)
        if bank_data[b] is not None:
            w.append(bank_data[b])
        sem = {"vector": "EV", "scalar": "SC"}[eng_name]
        tok = S.op(eng_name, fn, waits=w, inc=sem, tag=tag)
        bank_free[b].append(tok)
        return tok

    def vop(fn, waits=(), inc="EV", tag=""):
        return S.op("vector", fn, waits=waits, inc=inc, tag=tag)

    def sop(fn, waits=(), inc="SC", tag=""):
        return S.op("scalar", fn, waits=waits, inc=inc, tag=tag)

    class VChain:

        def __init__(self, tok=None):
            self.tok = tok

        def v(self, fn, waits=(), tag=""):
            self.tok = vop(fn, waits=list(waits) + [self.tok], tag=tag)
            return self.tok

        def s(self, fn, waits=(), tag=""):
            self.tok = sop(fn, waits=list(waits) + [self.tok], tag=tag)
            return self.tok

    dump_map.update(xn=(xn_fm, BF16), q=(q_fm, BF16), nacc=(n_acc, BF16),
                    sagg=(s_agg, F32), yacc=(y_acc, F32), graw=(g_raw, BF16),
                    xnscr=(xn_scr, F32), xt0=(x_t0, F32), mu=(mu_sc, F32),
                    )

    def _finish():
        if dump is not None:
            buf, dt_ = dump_map[dump]
            ncol = buf.shape[1]
            view3 = out_t.rearrange("(o p) d -> p o d", p=128)
            w = [("EV", S.cnt["EV"]), ("SC", S.cnt["SC"]),
                 ("MM", S.cnt["MM"])]
            nfull, rem = divmod(ncol, D)
            if nfull:
                dma("gpsimd", "ST", view3[:, 0:nfull, :],
                    buf.ap()[:, 0:nfull * D].rearrange("p (o d) -> p o d", d=D),
                    waits=w, tag="dump")
            if rem:
                dma("gpsimd", "ST", view3[:, nfull:nfull + 1, 0:rem],
                    buf.ap()[:, nfull * D:ncol].rearrange(
                        "p (o d) -> p o d", o=1),
                    waits=w, tag="dumpr")
        S.op("sync", lambda eng: None, waits=[S.tok("ST")], tag="fin")
        S.check()
        return nc, S, SEMS

    dma("sync", "LC", identf.ap(), identf_in, tag="identf")
    dma("sync", "LC", ebias_sb.ap(), ebias_in, tag="ebias")
    dma("sync", "LC", pmask_sb.ap(), pmask_in, tag="pmask")
    dma("sync", "LC", ones_f.ap(), ones_in[0:1, :], tag="onesf")
    dma("sync", "LC", gwT_sb.ap().rearrange("p (c e) -> p c e", c=DC),
        gwT.rearrange("(c p) e -> p c e", p=128), tag="gwT")
    dma("gpsimd", "LG", identb.ap(), identf_in, tag="identb")
    dma("gpsimd", "LG", protT.ap(), protT_in, tag="protT")
    dma("gpsimd", "LG", trilm.ap(), trilm_in, tag="trilm")
    dma("gpsimd", "LG", ones_b.ap(), ones_in[:, 0:1], tag="onesb", slow=True)
    dma("gpsimd", "LG", ones_r.ap(), ones_in[0:1, :], tag="onesr")
    dma("gpsimd", "LG", cosf.ap(), cosf_in, tag="cosf")
    dma("gpsimd", "LG", sinf.ap(), sinf_in, tag="sinf")
    tok_const_g = S.tok("LG")
    tok_const_s = S.tok("LC")



    x_bufs = [x_t0, x_t1]
    x_tok = [None, None]
    x_free = [None, None]
    for tt in range(2):
        b = tt % 2
        x_tok[b] = dma("sync", f"LX{b}", x_bufs[b].ap(),
                       x_in[tt * 128:(tt + 1) * 128, :], tag=f"xld{tt}")

    gmm_tok = [None, None]
    evx_tok = [None, None]
    prev_tr_tok = None
    NT = 1 if "t1" in feats else TT
    stats_tok = [None] * max(NT, 1)

    def phase_a(tt):
        b = tt % 2
        xb = x_bufs[b]
        this_x_tok = x_tok[b]
        tok = None
        for qq in range(4):
            tok = vop(lambda eng, xb=xb, tt=tt, qq=qq: eng.bn_stats(
                lnst.ap()[:, tt * 24 + qq * 6: tt * 24 + (qq + 1) * 6],
                xb.ap()[:, qq * 512:(qq + 1) * 512]),
                waits=[this_x_tok] if qq == 0 else (), tag=f"lnA{tt}_{qq}")
        stats_tok[tt] = vop(lambda eng, tt=tt: eng.bn_aggr(
            lnst.ap()[:, 192 + 2 * tt: 194 + 2 * tt],
            lnst.ap()[:, tt * 24: (tt + 1) * 24]),
            waits=[tok], tag=f"lnAg{tt}")
        x_free[b] = stats_tok[tt]

    def phase_b(t0, n):
        mv = lnst.ap()[:, 192 + 2 * t0:192 + 2 * (t0 + n)].rearrange(
            "p (t two) -> p t two", two=2)
        var8 = mv[:, :, 1:2]
        ve8 = lnst.ap()[:, 208 + t0:208 + t0 + n]
        rc8 = lnst.ap()[:, 216 + t0:216 + t0 + n]
        rs8 = lnst.ap()[:, 224 + t0:224 + t0 + n]
        n1 = lnst.ap()[:, 232 + t0:232 + t0 + n]
        n2 = lnst.ap()[:, 240 + t0:240 + t0 + n]
        cB = VChain(stats_tok[t0 + n - 1])
        cB.v(lambda eng: eng.tensor_scalar_add(ve8, var8, 1e-5), tag=f"lnBve{t0}")
        cB.v(lambda eng: eng.reciprocal(rc8, ve8), tag=f"lnBrc{t0}")
        cB.s(lambda eng: eng.activation(rs8, rc8, AF.Sqrt), tag=f"lnBsq{t0}")
        cB.v(lambda eng: eng.tensor_tensor(n1, rs8, rs8, OP.mult), tag=f"lnBn1{t0}")
        cB.v(lambda eng: eng.tensor_tensor(n2, ve8, n1, OP.mult), tag=f"lnBn2{t0}")
        cB.v(lambda eng: eng.tensor_scalar(n2, n2, -0.5, 1.5, OP.mult, OP.add),
             tag=f"lnBn3{t0}")
        return cB.v(lambda eng: eng.tensor_tensor(rs8, rs8, n2, OP.mult),
                    tag=f"lnBn4{t0}")

    n0 = min(4, NT)
    for tt in range(n0):
        b = tt % 2
        if tt >= 2:
            x_tok[b] = dma("sync", f"LX{b}", x_bufs[b].ap(),
                           x_in[tt * 128:(tt + 1) * 128, :],
                           waits=[x_free[b]], tag=f"xldA{tt}")
        phase_a(tt)
    tokB = {0: phase_b(0, n0)}
    for tt in range(NT):
        tok_rstd = tokB[tt // 4]
        b = tt % 2
        xb = x_bufs[b]
        this_x_tok = dma("sync", f"LX{b}", xb.ap(),
                         x_in[tt * 128:(tt + 1) * 128, :],
                         waits=[x_free[b]], tag=f"xldC{tt}")
        x_free[b] = vop(lambda eng, xb=xb, tt=tt: eng.tensor_scalar(
            xn_scr.ap(), xb.ap(), lnst.ap()[:, 192 + 2 * tt:193 + 2 * tt],
            lnst.ap()[:, 224 + tt:225 + tt], OP.subtract, OP.mult),
            waits=[this_x_tok, tok_rstd, prev_tr_tok], tag=f"lnXn{tt}")
        xn_ready = x_free[b]

        pend = None
        for dc in range(DC if "notrans" not in feats else 0):
            tb = dc % 2
            xf = dc % 2
            ttok = trans(pb[tb][:, 0:128],
                         xn_scr.ap()[:, dc * 128:(dc + 1) * 128],
                         identf.ap(), tb,
                         waits=[xn_ready, tok_const_s], tag=f"trx{tt}_{dc}")
            sc_tok = evac("scalar",
                          lambda eng, tb=tb, xf=xf: eng.copy(
                              xnf32[xf].ap(), pb[tb][:, 0:128]),
                          tb, waits=[gmm_tok[xf], evx_tok[xf]],
                          tag=f"evf{tt}_{dc}")
            evx_tok[xf] = vop(
                lambda eng, tt=tt, dc=dc, xf=xf: eng.tensor_copy(
                    xn_fm.ap()[:, dc * T + tt * 128: dc * T + (tt + 1) * 128],
                    xnf32[xf].ap()),
                waits=[sc_tok], tag=f"evx{tt}_{dc}")
            if pend is not None and "nogate" not in feats:
                pdc, pxf, ptok = pend
                gb = 6 + (tt // 4)
                col = (tt % 4) * 128
                gmm_tok[pxf] = S.op(
                    "tensor",
                    lambda eng, pdc=pdc, pxf=pxf, gb=gb, col=col: eng.matmul(
                        pb[gb][0:E, col:col + 128],
                        gwT_sb.ap()[:, pdc * E:(pdc + 1) * E],
                        xnf32[pxf].ap(),
                        start=(pdc == 0), stop=(pdc == DC - 1)),
                    waits=[ptok], inc=("MM" if pdc == DC - 1 else None),
                    tag=f"gmm{tt}_{pdc}")
            pend = (dc, xf, sc_tok)
        if "nogate" not in feats and "notrans" not in feats:
          pdc, pxf, ptok = pend
          gb = 6 + (tt // 4)
          col = (tt % 4) * 128
          gmm_tok[pxf] = S.op(
            "tensor",
            lambda eng, pdc=pdc, pxf=pxf, gb=gb, col=col: eng.matmul(
                pb[gb][0:E, col:col + 128],
                gwT_sb.ap()[:, pdc * E:(pdc + 1) * E],
                xnf32[pxf].ap(),
                start=(pdc == 0), stop=(pdc == DC - 1)),
            waits=[ptok], inc=("MM" if pdc == DC - 1 else None),
            tag=f"gmm{tt}_{pdc}")
        if tt == 3 and "nogate" not in feats:
            bank_data[6] = gmm_tok[1]
        if tt == 7 and "nogate" not in feats:
            bank_data[7] = gmm_tok[1]
        prev_tr_tok = S.tok("MM")
        if NT > 4 and tt < 4:
            bA = (tt + 4) % 2
            x_tok[bA] = dma("sync", f"LX{bA}", x_bufs[bA].ap(),
                            x_in[(tt + 4) * 128:(tt + 5) * 128, :],
                            waits=[x_free[bA]], tag=f"xldA{tt + 4}")
            phase_a(tt + 4)
            if tt == 3:
                tokB[1] = phase_b(4, 4)
    tok_xn = {0: None, 1: None}

    tok_xn_all = S.tok("EV")
    tok_msagg = vop(lambda eng: eng.memset(s_agg.ap(), 0.0),
                    waits=[S.tok("MM")], tag="msagg")

    if "nogate" in feats or "notrans" in feats:
        return _finish()
    e0 = evac("vector", lambda eng: eng.tensor_copy(
        logit_sb.ap()[:, 0:512], pb[6][0:E, 0:512]), 6, tag="evg0")
    tok_logits = evac("vector", lambda eng: eng.tensor_copy(
        logit_sb.ap()[:, 512:1024], pb[7][0:E, 0:512]), 7, tag="evg1")

    for tt in range(TT):
        tb = tt % 2
        trans(pb[tb][0:128, 0:E],
              logit_sb.ap()[:, tt * 128:(tt + 1) * 128],
              identf.ap()[0:E, 0:E], tb, waits=[tok_logits], tag=f"trg{tt}")
        evac("vector", lambda eng, tt=tt, tb=tb: eng.tensor_copy(
            score_tm.ap()[:, tt * E:(tt + 1) * E], pb[tb][0:128, 0:E]),
            tb, tag=f"evsc{tt}")
    tok_scores = S.tok("EV")
    tok_sig = sop(lambda eng: eng.activation(
        score_tm.ap(), score_tm.ap(), AF.Sigmoid),
        waits=[tok_scores], tag="sig")
    for tt in range(TT):
        sc_ap = score_tm.ap()[:, tt * E:(tt + 1) * E]
        y_ap = g_scr.ap()[:, 0:E]
        m1_ap = g_scr.ap()[:, E:2 * E]
        y2_ap = g_scr.ap()[:, 2 * E:3 * E]
        mx1 = mu_sc.ap()[:, 0:1]
        mx2 = mu_sc.ap()[:, 1:2]
        c = VChain()
        c.v(lambda eng, sc_ap=sc_ap, y_ap=y_ap: eng.tensor_add(
            y_ap, sc_ap, ebias_sb.ap()), waits=[tok_sig], tag=f"gy{tt}")
        c.v(lambda eng, y_ap=y_ap, mx1=mx1: eng.reduce_max(mx1, y_ap, axis=AX),
            tag=f"gm1{tt}")
        c.v(lambda eng, y_ap=y_ap, m1_ap=m1_ap, mx1=mx1:
            eng.tensor_single_scalar(m1_ap, y_ap, mx1, OP.is_equal),
            tag=f"geq{tt}")
        c.v(lambda eng, m1_ap=m1_ap, y_ap=y_ap, y2_ap=y2_ap:
            eng.scalar_tensor_tensor(y2_ap, m1_ap, -1e30, y_ap,
                                     OP.mult, OP.add), tag=f"gy2{tt}")
        c.v(lambda eng, y2_ap=y2_ap, mx2=mx2: eng.reduce_max(
            mx2, y2_ap, axis=AX), tag=f"gm2{tt}")
        c.v(lambda eng, y_ap=y_ap, m1_ap=m1_ap, mx2=mx2:
            eng.tensor_single_scalar(m1_ap, y_ap, mx2, OP.is_ge),
            tag=f"gmask{tt}")
        c.v(lambda eng, sc_ap=sc_ap, m1_ap=m1_ap, tt=tt: eng.tensor_tensor(
            g_tm2.ap()[:, tt * E:(tt + 1) * E], sc_ap, m1_ap, OP.mult),
            tag=f"gg{tt}")
    tok_gtm = S.tok("EV")

    slot_free = [None] * 4
    slot_rr = [0]

    def wload(panel_ap, n_dc, tag=""):
        s = slot_rr[0]
        slot_rr[0] = (s + 1) % 4
        tok = dma("gpsimd", f"LW{s}",
                  wslot[s].ap()[:, 0:n_dc * 128].rearrange(
                      "p (c n) -> p c n", c=n_dc),
                  panel_ap, waits=[slot_free[s]], tag=f"wl{tag}")
        return s, tok

    def wpanel(src2d, c0, c1, p=128):
        return src2d[:, c0:c1].rearrange("(c p) n -> p c n", p=p)

    rope_prev = {"q2": None, "ex": None}

    def rope_elu(mb, src_cols_th, dst_ap, tag):
        th = src_cols_th
        t0 = th * 512
        qtok = evac("scalar", lambda eng, mb=mb: eng.copy(
            qr_s.ap(), pb[mb][:, 0:512]), mb,
            waits=[rope_prev["q2"]], tag=f"{tag}qr")
        rb = mb
        mmg([(pb[rb][:, 0:512], protT.ap(), qr_s.ap())], rb,
            waits=[qtok], tag=f"{tag}rot")
        ch = VChain()
        ch.v(lambda eng, t0=t0: eng.tensor_tensor(
            rp_s.ap(), qr_s.ap(), cosf.ap()[:, t0:t0 + 512], OP.mult),
            waits=[qtok, tok_const_g, rope_prev["ex"]], tag=f"{tag}t1")
        ch.tok = evac("vector", lambda eng, t0=t0, rb=rb, _p=ch.tok:
                      eng.tensor_tensor(
                          rq_s.ap(), pb[rb][:, 0:512],
                          sinf.ap()[:, t0:t0 + 512], OP.mult),
                      rb, waits=[ch.tok], tag=f"{tag}t2")
        t_qp = ch.v(lambda eng: eng.tensor_add(
            rp_s.ap(), rp_s.ap(), rq_s.ap()), tag=f"{tag}qp")
        t_ex = sop(lambda eng: eng.activation(ex_s.ap(), rp_s.ap(), AF.Exp),
                   waits=[t_qp], tag=f"{tag}ex")
        ch.v(lambda eng: eng.tensor_scalar_max(rl_s.ap(), rp_s.ap(), 0.0),
             tag=f"{tag}rl")
        t_q2 = ch.v(lambda eng, dst_ap=dst_ap: eng.scalar_tensor_tensor(
            dst_ap, ex_s.ap(), 1.0, rl_s.ap(), OP.min, OP.add),
            waits=[t_ex], tag=f"{tag}q2")
        rope_prev["q2"] = t_q2
        rope_prev["ex"] = t_ex
        return t_q2

    if upto < 2:
        return _finish()
    for h in range(H):
        s, wtok = wload(wpanel(wqkvT, h * 128, (h + 1) * 128), DC, tag=f"q{h}")
        last_mm = None
        for th in range(TH):
            mb = 2 + th
            items = [(pb[mb][:, 0:512],
                      wslot[s].ap()[:, dc * 128:(dc + 1) * 128],
                      xn_fm.ap()[:, dc * T + th * 512: dc * T + (th + 1) * 512])
                     for dc in range(DC)]
            last_mm = mmg(items, mb, waits=[wtok, tok_xn_all],
                          tag=f"qmm{h}_{th}")
            rope_elu(mb, th, q_fm.ap()[:, h * T + th * 512: h * T + (th + 1) * 512],
                     tag=f"q{h}_{th}_")
        slot_free[s] = last_mm
    tok_q = S.tok("EV")

    if upto < 3:
        return _finish()
    agg_prev = {}
    for h in range(H):
        par = h % 2
        kf, kt, vt, vf, ats = k_fm[par], k_tm[par], v_tm[par], v_fs[par], att_s[par]
        kcol = 2048 + h * 128
        vcol = 4096 + h * 128
        sk, ktok = wload(wpanel(wqkvT, kcol, kcol + 128), DC, tag=f"k{h}")
        sv, vtok = wload(wpanel(wqkvT, vcol, vcol + 128), DC, tag=f"v{h}")
        last_kmm = None
        for th in range(TH):
            mb = th
            items = [(pb[mb][:, 0:512],
                      wslot[sk].ap()[:, dc * 128:(dc + 1) * 128],
                      xn_fm.ap()[:, dc * T + th * 512: dc * T + (th + 1) * 512])
                     for dc in range(DC)]
            last_kmm = mmg(items, mb, waits=[ktok, tok_xn_all],
                           tag=f"kmm{h}_{th}")
            rope_elu(mb, th, kf.ap()[:, th * 512:(th + 1) * 512],
                     tag=f"k{h}_{th}_")
        slot_free[sk] = last_kmm
        tok_k = S.tok("EV")

        last_vmm = None
        for th in range(TH):
            mb = th
            items = [(pb[mb][:, 0:512],
                      wslot[sv].ap()[:, dc * 128:(dc + 1) * 128],
                      xn_fm.ap()[:, dc * T + th * 512: dc * T + (th + 1) * 512])
                     for dc in range(DC)]
            last_vmm = mmg(items, mb, waits=[vtok, tok_xn_all],
                           tag=f"vmm{h}_{th}")
            evac("scalar", lambda eng, th=th, mb=mb, vf=vf: eng.copy(
                vf.ap()[:, th * 512:(th + 1) * 512], pb[mb][:, 0:512]),
                mb, tag=f"evv{h}_{th}")
        slot_free[sv] = last_vmm
        tok_vfs = S.tok("EV")

        for src_buf, dst_buf, wtk, nm in ((kf, kt, tok_k, "k"),
                                          (vf, vt, tok_vfs, "v")):
            for g in range(2):
                tb = g % 2
                for i in range(4):
                    tt = g * 4 + i
                    trans(pb[tb].bitcast(BF16)[:, i * 128:(i + 1) * 128],
                          src_buf.ap()[:, tt * 128:(tt + 1) * 128],
                          identb.ap(), tb,
                          waits=[wtk] if i == 0 else (),
                          tag=f"tr{nm}{h}_{tt}")
                evac("scalar",
                     lambda eng, g=g, tb=tb, dst_buf=dst_buf: eng.copy(
                         dst_buf.ap()[:, g * 512:(g + 1) * 512],
                         pb[tb].bitcast(BF16)[:, 0:512]),
                     tb, tag=f"ev{nm}t{h}_{g}")
        tok_kv = S.tok("EV")

        for c in range(CH):
            ab = 2 + (c % 2)
            nb = 4 + (c % 2)
            gbk = 6 + (c % 2)

            for ms in range(2):
                mmg([(pb[ab][:, ms * 256:(ms + 1) * 256],
                      kf.ap()[:, c * 256 + ms * 128: c * 256 + (ms + 1) * 128],
                      q_fm.ap()[:, h * T + c * 256: h * T + (c + 1) * 256])],
                    ab, waits=[tok_kv, tok_q], tag=f"att{h}_{c}_{ms}")
            t_att = evac("vector", lambda eng, ab=ab, ats=ats: eng.tensor_tensor(
                ats.ap(), pb[ab][:, 0:512], trilm.ap(), OP.mult),
                ab, tag=f"evatt{h}_{c}")

            nitems = [(pb[nb][:, 0:256],
                       vt.ap()[:, (2 * c + ms) * 128:(2 * c + ms + 1) * 128],
                       ats.ap()[:, ms * 256:(ms + 1) * 256])
                      for ms in range(2)]
            mmg(nitems, nb, waits=[t_att], tag=f"num{h}_{c}")
            t_num_mm = bank_data[nb]
            ditems = [(pb[nb][0:1, 256:512],
                       ones_b.ap(),
                       ats.ap()[:, ms * 256:(ms + 1) * 256])
                      for ms in range(2)]
            for j, (o, l, r) in enumerate(ditems):
                S.op("tensor",
                     lambda eng, o=o, l=l, r=r, j=j: eng.matmul(
                         o, l, r, start=(j == 0), stop=(j == 1)),
                     inc=("MM" if j == 1 else None), tag=f"den{h}_{c}_{j}")
            t_den_mm = S.tok("MM")
            bank_data[nb] = t_den_mm

            evac("scalar", lambda eng, h=h, c=c, nb=nb: eng.copy(
                n_acc.ap()[:, h * T + c * 256: h * T + (c + 1) * 256],
                pb[nb][:, 0:256]), nb, waits=[t_num_mm],
                tag=f"evnum{h}_{c}")
            evac("scalar", lambda eng, h=h, c=c, nb=nb: eng.copy(
                den_i.ap()[0:1, h * T + c * 256: h * T + (c + 1) * 256],
                pb[nb][0:1, 256:512]), nb, tag=f"evden{h}_{c}")

            gitems = [(pb[gbk][:, 0:128],
                       kt.ap()[:, (2 * c + ms) * 128:(2 * c + ms + 1) * 128],
                       vt.ap()[:, (2 * c + ms) * 128:(2 * c + ms + 1) * 128])
                      for ms in range(2)]
            mmg(gitems, gbk, waits=[tok_kv], tag=f"G{h}_{c}")
            zitems = [(pb[gbk][:, 128:129],
                       kt.ap()[:, (2 * c + ms) * 128:(2 * c + ms + 1) * 128],
                       ones_b.ap())
                      for ms in range(2)]
            for j, (o, l, r) in enumerate(zitems):
                S.op("tensor",
                     lambda eng, o=o, l=l, r=r, j=j: eng.matmul(
                         o, l, r, start=(j == 0), stop=(j == 1)),
                     inc=("MM" if j == 1 else None), tag=f"z{h}_{c}_{j}")
            bank_data[gbk] = S.tok("MM")

            agg_prev[h] = vop(lambda eng, h=h, gbk=gbk: eng.scalar_tensor_tensor(
                s_agg.ap()[:, h * SW:(h + 1) * SW],
                pb[gbk][:, 0:129], 1.0,
                s_agg.ap()[:, h * SW:(h + 1) * SW], OP.mult, OP.add),
                waits=[bank_data[gbk], tok_msagg, agg_prev.get(h)],
                tag=f"agg{h}_{c}")
            t_g = evac("vector", lambda eng, h=h, c=c, gbk=gbk: eng.tensor_copy(
                g_raw.ap()[:, (h * CH + c) * SW:(h * CH + c + 1) * SW],
                pb[gbk][:, 0:129]), gbk, tag=f"evG{h}_{c}")
    tok_p3_v = S.tok("EV")
    tok_p3_sc = S.tok("SC")
    tok_p3_mm = S.tok("MM")

    if upto < 4:
        return _finish()
    _pre = {hc: (wload(wpanel(sw1T, hc * 128, (hc + 1) * 128), DC,
                       tag=f"su1_{hc}p"),
                 wload(wpanel(sw3T, hc * 128, (hc + 1) * 128), DC,
                       tag=f"su3_{hc}p"))
            for hc in (0, 1)}
    uh_prev = {"h": tok_p3_v}

    def u_phase(hcp, xsrc, xtok, m1src, m3src, tagp, preload=None):
        group = 0 if hcp % 2 == 0 else 4
        for hc in (2 * hcp, 2 * hcp + 1):
            if preload is not None and hc in preload:
                (s1s, w1tok), (s3s, w3tok) = preload[hc]
            else:
                s1s, w1tok = wload(wpanel(m1src, hc * 128, (hc + 1) * 128), DC,
                                   tag=f"{tagp}u1_{hc}")
                s3s, w3tok = wload(wpanel(m3src, hc * 128, (hc + 1) * 128), DC,
                                   tag=f"{tagp}u3_{hc}")
            for th in range(TH):
                b1 = group + th
                b3 = group + 2 + th
                items1 = [(pb[b1][:, 0:512],
                           wslot[s1s].ap()[:, dc * 128:(dc + 1) * 128],
                           xsrc.ap()[:, dc * T + th * 512: dc * T + (th + 1) * 512])
                          for dc in range(DC)]
                mmg(items1, b1, waits=[w1tok, xtok], tag=f"{tagp}u1mm{hc}_{th}")
                items3 = [(pb[b3][:, 0:512],
                           wslot[s3s].ap()[:, dc * 128:(dc + 1) * 128],
                           xsrc.ap()[:, dc * T + th * 512: dc * T + (th + 1) * 512])
                          for dc in range(DC)]
                mmg(items3, b3, waits=[w3tok, xtok], tag=f"{tagp}u3mm{hc}_{th}")
                t_s1 = evac("scalar", lambda eng, b1=b1: eng.activation(
                    s1_scr.ap(), pb[b1][:, 0:512], AF.Silu), b1,
                    waits=[uh_prev["h"]], tag=f"{tagp}s1{hc}_{th}")
                t_t3 = evac("scalar", lambda eng, b3=b3: eng.copy(
                    t3_scr.ap(), pb[b3][:, 0:512]), b3,
                    tag=f"{tagp}t3{hc}_{th}")
                uh_prev["h"] = vop(lambda eng, hc=hc, th=th: eng.tensor_tensor(
                    h_e.ap()[:, hc * T + th * 512: hc * T + (th + 1) * 512],
                    s1_scr.ap(), t3_scr.ap(), OP.mult),
                    waits=[t_s1, t_t3], tag=f"{tagp}h{hc}_{th}")
            slot_free[s1s] = S.tok("MM")
            slot_free[s3s] = S.tok("MM")

    rk = {"i": 0, "acc": [None, None], "free": [None, None]}
    ccbox = {}

    def plan_rank_batch(n):
        for _ in range(n):
            i = rk["i"]
            if i >= 2 * NCORES:
                return
            rk["i"] += 1
            r, hh = i // 2, i % 2
            rtok = dma("sync", "RK0",
                       rank_h[hh].ap().rearrange("p (h s) -> p h s", h=8),
                       agg_out[r][hh * 1024:(hh + 1) * 1024].rearrange(
                           "(h p) s -> p h s", p=128),
                       waits=[ccbox["cc"], rk["free"][hh]], tag=f"rk{r}_{hh}")
            rk["acc"][hh] = vop(lambda eng, r=r, hh=hh:
                                eng.scalar_tensor_tensor(
                s_agg.ap()[:, hh * 8 * SW:(hh + 1) * 8 * SW],
                rank_h[hh].ap(), pmask_sb.ap()[:, r:r + 1],
                s_agg.ap()[:, hh * 8 * SW:(hh + 1) * 8 * SW],
                OP.mult, OP.add),
                waits=[rtok, rk["acc"][hh]], tag=f"rka{r}_{hh}")
            rk["free"][hh] = rk["acc"][hh]

    for hcp in range(HC // 2):
        u_phase(hcp, xn_fm, tok_xn_all, sw1T, sw3T, "su",
                preload=_pre if hcp == 0 else None)
        if hcp == 1:
            t_aggd = dma("sync", "LA",
                         agg_in.ap().rearrange("(h p) s -> p h s", p=128),
                         s_agg.ap().rearrange("p (h s) -> p h s", h=H),
                         waits=[agg_prev[H - 1]], tag="aggout")

            def f_cc(eng):
                return eng.collective_compute(
                    "AllGather", OP.bypass,
                    replica_groups=[list(range(NCORES))],
                    ins=[agg_in.ap().opt()], outs=[agg_out.ap().opt()])
            ccbox["cc"] = S.op("gpsimd", f_cc, waits=[t_aggd], inc="CC",
                               tag="cc")
            t_clr = vop(lambda eng: eng.memset(s_agg.ap(), 0.0),
                        waits=[t_aggd], tag="clr_sagg")
            rk["acc"] = [t_clr, t_clr]
            rk["free"] = [t_clr, t_clr]
        if hcp >= 2:
            plan_rank_batch(8)
    tok_hsh = S.tok("EV")

    acc_half = rk["acc"]
    prev_cast_mm = [None, None]
    prev_num_rd = [None, None]
    prev_den_rd = [None, None]
    prev_dbc = [None, None]
    init_free = [list(bank_free[i]) for i in range(8)]
    for i in range(8):
        bank_free[i] = []
    ones1 = ones_b.ap()[0:1, 0:1]

    def plan_chainA(h):
        p = h % 2
        ca = VChain()
        cast_toks = []
        for c in range(CH):
            w = [acc_half[h // 8], prev_cast_mm[p]] if c == 0 else []
            cast_toks.append(ca.v(
                lambda eng, h=h, c=c, p=p: eng.tensor_copy(
                    sb4[p].ap()[:, c * SW:(c + 1) * SW],
                    s_agg.ap()[:, h * SW:(h + 1) * SW]),
                waits=w, tag=f"cs{h}_{c}"))
            if c < CH - 1:
                ca.v(lambda eng, h=h, c=c: eng.tensor_tensor(
                    s_agg.ap()[:, h * SW:(h + 1) * SW],
                    s_agg.ap()[:, h * SW:(h + 1) * SW],
                    g_raw.ap()[:, (h * CH + c) * SW:(h * CH + c + 1) * SW],
                    OP.add), tag=f"sr{h}_{c}")
        return cast_toks

    all_casts = {0: plan_chainA(0)}
    pipe = {}

    def plan_dbc_ao(h):
        st = pipe.pop(h)
        p = h % 2
        pdD = pd[2 + p].ap()
        dbc_toks = []
        for half in range(2):
            tok = S.op("tensor", lambda eng, half=half, pdD=pdD: eng.matmul(
                pdD[:, half * 512:(half + 1) * 512], ones_r.ap(),
                st["rc_ap"][0:1, half * 512:(half + 1) * 512],
                start=True, stop=True),
                waits=[st["rc"]], inc="MM", tag=f"dbc{h}_{half}")
            dbc_toks.append(tok)
        prev_dbc[p] = dbc_toks[-1]
        t_ao = vop(lambda eng, h=h, p=p, pdD=pdD: eng.tensor_tensor(
            n_acc.ap()[:, h * T:(h + 1) * T], at_b[p].ap(), pdD[:, 0:1024],
            OP.mult), waits=[dbc_toks[-1], st["nt"]], tag=f"ao{h}")
        prev_den_rd[p] = t_ao
        return dbc_toks[-1]

    for h in range(H):
        p = h % 2
        pdN = pd[p].ap()
        pdD = pd[2 + p].ap()
        q_h = q_fm.ap()[:, h * T:(h + 1) * T]
        cast_toks = all_casts[h]
        if h - 2 in pipe:
            plan_dbc_ao(h - 2)
        for c in range(CH):
            wN = [cast_toks[c], prev_num_rd[p]] + (init_free[2 * p] +
                 init_free[2 * p + 1] if c == 0 and h < 2 else [])
            S.op("tensor", lambda eng, h=h, c=c, p=p, pdN=pdN, q_h=q_h:
                 eng.matmul(pdN[:, c * 256:(c + 1) * 256],
                            sb4[p].ap()[:, c * SW:c * SW + 128],
                            q_h[:, c * 256:(c + 1) * 256],
                            start=True, stop=True),
                 waits=wN, inc="MM", tag=f"xn{h}_{c}")
            wD = [prev_den_rd[p], tok_p3_sc] + (init_free[4 + 2 * p] +
                 init_free[5 + 2 * p] if c == 0 and h < 2 else [])
            S.op("tensor", lambda eng, h=h, c=c, pdD=pdD: eng.matmul(
                pdD[0:1, c * 256:(c + 1) * 256], ones1,
                den_i.ap()[0:1, h * T + c * 256: h * T + (c + 1) * 256],
                start=True, stop=False),
                waits=wD, tag=f"xdi{h}_{c}")
            S.op("tensor", lambda eng, h=h, c=c, p=p, pdD=pdD, q_h=q_h:
                 eng.matmul(pdD[0:1, c * 256:(c + 1) * 256],
                            sb4[p].ap()[:, c * SW + 128:c * SW + 129],
                            q_h[:, c * 256:(c + 1) * 256],
                            start=False, stop=True),
                 inc="MM", tag=f"xdz{h}_{c}")
        t_mms = S.tok("MM")
        prev_cast_mm[p] = t_mms
        if h + 1 < H:
            all_casts[h + 1] = plan_chainA(h + 1)
        rc_ap = (rc_b if p == 0 else rc_s2).ap()[0:1, 0:1024]
        dn_ap = dn_b.ap()[0:1, 0:1024]
        t_mx = vop(lambda eng, pdD=pdD, dn_ap=dn_ap: eng.tensor_scalar_max(
            dn_ap, pdD[0:1, 0:1024], 1e-6),
            waits=[t_mms, prev_dbc[p]], tag=f"mx{h}")
        def f_rc(eng, rc_ap=rc_ap, dn_ap=dn_ap):
            with eng.bass.allow_low_precision(
                    reason="bf16 den/recip: within calibrated tolerance"):
                return eng.reciprocal(rc_ap, dn_ap)
        t_rc = vop(f_rc, waits=[t_mx, prev_dbc[p]], tag=f"rc{h}")
        t_nt = vop(lambda eng, h=h, p=p, pdN=pdN: eng.scalar_tensor_tensor(
            at_b[p].ap(), pdN[:, 0:1024], 1.0,
            n_acc.ap()[:, h * T:(h + 1) * T], OP.mult, OP.add),
            waits=[t_mms], tag=f"nt{h}")
        prev_num_rd[p] = t_nt
        pipe[h] = {"rc": t_rc, "rc_ap": rc_ap, "nt": t_nt}
    plan_dbc_ao(H - 2)
    plan_dbc_ao(H - 1)
    for p in range(2):
        bank_free[2 * p].append(prev_num_rd[p])
        bank_free[2 * p + 1].append(prev_num_rd[p])
        bank_free[4 + 2 * p].append(prev_den_rd[p])
        bank_free[5 + 2 * p].append(prev_den_rd[p])
    tok_attn = S.tok("EV")

    if upto < 5:
        return _finish()
    y_banks = [0, 1, 2, 3]
    for oc in range(OC):
        s, wtok = wload(wpanel(woutT, oc * 128, (oc + 1) * 128), DC,
                        tag=f"wo{oc}")
        last_mm = None
        for th in range(TH):
            yb = y_banks[(oc * TH + th) % 4]
            items = [(pb[yb][:, 0:512],
                      wslot[s].ap()[:, hh * 128:(hh + 1) * 128],
                      n_acc.ap()[:, hh * T + th * 512: hh * T + (th + 1) * 512])
                     for hh in range(H)]
            last_mm = mmg(items, yb, waits=[wtok, tok_attn], tag=f"womm{oc}_{th}")
            evac("vector", lambda eng, oc=oc, th=th, yb=yb: eng.tensor_copy(
                y_acc.ap()[:, oc * T + th * 512: oc * T + (th + 1) * 512],
                pb[yb][:, 0:512]), yb, tag=f"evwo{oc}_{th}")
        slot_free[s] = last_mm
    tok_y0 = S.tok("EV")

    if upto < 6:
        return _finish()
    for oc in range(OC):
        s, wtok = wload(wpanel(sw2T, oc * 128, (oc + 1) * 128), HC,
                        tag=f"sw2_{oc}")
        last_mm = None
        for th in range(TH):
            yb = y_banks[(oc * TH + th) % 4]
            items = [(pb[yb][:, 0:512],
                      wslot[s].ap()[:, hc * 128:(hc + 1) * 128],
                      h_e.ap()[:, hc * T + th * 512: hc * T + (th + 1) * 512])
                     for hc in range(HC)]
            last_mm = mmg(items, yb, waits=[wtok, tok_hsh, tok_y0],
                          tag=f"symm{oc}_{th}")
            evac("vector", lambda eng, oc=oc, th=th, yb=yb: eng.tensor_tensor(
                y_acc.ap()[:, oc * T + th * 512: oc * T + (th + 1) * 512],
                y_acc.ap()[:, oc * T + th * 512: oc * T + (th + 1) * 512],
                pb[yb][:, 0:512], OP.add), yb, tag=f"evsy{oc}_{th}")
        slot_free[s] = last_mm

    if upto < 7:
        return _finish()
    prev_gbc_gate = [tok_attn]

    def plan_gbc_xng(e, extra_wait):
        gb_toks = []
        for th in range(TH):
            gband = 6 + th
            for i, tt in enumerate(range(th * 4, th * 4 + 4)):
                trans(pb[gband][0:1, i * 128:(i + 1) * 128],
                      g_tm2.ap()[:, tt * E + e: tt * E + e + 1],
                      identf.ap(), gband, waits=[tok_gtm], tag=f"gtr{e}_{tt}")
            t_gs = evac("vector", lambda eng, gband=gband: eng.tensor_copy(
                g_stage.ap(), pb[gband][0:1, 0:512]), gband,
                waits=[prev_gbc_gate[0]], tag=f"gst{e}_{th}")
            bb = 6 + th
            mmg([(pb[bb][:, 0:512], ones_r.ap(), g_stage.ap())], bb,
                waits=[t_gs], tag=f"gbb{e}_{th}")
            gb_toks.append(evac("vector",
                                lambda eng, th=th, bb=bb: eng.tensor_copy(
                                    gbc.ap()[:, th * 512:(th + 1) * 512],
                                    pb[bb][:, 0:512]), bb,
                                waits=[prev_gbc_gate[0]], tag=f"gbe{e}_{th}"))
        xng_tok = None
        for dc in range(DC):
            xng_tok = vop(lambda eng, dc=dc: eng.tensor_tensor(
                xng.ap()[:, dc * T:(dc + 1) * T],
                xn_fm.ap()[:, dc * T:(dc + 1) * T], gbc.ap(), OP.mult),
                waits=(gb_toks + [extra_wait]) if dc == 0 else (),
                tag=f"xng{e}_{dc}")
        prev_gbc_gate[0] = xng_tok
        return xng_tok

    xng_tok = plan_gbc_xng(0, None)
    for e in range(E):
        for hcp in range(HC // 2):
            u_phase(hcp, xng, xng_tok, w1T[e], w3T[e], f"e{e}")
        tok_umm = S.tok("MM")
        tok_he = S.tok("EV")
        if e + 1 < E:
            xng_tok = plan_gbc_xng(e + 1, tok_umm)
        for oc in range(OC):
            s, wtok = wload(wpanel(w2T[e], oc * 128, (oc + 1) * 128), HC,
                            tag=f"e{e}w2_{oc}")
            last_mm = None
            for th in range(TH):
                yb = y_banks[(oc * TH + th) % 4]
                items = [(pb[yb][:, 0:512],
                          wslot[s].ap()[:, hc * 128:(hc + 1) * 128],
                          h_e.ap()[:, hc * T + th * 512: hc * T + (th + 1) * 512])
                         for hc in range(HC)]
                last_mm = mmg(items, yb, waits=[wtok, tok_he],
                              tag=f"e{e}ymm{oc}_{th}")
                evac("vector", lambda eng, oc=oc, th=th, yb=yb: eng.tensor_tensor(
                    y_acc.ap()[:, oc * T + th * 512: oc * T + (th + 1) * 512],
                    y_acc.ap()[:, oc * T + th * 512: oc * T + (th + 1) * 512],
                    pb[yb][:, 0:512], OP.add), yb, tag=f"e{e}ev{oc}_{th}")
            slot_free[s] = last_mm
    tok_yfull = S.tok("EV")

    if upto < 8:
        return _finish()
    tok_mm_p8 = S.tok("MM")
    st_tok = [None, None]
    xr_tok = [None, None]
    xr_free = [None, None]
    for tt in range(TT):
        b = tt % 2
        xr_tok[b] = dma("scalar", f"LX{b}", x_rd[b].ap(),
                        x_in[tt * 128:(tt + 1) * 128, :],
                        waits=[xr_free[b], tok_mm_p8], tag=f"xrd{tt}")
        last_add = None
        for g in range(4):
            tb = g % 4
            for i in range(4):
                oc = g * 4 + i
                trans(pb[tb][:, i * 128:(i + 1) * 128],
                      y_acc.ap()[:, oc * T + tt * 128: oc * T + (tt + 1) * 128],
                      identf.ap(), tb, waits=[tok_yfull], tag=f"try{tt}_{oc}")
            last_add = evac("vector",
                            lambda eng, g=g, tb=tb, b=b: eng.tensor_tensor(
                                out_st[b].ap()[:, g * 512:(g + 1) * 512],
                                pb[tb][:, 0:512],
                                x_rd[b].ap()[:, g * 512:(g + 1) * 512],
                                OP.add),
                            tb, waits=[xr_tok[b], st_tok[b]],
                            tag=f"oadd{tt}_{g}")
        xr_free[b] = last_add
        st_tok[b] = dma("sync", "ST" if b == 0 else "ST1",
                        out_t[tt * 128:(tt + 1) * 128, :],
                        out_st[b].ap(), waits=[last_add], tag=f"ost{tt}")

    S.op("sync", lambda eng: None, waits=[S.tok("ST"), S.tok("ST1")],
         tag="fin")

    S.check()
    return nc, S, SEMS


_CACHE = {}


def _host_constants():
    inv = 1.0 / (10000.0 ** (np.arange(0, 128, 2, dtype=np.float64) / 128.0))
    invf = np.concatenate([inv, inv])
    ident = np.eye(128, dtype=np.float32)
    protT = np.zeros((128, 128), dtype=np.float32)
    for i in range(64):
        protT[2 * i + 1, 2 * i] = -1.0
        protT[2 * i, 2 * i + 1] = 1.0
    trilm = np.zeros((128, 512), dtype=np.float32)
    for s_ in range(2):
        for p in range(128):
            for l in range(256):
                if s_ * 128 + p <= l:
                    trilm[p, s_ * 256 + l] = 1.0
    ones = np.ones((128, 128), dtype=np.float32)
    return invf, ident, protT, trilm, ones


def kernel(L0_attn_x, L0_attn_W_qkv, L0_attn_W_out, L0_ffn_expert_bias,
           L0_ffn_gate_w, L0_ffn_w1, L0_ffn_w2, L0_ffn_w3, L0_ffn_sw1,
           L0_ffn_sw2, L0_ffn_sw3):
    x = np.ascontiguousarray(np.asarray(L0_attn_x, dtype=np.float32)
                             .reshape(NCORES * T, D))
    wqkvT = np.ascontiguousarray(np.asarray(L0_attn_W_qkv, np.float32).T)
    woutT = np.ascontiguousarray(np.asarray(L0_attn_W_out, np.float32).T)
    gwT = np.ascontiguousarray(np.asarray(L0_ffn_gate_w, np.float32).T)
    w1T = np.ascontiguousarray(np.asarray(L0_ffn_w1, np.float32)
                               .transpose(0, 2, 1))
    w3T = np.ascontiguousarray(np.asarray(L0_ffn_w3, np.float32)
                               .transpose(0, 2, 1))
    w2T = np.ascontiguousarray(np.asarray(L0_ffn_w2, np.float32)
                               .transpose(0, 2, 1))
    sw1T = np.ascontiguousarray(np.asarray(L0_ffn_sw1, np.float32).T)
    sw3T = np.ascontiguousarray(np.asarray(L0_ffn_sw3, np.float32).T)
    sw2T = np.ascontiguousarray(np.asarray(L0_ffn_sw2, np.float32).T)
    ebias = np.ascontiguousarray(
        np.broadcast_to(np.asarray(L0_ffn_expert_bias, np.float32), (128, E)))

    invf, ident, protT, trilm, ones = _host_constants()

    if "nc" not in _CACHE:
        nc, S, SEMS = build_program()
        _CACHE["nc"] = nc
        _CACHE["S"] = S
        _CACHE["SEMS"] = SEMS
    nc, S, SEMS = _CACHE["nc"], _CACHE["S"], _CACHE["SEMS"]

    in_maps = []
    for c in range(NCORES):
        pos = np.arange(c * T, (c + 1) * T, dtype=np.float64)
        ang = pos[None, :] * invf[:, None]
        cosf = np.ascontiguousarray(np.cos(ang).astype(np.float32))
        sinf = np.ascontiguousarray(np.sin(ang).astype(np.float32))
        pmask = np.zeros((128, E), np.float32)
        pmask[:, :c] = 1.0
        in_maps.append({
            "x": x[c * T:(c + 1) * T],
            "wqkvT": wqkvT, "woutT": woutT, "gwT": gwT,
            "w1T": w1T, "w3T": w3T, "w2T": w2T,
            "sw1T": sw1T, "sw3T": sw3T, "sw2T": sw2T,
            "ebias": ebias, "pmask": pmask,
            "cosf": cosf, "sinf": sinf,
            "identf": ident, "protT": protT, "trilm": trilm, "onesin": ones,
        })

    res = _run(nc, S, SEMS, in_maps)
    out = np.concatenate([res[c]["out"] for c in range(NCORES)], axis=0)
    return out.reshape(1, NCORES * T, D).astype(np.float32)


def _run(nc, S, SEMS, in_maps):
    if "emitted" not in _CACHE:
        with ExitStack() as ctx:
            sems = {s: ctx.enter_context(nc.semaphore(s)) for s in SEMS}
            block = ctx.enter_context(nc.Block())
            mk = S.emit(block, sems)
            block.tensor(mk("tensor"))
            block.vector(mk("vector"))
            block.scalar(mk("scalar"))
            block.sync(mk("sync"))
            block.gpsimd(mk("gpsimd"))
        _CACHE["emitted"] = True
    res = run_bass_kernel_spmd(nc, in_maps, core_ids=list(range(NCORES)))
    return res.results


# version 12
# speedup vs baseline: 1.0438x; 1.0263x over previous
from collections import defaultdict
from contextlib import ExitStack

import numpy as np

import concourse.bass as bass
import concourse.mybir as mybir
from concourse.bass_utils import run_bass_kernel_spmd

F32 = mybir.dt.float32
BF16 = mybir.dt.bfloat16
AX = mybir.AxisListType.X
OP = mybir.AluOpType
AF = mybir.ActivationFunctionType

NCORES = 8
T = 1024
TT = 8
TH = 2
D = 2048
DC = 16
H = 16
CH = 4
E = 8
HID = 1024
HC = 8
OC = 16
SW = 129

ENGINES = ("tensor", "vector", "scalar", "sync", "gpsimd")


class Sched:

    def __init__(self):
        self.streams = {e: [] for e in ENGINES}
        self.cnt = defaultdict(int)

    def op(self, eng, fn, waits=(), inc=None, n=1, tag=""):
        waits = tuple(w for w in waits if w is not None)
        val = None
        if inc is not None:
            self.cnt[inc] += n
            val = self.cnt[inc]
        self.streams[eng].append((fn, waits, inc, n, tag))
        return (inc, val) if inc else None

    def tok(self, sem):
        return (sem, self.cnt[sem])

    def check(self):
        pos = {e: 0 for e in ENGINES}
        val = defaultdict(int)
        while True:
            progressed = False
            for e in ENGINES:
                while pos[e] < len(self.streams[e]):
                    fn, waits, inc, n, tag = self.streams[e][pos[e]]
                    if any(val[s] < w for s, w in waits):
                        break
                    if inc is not None:
                        val[inc] += n
                    pos[e] += 1
                    progressed = True
            if all(pos[e] == len(self.streams[e]) for e in ENGINES):
                return
            if not progressed:
                stuck = {
                    e: (pos[e], len(self.streams[e]),
                        self.streams[e][pos[e]][1], self.streams[e][pos[e]][4])
                    for e in ENGINES if pos[e] < len(self.streams[e])
                }
                raise RuntimeError(
                    f"schedule deadlock; sems={dict(val)}; stuck={stuck}")

    def emit(self, block, sems):
        water = {e: defaultdict(int) for e in ENGINES}

        def mk(eng_name):
            def body(eng):
                for fn, waits, inc, n, tag in self.streams[eng_name]:
                    for s, w in waits:
                        if w > water[eng_name][s]:
                            eng.wait_ge(sems[s], w)
                            water[eng_name][s] = w
                    ins = fn(eng)
                    if inc is not None:
                        assert ins is not None, f"{tag}: inc op returned None"
                        ins.then_inc(sems[inc], n)
            return body
        return mk


def build_program(upto=99, dump=None, feats=()):
    nc = bass.Bass("TRN2", target_bir_lowering=False,
                   detect_race_conditions=False)

    x_in = nc.dram_tensor("x", [T, D], F32, kind="ExternalInput").ap()
    wqkvT = nc.dram_tensor("wqkvT", [D, 3 * D], F32, kind="ExternalInput").ap()
    woutT = nc.dram_tensor("woutT", [D, D], F32, kind="ExternalInput").ap()
    gwT = nc.dram_tensor("gwT", [D, E], F32, kind="ExternalInput").ap()
    w1T = nc.dram_tensor("w1T", [E, D, HID], F32, kind="ExternalInput").ap()
    w3T = nc.dram_tensor("w3T", [E, D, HID], F32, kind="ExternalInput").ap()
    w2T = nc.dram_tensor("w2T", [E, HID, D], F32, kind="ExternalInput").ap()
    sw1T = nc.dram_tensor("sw1T", [D, HID], F32, kind="ExternalInput").ap()
    sw3T = nc.dram_tensor("sw3T", [D, HID], F32, kind="ExternalInput").ap()
    sw2T = nc.dram_tensor("sw2T", [HID, D], F32, kind="ExternalInput").ap()
    ebias_in = nc.dram_tensor("ebias", [128, E], F32, kind="ExternalInput").ap()
    pmask_in = nc.dram_tensor("pmask", [128, E], F32, kind="ExternalInput").ap()
    cosf_in = nc.dram_tensor("cosf", [128, T], F32, kind="ExternalInput").ap()
    sinf_in = nc.dram_tensor("sinf", [128, T], F32, kind="ExternalInput").ap()
    identf_in = nc.dram_tensor("identf", [128, 128], F32, kind="ExternalInput").ap()
    protT_in = nc.dram_tensor("protT", [128, 128], F32, kind="ExternalInput").ap()
    trilm_in = nc.dram_tensor("trilm", [128, 512], F32, kind="ExternalInput").ap()
    ones_in = nc.dram_tensor("onesin", [128, 128], F32, kind="ExternalInput").ap()
    out_t = nc.dram_tensor("out", [T, D], F32, kind="ExternalOutput").ap()

    agg_in = nc.dram_tensor("agg_in", [H * 128, SW], F32)
    agg_out = nc.dram_tensor("agg_out", [NCORES, H * 128, SW], F32,
                             addr_space="Shared")

    LIMIT = 212608
    arena = nc.alloc_sbuf_tensor("arena", [128, LIMIT], mybir.dt.uint8)
    base = nc.lookup_mloc(arena).addr
    nxt = [0]
    placed = []

    def at(name, shape, dtype, off):
        return nc.alloc_sbuf_tensor_at(name, shape, dtype, offset=base + off)

    def seq(name, shape, dtype):
        off = (nxt[0] + 31) // 32 * 32
        per = int(np.prod(shape[1:])) * mybir.dt.size(dtype)
        nxt[0] = off + per
        assert nxt[0] <= LIMIT, f"SBUF overflow at {name}: {nxt[0]}"
        placed.append((name, off, per))
        return at(name, shape, dtype, off), off

    xn_fm, _ = seq("xn_fm", [128, DC * T], BF16)

    regA, offA = seq("regA", [128, 16384], F32)
    q_fm = at("q_fm", [128, H * T], BF16, offA)
    den_i = at("den_i", [1, H * T], BF16, offA + 32768)
    y_acc = regA

    regB, offB = seq("regB", [128, 8192], F32)
    n_acc = at("n_acc", [128, H * T], BF16, offB)
    logit_sb = at("logit_sb", [E, T], F32, offB)
    score_tm = at("score_tm", [128, TT * E], F32, offB + 4096)
    g_scr = at("g_scr", [128, 3 * E], F32, offB + 4352)
    x_rd = [at(f"x_rd{i}", [128, D], F32, offB + i * 8192) for i in range(2)]
    out_st = [at(f"out_st{i}", [128, D], F32, offB + 16384 + i * 8192)
              for i in range(2)]

    regC, offC = seq("regC", [128, 16384], BF16)
    g_raw = at("g_raw", [128, H * CH * SW], BF16, offC)
    s_agg = at("s_agg", [128, H * SW], F32, offC + 16512)
    lnst = at("lnst", [128, 256], F32, offC + 16512)
    cosf = at("cosf_sb", [128, T], BF16, offC + 24768)
    sinf = at("sinf_sb", [128, T], BF16, offC + 26816)
    xng = regC

    regD, offD = seq("regD", [128, 8192], BF16)
    h_e = regD
    x_t0 = at("x_t0", [128, D], F32, offD)
    xn_scr = at("xn_scr", [128, D], F32, offD + 8192)

    regE, offE = seq("regE", [128, 8256], mybir.dt.uint8)
    x_t1 = at("x_t1", [128, D], F32, offE)
    qr_s = at("qr_s", [128, 512], BF16, offD + 7168)
    rp_s = at("rp_s", [128, 512], BF16, offD + 8192)
    rq_s = at("rq_s", [128, 512], BF16, offD + 9216)
    ex_s = at("ex_s", [128, 512], BF16, offD + 10240)
    rl_s = at("rl_s", [128, 512], BF16, offD + 11264)
    _abase = [offE, offD]
    k_tm = [at(f"k_tm{i}", [128, T], BF16, _abase[i]) for i in range(2)]
    v_tm = [at(f"v_tm{i}", [128, T], BF16, _abase[i] + 2048) for i in range(2)]
    v_fs = [at(f"v_fs{i}", [128, T], BF16, _abase[i] + 4096) for i in range(2)]
    att_s = [at(f"att_s{i}", [128, 512], BF16, _abase[i] + 4096)
             for i in range(2)]
    k_fm = [at(f"k_fm{i}", [128, T], BF16, _abase[i] + 6144) for i in range(2)]
    rank_h = [at(f"rank_h{i}", [128, 8 * SW], F32, offE + i * 4128)
              for i in range(2)]
    sb4 = [at(f"s_bf4{i}", [128, 4 * SW], BF16, offC + 28864 + i * 1056)
           for i in range(2)]
    dn_b = at("dn_b", [1, 1024], BF16, offC + 24768)
    rc_b = at("rc_b", [1, 1024], BF16, offC + 26816)
    gbc = at("gbc", [128, T], BF16, offE)

    _, offW = seq("warena", [128, 8192], BF16)
    wslot = [at(f"wslot{i}", [128, DC * 128], BF16, offW + i * 4096)
             for i in range(4)]

    _, offS = seq("regS", [128, 3072], mybir.dt.uint8)
    trilm = at("trilm", [128, 512], BF16, offS)
    xnf32 = [at(f"xnf32_{i}", [128, 128], F32, offS + 1024 + i * 512)
             for i in range(2)]
    s1_scr = at("s1_scr", [128, 512], BF16, offS)
    t3_scr = at("t3_scr", [128, 512], BF16, offS + 1024)
    at_b = [at("at_b0", [128, 1024], BF16, offE + 2048),
            at("at_b1", [128, 1024], BF16, offE)]
    rc_p = [None, None]
    gwT_sb = at("gwT_sb", [128, DC * E], F32, offS + 2048)
    identf, _ = seq("identf", [128, 128], F32)
    identb, _ = seq("identb", [128, 128], BF16)
    protT, _ = seq("protT", [128, 128], BF16)
    ones_b, _ = seq("ones_b", [128, 1], BF16)
    ones_r, _ = seq("ones_r", [1, 128], BF16)
    ones_f, _ = seq("ones_f", [1, 128], F32)
    ebias_sb, _ = seq("ebias_sb", [128, E], F32)
    pmask_sb, _ = seq("pmask_sb", [128, E], F32)
    rc_s2 = at("rc_s2", [1, 1024], BF16, offS)
    g_tm2, _ = seq("g_tm2", [128, TT * E], F32)
    g_stage = at("g_stage", [1, 512], BF16, offS + 2048)
    mu_sc, _ = seq("mu_sc", [128, 8], F32)

    pd = [nc.alloc_psum_tensor(f"pd{i}", [128, 1024], F32) for i in range(4)]
    pb = [pd[i // 2].ap()[:, (i % 2) * 512:(i % 2) * 512 + 512]
          for i in range(8)]

    S = Sched()
    SEMS = ("LX0", "LX1", "LW0", "LW1", "LW2", "LW3", "LC", "LG", "LA",
            "RK0", "RK1", "MM", "EV", "SC", "CC", "ST", "ST1", "GP")
    dump_map = {}


    bank_free = [[] for _ in range(8)]
    bank_data = [None] * 8

    def dma(q, sem, out_ap, in_ap, waits=(), tag="", slow=False):
        waits = list(waits)
        if S.cnt[sem]:
            waits.append((sem, S.cnt[sem]))
        def f(eng, o=out_ap, i=in_ap, slow=slow):
            return eng.dma_start(out=o, in_=i,
                                 allow_slow_non_contiguous=slow)
        return S.op(q, f, waits=waits, inc=sem, n=16, tag=tag)

    def mmg(items, b, waits=(), tag=""):
        w = list(waits) + bank_free[b]
        bank_free[b] = []
        n = len(items)
        tok = None
        for j, (o, l, r) in enumerate(items):
            def f(eng, o=o, l=l, r=r, j=j, n=n):
                return eng.matmul(o, l, r, start=(j == 0), stop=(j == n - 1))
            tok = S.op("tensor", f, waits=(w if j == 0 else ()),
                       inc=("MM" if j == n - 1 else None), tag=tag)
        bank_data[b] = tok
        return tok

    def trans(out_ap, in_ap, ident_ap, b, waits=(), tag=""):
        w = list(waits) + bank_free[b]
        bank_free[b] = []
        def f(eng, o=out_ap, i=in_ap, idn=ident_ap):
            return eng.transpose(o, i, idn)
        tok = S.op("tensor", f, waits=w, inc="MM", tag=tag)
        bank_data[b] = tok
        return tok

    def evac(eng_name, fn, b, waits=(), last=True, tag=""):
        w = list(waits)
        if bank_data[b] is not None:
            w.append(bank_data[b])
        sem = {"vector": "EV", "scalar": "SC"}[eng_name]
        tok = S.op(eng_name, fn, waits=w, inc=sem, tag=tag)
        bank_free[b].append(tok)
        return tok

    def vop(fn, waits=(), inc="EV", tag=""):
        return S.op("vector", fn, waits=waits, inc=inc, tag=tag)

    def sop(fn, waits=(), inc="SC", tag=""):
        return S.op("scalar", fn, waits=waits, inc=inc, tag=tag)

    def gop(fn, waits=(), inc="GP", tag=""):
        return S.op("gpsimd", fn, waits=waits, inc=inc, tag=tag)

    class VChain:

        def __init__(self, tok=None):
            self.tok = tok

        def v(self, fn, waits=(), tag=""):
            self.tok = vop(fn, waits=list(waits) + [self.tok], tag=tag)
            return self.tok

        def s(self, fn, waits=(), tag=""):
            self.tok = sop(fn, waits=list(waits) + [self.tok], tag=tag)
            return self.tok

    dump_map.update(xn=(xn_fm, BF16), q=(q_fm, BF16), nacc=(n_acc, BF16),
                    sagg=(s_agg, F32), yacc=(y_acc, F32), graw=(g_raw, BF16),
                    xnscr=(xn_scr, F32), xt0=(x_t0, F32), mu=(mu_sc, F32),
                    )

    def _finish():
        if dump is not None:
            buf, dt_ = dump_map[dump]
            ncol = buf.shape[1]
            view3 = out_t.rearrange("(o p) d -> p o d", p=128)
            w = [("EV", S.cnt["EV"]), ("SC", S.cnt["SC"]),
                 ("MM", S.cnt["MM"])]
            nfull, rem = divmod(ncol, D)
            if nfull:
                dma("gpsimd", "ST", view3[:, 0:nfull, :],
                    buf.ap()[:, 0:nfull * D].rearrange("p (o d) -> p o d", d=D),
                    waits=w, tag="dump")
            if rem:
                dma("gpsimd", "ST", view3[:, nfull:nfull + 1, 0:rem],
                    buf.ap()[:, nfull * D:ncol].rearrange(
                        "p (o d) -> p o d", o=1),
                    waits=w, tag="dumpr")
        S.op("sync", lambda eng: None, waits=[S.tok("ST")], tag="fin")
        S.check()
        return nc, S, SEMS

    dma("sync", "LC", identf.ap(), identf_in, tag="identf")
    dma("sync", "LC", ebias_sb.ap(), ebias_in, tag="ebias")
    dma("sync", "LC", pmask_sb.ap(), pmask_in, tag="pmask")
    dma("sync", "LC", ones_f.ap(), ones_in[0:1, :], tag="onesf")
    dma("sync", "LC", gwT_sb.ap().rearrange("p (c e) -> p c e", c=DC),
        gwT.rearrange("(c p) e -> p c e", p=128), tag="gwT")
    dma("gpsimd", "LG", identb.ap(), identf_in, tag="identb")
    dma("gpsimd", "LG", protT.ap(), protT_in, tag="protT")
    dma("gpsimd", "LG", trilm.ap(), trilm_in, tag="trilm")
    dma("gpsimd", "LG", ones_b.ap(), ones_in[:, 0:1], tag="onesb", slow=True)
    dma("gpsimd", "LG", ones_r.ap(), ones_in[0:1, :], tag="onesr")
    dma("gpsimd", "LG", cosf.ap(), cosf_in, tag="cosf")
    dma("gpsimd", "LG", sinf.ap(), sinf_in, tag="sinf")
    tok_const_g = S.tok("LG")
    tok_const_s = S.tok("LC")



    x_bufs = [x_t0, x_t1]
    x_tok = [None, None]
    x_free = [None, None]
    for tt in range(2):
        b = tt % 2
        x_tok[b] = dma("sync", f"LX{b}", x_bufs[b].ap(),
                       x_in[tt * 128:(tt + 1) * 128, :], tag=f"xld{tt}")

    gmm_tok = [None, None]
    evx_tok = [None, None]
    prev_tr_tok = None
    NT = 1 if "t1" in feats else TT
    stats_tok = [None] * max(NT, 1)

    def phase_a(tt):
        b = tt % 2
        xb = x_bufs[b]
        this_x_tok = x_tok[b]
        tok = None
        for qq in range(4):
            tok = vop(lambda eng, xb=xb, tt=tt, qq=qq: eng.bn_stats(
                lnst.ap()[:, tt * 24 + qq * 6: tt * 24 + (qq + 1) * 6],
                xb.ap()[:, qq * 512:(qq + 1) * 512]),
                waits=[this_x_tok] if qq == 0 else (), tag=f"lnA{tt}_{qq}")
        stats_tok[tt] = vop(lambda eng, tt=tt: eng.bn_aggr(
            lnst.ap()[:, 192 + 2 * tt: 194 + 2 * tt],
            lnst.ap()[:, tt * 24: (tt + 1) * 24]),
            waits=[tok], tag=f"lnAg{tt}")
        x_free[b] = stats_tok[tt]

    def phase_b(t0, n):
        mv = lnst.ap()[:, 192 + 2 * t0:192 + 2 * (t0 + n)].rearrange(
            "p (t two) -> p t two", two=2)
        var8 = mv[:, :, 1:2]
        ve8 = lnst.ap()[:, 208 + t0:208 + t0 + n]
        rc8 = lnst.ap()[:, 216 + t0:216 + t0 + n]
        rs8 = lnst.ap()[:, 224 + t0:224 + t0 + n]
        n1 = lnst.ap()[:, 232 + t0:232 + t0 + n]
        n2 = lnst.ap()[:, 240 + t0:240 + t0 + n]
        cB = VChain(stats_tok[t0 + n - 1])
        cB.v(lambda eng: eng.tensor_scalar_add(ve8, var8, 1e-5), tag=f"lnBve{t0}")
        cB.v(lambda eng: eng.reciprocal(rc8, ve8), tag=f"lnBrc{t0}")
        cB.s(lambda eng: eng.activation(rs8, rc8, AF.Sqrt), tag=f"lnBsq{t0}")
        cB.v(lambda eng: eng.tensor_tensor(n1, rs8, rs8, OP.mult), tag=f"lnBn1{t0}")
        cB.v(lambda eng: eng.tensor_tensor(n2, ve8, n1, OP.mult), tag=f"lnBn2{t0}")
        cB.v(lambda eng: eng.tensor_scalar(n2, n2, -0.5, 1.5, OP.mult, OP.add),
             tag=f"lnBn3{t0}")
        return cB.v(lambda eng: eng.tensor_tensor(rs8, rs8, n2, OP.mult),
                    tag=f"lnBn4{t0}")

    n0 = min(4, NT)
    for tt in range(n0):
        b = tt % 2
        if tt >= 2:
            x_tok[b] = dma("sync", f"LX{b}", x_bufs[b].ap(),
                           x_in[tt * 128:(tt + 1) * 128, :],
                           waits=[x_free[b]], tag=f"xldA{tt}")
        phase_a(tt)
    tokB = {0: phase_b(0, n0)}
    for tt in range(NT):
        tok_rstd = tokB[tt // 4]
        b = tt % 2
        xb = x_bufs[b]
        this_x_tok = dma("sync", f"LX{b}", xb.ap(),
                         x_in[tt * 128:(tt + 1) * 128, :],
                         waits=[x_free[b]], tag=f"xldC{tt}")
        x_free[b] = vop(lambda eng, xb=xb, tt=tt: eng.tensor_scalar(
            xn_scr.ap(), xb.ap(), lnst.ap()[:, 192 + 2 * tt:193 + 2 * tt],
            lnst.ap()[:, 224 + tt:225 + tt], OP.subtract, OP.mult),
            waits=[this_x_tok, tok_rstd, prev_tr_tok], tag=f"lnXn{tt}")
        xn_ready = x_free[b]

        pend = None
        for dc in range(DC if "notrans" not in feats else 0):
            tb = dc % 2
            xf = dc % 2
            ttok = trans(pb[tb][:, 0:128],
                         xn_scr.ap()[:, dc * 128:(dc + 1) * 128],
                         identf.ap(), tb,
                         waits=[xn_ready, tok_const_s], tag=f"trx{tt}_{dc}")
            sc_tok = evac("scalar",
                          lambda eng, tb=tb, xf=xf: eng.copy(
                              xnf32[xf].ap(), pb[tb][:, 0:128]),
                          tb, waits=[gmm_tok[xf], evx_tok[xf]],
                          tag=f"evf{tt}_{dc}")
            evx_tok[xf] = vop(
                lambda eng, tt=tt, dc=dc, xf=xf: eng.tensor_copy(
                    xn_fm.ap()[:, dc * T + tt * 128: dc * T + (tt + 1) * 128],
                    xnf32[xf].ap()),
                waits=[sc_tok], tag=f"evx{tt}_{dc}")
            if pend is not None and "nogate" not in feats:
                pdc, pxf, ptok = pend
                gb = 6 + (tt // 4)
                col = (tt % 4) * 128
                gmm_tok[pxf] = S.op(
                    "tensor",
                    lambda eng, pdc=pdc, pxf=pxf, gb=gb, col=col: eng.matmul(
                        pb[gb][0:E, col:col + 128],
                        gwT_sb.ap()[:, pdc * E:(pdc + 1) * E],
                        xnf32[pxf].ap(),
                        start=(pdc == 0), stop=(pdc == DC - 1)),
                    waits=[ptok], inc=("MM" if pdc == DC - 1 else None),
                    tag=f"gmm{tt}_{pdc}")
            pend = (dc, xf, sc_tok)
        if "nogate" not in feats and "notrans" not in feats:
          pdc, pxf, ptok = pend
          gb = 6 + (tt // 4)
          col = (tt % 4) * 128
          gmm_tok[pxf] = S.op(
            "tensor",
            lambda eng, pdc=pdc, pxf=pxf, gb=gb, col=col: eng.matmul(
                pb[gb][0:E, col:col + 128],
                gwT_sb.ap()[:, pdc * E:(pdc + 1) * E],
                xnf32[pxf].ap(),
                start=(pdc == 0), stop=(pdc == DC - 1)),
            waits=[ptok], inc=("MM" if pdc == DC - 1 else None),
            tag=f"gmm{tt}_{pdc}")
        if tt == 3 and "nogate" not in feats:
            bank_data[6] = gmm_tok[1]
        if tt == 7 and "nogate" not in feats:
            bank_data[7] = gmm_tok[1]
        prev_tr_tok = S.tok("MM")
        if NT > 4 and tt < 4:
            bA = (tt + 4) % 2
            x_tok[bA] = dma("sync", f"LX{bA}", x_bufs[bA].ap(),
                            x_in[(tt + 4) * 128:(tt + 5) * 128, :],
                            waits=[x_free[bA]], tag=f"xldA{tt + 4}")
            phase_a(tt + 4)
            if tt == 3:
                tokB[1] = phase_b(4, 4)
    tok_xn = {0: None, 1: None}

    tok_xn_all = S.tok("EV")
    tok_msagg = vop(lambda eng: eng.memset(s_agg.ap(), 0.0),
                    waits=[S.tok("MM")], tag="msagg")

    if "nogate" in feats or "notrans" in feats:
        return _finish()
    e0 = evac("vector", lambda eng: eng.tensor_copy(
        logit_sb.ap()[:, 0:512], pb[6][0:E, 0:512]), 6, tag="evg0")
    tok_logits = evac("vector", lambda eng: eng.tensor_copy(
        logit_sb.ap()[:, 512:1024], pb[7][0:E, 0:512]), 7, tag="evg1")

    for tt in range(TT):
        tb = tt % 2
        trans(pb[tb][0:128, 0:E],
              logit_sb.ap()[:, tt * 128:(tt + 1) * 128],
              identf.ap()[0:E, 0:E], tb, waits=[tok_logits], tag=f"trg{tt}")
        evac("vector", lambda eng, tt=tt, tb=tb: eng.tensor_copy(
            score_tm.ap()[:, tt * E:(tt + 1) * E], pb[tb][0:128, 0:E]),
            tb, tag=f"evsc{tt}")
    tok_scores = S.tok("EV")
    tok_sig = sop(lambda eng: eng.activation(
        score_tm.ap(), score_tm.ap(), AF.Sigmoid),
        waits=[tok_scores], tag="sig")
    for tt in range(TT):
        sc_ap = score_tm.ap()[:, tt * E:(tt + 1) * E]
        y_ap = g_scr.ap()[:, 0:E]
        m1_ap = g_scr.ap()[:, E:2 * E]
        y2_ap = g_scr.ap()[:, 2 * E:3 * E]
        mx1 = mu_sc.ap()[:, 0:1]
        mx2 = mu_sc.ap()[:, 1:2]
        c = VChain()
        c.v(lambda eng, sc_ap=sc_ap, y_ap=y_ap: eng.tensor_add(
            y_ap, sc_ap, ebias_sb.ap()), waits=[tok_sig], tag=f"gy{tt}")
        c.v(lambda eng, y_ap=y_ap, mx1=mx1: eng.reduce_max(mx1, y_ap, axis=AX),
            tag=f"gm1{tt}")
        c.v(lambda eng, y_ap=y_ap, m1_ap=m1_ap, mx1=mx1:
            eng.tensor_single_scalar(m1_ap, y_ap, mx1, OP.is_equal),
            tag=f"geq{tt}")
        c.v(lambda eng, m1_ap=m1_ap, y_ap=y_ap, y2_ap=y2_ap:
            eng.scalar_tensor_tensor(y2_ap, m1_ap, -1e30, y_ap,
                                     OP.mult, OP.add), tag=f"gy2{tt}")
        c.v(lambda eng, y2_ap=y2_ap, mx2=mx2: eng.reduce_max(
            mx2, y2_ap, axis=AX), tag=f"gm2{tt}")
        c.v(lambda eng, y_ap=y_ap, m1_ap=m1_ap, mx2=mx2:
            eng.tensor_single_scalar(m1_ap, y_ap, mx2, OP.is_ge),
            tag=f"gmask{tt}")
        c.v(lambda eng, sc_ap=sc_ap, m1_ap=m1_ap, tt=tt: eng.tensor_tensor(
            g_tm2.ap()[:, tt * E:(tt + 1) * E], sc_ap, m1_ap, OP.mult),
            tag=f"gg{tt}")
    tok_gtm = S.tok("EV")

    slot_free = [None] * 4
    slot_rr = [0]

    def wload(panel_ap, n_dc, tag=""):
        s = slot_rr[0]
        slot_rr[0] = (s + 1) % 4
        tok = dma("gpsimd", f"LW{s}",
                  wslot[s].ap()[:, 0:n_dc * 128].rearrange(
                      "p (c n) -> p c n", c=n_dc),
                  panel_ap, waits=[slot_free[s]], tag=f"wl{tag}")
        return s, tok

    def wpanel(src2d, c0, c1, p=128):
        return src2d[:, c0:c1].rearrange("(c p) n -> p c n", p=p)

    rope_prev = {"q2": None, "ex": None}

    def rope_elu(mb, src_cols_th, dst_ap, tag):
        th = src_cols_th
        t0 = th * 512
        qtok = evac("scalar", lambda eng, mb=mb: eng.copy(
            qr_s.ap(), pb[mb][:, 0:512]), mb,
            waits=[rope_prev["q2"]], tag=f"{tag}qr")
        rb = mb
        mmg([(pb[rb][:, 0:512], protT.ap(), qr_s.ap())], rb,
            waits=[qtok], tag=f"{tag}rot")
        ch = VChain()
        ch.v(lambda eng, t0=t0: eng.tensor_tensor(
            rp_s.ap(), qr_s.ap(), cosf.ap()[:, t0:t0 + 512], OP.mult),
            waits=[qtok, tok_const_g, rope_prev["ex"]], tag=f"{tag}t1")
        ch.tok = evac("vector", lambda eng, t0=t0, rb=rb, _p=ch.tok:
                      eng.tensor_tensor(
                          rq_s.ap(), pb[rb][:, 0:512],
                          sinf.ap()[:, t0:t0 + 512], OP.mult),
                      rb, waits=[ch.tok], tag=f"{tag}t2")
        t_qp = ch.v(lambda eng: eng.tensor_add(
            rp_s.ap(), rp_s.ap(), rq_s.ap()), tag=f"{tag}qp")
        t_ex = sop(lambda eng: eng.activation(ex_s.ap(), rp_s.ap(), AF.Exp),
                   waits=[t_qp], tag=f"{tag}ex")
        ch.v(lambda eng: eng.tensor_scalar_max(rl_s.ap(), rp_s.ap(), 0.0),
             tag=f"{tag}rl")
        t_q2 = ch.v(lambda eng, dst_ap=dst_ap: eng.scalar_tensor_tensor(
            dst_ap, ex_s.ap(), 1.0, rl_s.ap(), OP.min, OP.add),
            waits=[t_ex], tag=f"{tag}q2")
        rope_prev["q2"] = t_q2
        rope_prev["ex"] = t_ex
        return t_q2

    if upto < 2:
        return _finish()
    for h in range(H):
        s, wtok = wload(wpanel(wqkvT, h * 128, (h + 1) * 128), DC, tag=f"q{h}")
        last_mm = None
        for th in range(TH):
            mb = 2 + th
            items = [(pb[mb][:, 0:512],
                      wslot[s].ap()[:, dc * 128:(dc + 1) * 128],
                      xn_fm.ap()[:, dc * T + th * 512: dc * T + (th + 1) * 512])
                     for dc in range(DC)]
            last_mm = mmg(items, mb, waits=[wtok, tok_xn_all],
                          tag=f"qmm{h}_{th}")
            rope_elu(mb, th, q_fm.ap()[:, h * T + th * 512: h * T + (th + 1) * 512],
                     tag=f"q{h}_{th}_")
        slot_free[s] = last_mm
    tok_q = S.tok("EV")

    if upto < 3:
        return _finish()
    agg_prev = {}
    for h in range(H):
        par = h % 2
        kf, kt, vt, vf, ats = k_fm[par], k_tm[par], v_tm[par], v_fs[par], att_s[par]
        kcol = 2048 + h * 128
        vcol = 4096 + h * 128
        sk, ktok = wload(wpanel(wqkvT, kcol, kcol + 128), DC, tag=f"k{h}")
        sv, vtok = wload(wpanel(wqkvT, vcol, vcol + 128), DC, tag=f"v{h}")
        last_kmm = None
        for th in range(TH):
            mb = th
            items = [(pb[mb][:, 0:512],
                      wslot[sk].ap()[:, dc * 128:(dc + 1) * 128],
                      xn_fm.ap()[:, dc * T + th * 512: dc * T + (th + 1) * 512])
                     for dc in range(DC)]
            last_kmm = mmg(items, mb, waits=[ktok, tok_xn_all],
                           tag=f"kmm{h}_{th}")
            rope_elu(mb, th, kf.ap()[:, th * 512:(th + 1) * 512],
                     tag=f"k{h}_{th}_")
        slot_free[sk] = last_kmm
        tok_k = S.tok("EV")

        last_vmm = None
        for th in range(TH):
            mb = th
            items = [(pb[mb][:, 0:512],
                      wslot[sv].ap()[:, dc * 128:(dc + 1) * 128],
                      xn_fm.ap()[:, dc * T + th * 512: dc * T + (th + 1) * 512])
                     for dc in range(DC)]
            last_vmm = mmg(items, mb, waits=[vtok, tok_xn_all],
                           tag=f"vmm{h}_{th}")
            evac("scalar", lambda eng, th=th, mb=mb, vf=vf: eng.copy(
                vf.ap()[:, th * 512:(th + 1) * 512], pb[mb][:, 0:512]),
                mb, tag=f"evv{h}_{th}")
        slot_free[sv] = last_vmm
        tok_vfs = S.tok("EV")

        for src_buf, dst_buf, wtk, nm in ((kf, kt, tok_k, "k"),
                                          (vf, vt, tok_vfs, "v")):
            for g in range(2):
                tb = g % 2
                for i in range(4):
                    tt = g * 4 + i
                    trans(pb[tb].bitcast(BF16)[:, i * 128:(i + 1) * 128],
                          src_buf.ap()[:, tt * 128:(tt + 1) * 128],
                          identb.ap(), tb,
                          waits=[wtk] if i == 0 else (),
                          tag=f"tr{nm}{h}_{tt}")
                evac("scalar",
                     lambda eng, g=g, tb=tb, dst_buf=dst_buf: eng.copy(
                         dst_buf.ap()[:, g * 512:(g + 1) * 512],
                         pb[tb].bitcast(BF16)[:, 0:512]),
                     tb, tag=f"ev{nm}t{h}_{g}")
        tok_kv = S.tok("EV")

        for c in range(CH):
            ab = 2 + (c % 2)
            nb = 4 + (c % 2)
            gbk = 6 + (c % 2)

            for ms in range(2):
                mmg([(pb[ab][:, ms * 256:(ms + 1) * 256],
                      kf.ap()[:, c * 256 + ms * 128: c * 256 + (ms + 1) * 128],
                      q_fm.ap()[:, h * T + c * 256: h * T + (c + 1) * 256])],
                    ab, waits=[tok_kv, tok_q], tag=f"att{h}_{c}_{ms}")
            t_att = evac("vector", lambda eng, ab=ab, ats=ats: eng.tensor_tensor(
                ats.ap(), pb[ab][:, 0:512], trilm.ap(), OP.mult),
                ab, tag=f"evatt{h}_{c}")

            nitems = [(pb[nb][:, 0:256],
                       vt.ap()[:, (2 * c + ms) * 128:(2 * c + ms + 1) * 128],
                       ats.ap()[:, ms * 256:(ms + 1) * 256])
                      for ms in range(2)]
            mmg(nitems, nb, waits=[t_att], tag=f"num{h}_{c}")
            t_num_mm = bank_data[nb]
            ditems = [(pb[nb][0:1, 256:512],
                       ones_b.ap(),
                       ats.ap()[:, ms * 256:(ms + 1) * 256])
                      for ms in range(2)]
            for j, (o, l, r) in enumerate(ditems):
                S.op("tensor",
                     lambda eng, o=o, l=l, r=r, j=j: eng.matmul(
                         o, l, r, start=(j == 0), stop=(j == 1)),
                     inc=("MM" if j == 1 else None), tag=f"den{h}_{c}_{j}")
            t_den_mm = S.tok("MM")
            bank_data[nb] = t_den_mm

            evac("scalar", lambda eng, h=h, c=c, nb=nb: eng.copy(
                n_acc.ap()[:, h * T + c * 256: h * T + (c + 1) * 256],
                pb[nb][:, 0:256]), nb, waits=[t_num_mm],
                tag=f"evnum{h}_{c}")
            evac("scalar", lambda eng, h=h, c=c, nb=nb: eng.copy(
                den_i.ap()[0:1, h * T + c * 256: h * T + (c + 1) * 256],
                pb[nb][0:1, 256:512]), nb, tag=f"evden{h}_{c}")

            gitems = [(pb[gbk][:, 0:128],
                       kt.ap()[:, (2 * c + ms) * 128:(2 * c + ms + 1) * 128],
                       vt.ap()[:, (2 * c + ms) * 128:(2 * c + ms + 1) * 128])
                      for ms in range(2)]
            mmg(gitems, gbk, waits=[tok_kv], tag=f"G{h}_{c}")
            zitems = [(pb[gbk][:, 128:129],
                       kt.ap()[:, (2 * c + ms) * 128:(2 * c + ms + 1) * 128],
                       ones_b.ap())
                      for ms in range(2)]
            for j, (o, l, r) in enumerate(zitems):
                S.op("tensor",
                     lambda eng, o=o, l=l, r=r, j=j: eng.matmul(
                         o, l, r, start=(j == 0), stop=(j == 1)),
                     inc=("MM" if j == 1 else None), tag=f"z{h}_{c}_{j}")
            bank_data[gbk] = S.tok("MM")

            agg_prev[h] = vop(lambda eng, h=h, gbk=gbk: eng.scalar_tensor_tensor(
                s_agg.ap()[:, h * SW:(h + 1) * SW],
                pb[gbk][:, 0:129], 1.0,
                s_agg.ap()[:, h * SW:(h + 1) * SW], OP.mult, OP.add),
                waits=[bank_data[gbk], tok_msagg, agg_prev.get(h)],
                tag=f"agg{h}_{c}")
            t_g = evac("vector", lambda eng, h=h, c=c, gbk=gbk: eng.tensor_copy(
                g_raw.ap()[:, (h * CH + c) * SW:(h * CH + c + 1) * SW],
                pb[gbk][:, 0:129]), gbk, tag=f"evG{h}_{c}")
    tok_p3_v = S.tok("EV")
    tok_p3_sc = S.tok("SC")
    tok_p3_mm = S.tok("MM")

    if upto < 4:
        return _finish()
    _pre = {hc: (wload(wpanel(sw1T, hc * 128, (hc + 1) * 128), DC,
                       tag=f"su1_{hc}p"),
                 wload(wpanel(sw3T, hc * 128, (hc + 1) * 128), DC,
                       tag=f"su3_{hc}p"))
            for hc in (0, 1)}
    uh_prev = {"h": tok_p3_v}

    def u_phase(hcp, xsrc, xtok, m1src, m3src, tagp, preload=None):
        group = 0 if hcp % 2 == 0 else 4
        for hc in (2 * hcp, 2 * hcp + 1):
            if preload is not None and hc in preload:
                (s1s, w1tok), (s3s, w3tok) = preload[hc]
            else:
                s1s, w1tok = wload(wpanel(m1src, hc * 128, (hc + 1) * 128), DC,
                                   tag=f"{tagp}u1_{hc}")
                s3s, w3tok = wload(wpanel(m3src, hc * 128, (hc + 1) * 128), DC,
                                   tag=f"{tagp}u3_{hc}")
            for th in range(TH):
                b1 = group + th
                b3 = group + 2 + th
                items1 = [(pb[b1][:, 0:512],
                           wslot[s1s].ap()[:, dc * 128:(dc + 1) * 128],
                           xsrc.ap()[:, dc * T + th * 512: dc * T + (th + 1) * 512])
                          for dc in range(DC)]
                mmg(items1, b1, waits=[w1tok, xtok], tag=f"{tagp}u1mm{hc}_{th}")
                items3 = [(pb[b3][:, 0:512],
                           wslot[s3s].ap()[:, dc * 128:(dc + 1) * 128],
                           xsrc.ap()[:, dc * T + th * 512: dc * T + (th + 1) * 512])
                          for dc in range(DC)]
                mmg(items3, b3, waits=[w3tok, xtok], tag=f"{tagp}u3mm{hc}_{th}")
                t_s1 = evac("scalar", lambda eng, b1=b1: eng.activation(
                    s1_scr.ap(), pb[b1][:, 0:512], AF.Silu), b1,
                    waits=[uh_prev["h"]], tag=f"{tagp}s1{hc}_{th}")
                t_t3 = evac("scalar", lambda eng, b3=b3: eng.copy(
                    t3_scr.ap(), pb[b3][:, 0:512]), b3,
                    tag=f"{tagp}t3{hc}_{th}")
                uh_prev["h"] = vop(lambda eng, hc=hc, th=th: eng.tensor_tensor(
                    h_e.ap()[:, hc * T + th * 512: hc * T + (th + 1) * 512],
                    s1_scr.ap(), t3_scr.ap(), OP.mult),
                    waits=[t_s1, t_t3], tag=f"{tagp}h{hc}_{th}")
            slot_free[s1s] = S.tok("MM")
            slot_free[s3s] = S.tok("MM")

    rk = {"i": 0, "acc": [None, None], "free": [None, None]}
    ccbox = {}

    def plan_rank_batch(n):
        for _ in range(n):
            i = rk["i"]
            if i >= 2 * NCORES:
                return
            rk["i"] += 1
            r, hh = i // 2, i % 2
            rtok = dma("sync", "RK0",
                       rank_h[hh].ap().rearrange("p (h s) -> p h s", h=8),
                       agg_out[r][hh * 1024:(hh + 1) * 1024].rearrange(
                           "(h p) s -> p h s", p=128),
                       waits=[ccbox["cc"], rk["free"][hh]], tag=f"rk{r}_{hh}")
            rk["acc"][hh] = vop(lambda eng, r=r, hh=hh:
                                eng.scalar_tensor_tensor(
                s_agg.ap()[:, hh * 8 * SW:(hh + 1) * 8 * SW],
                rank_h[hh].ap(), pmask_sb.ap()[:, r:r + 1],
                s_agg.ap()[:, hh * 8 * SW:(hh + 1) * 8 * SW],
                OP.mult, OP.add),
                waits=[rtok, rk["acc"][hh]], tag=f"rka{r}_{hh}")
            rk["free"][hh] = rk["acc"][hh]

    for hcp in range(HC // 2):
        u_phase(hcp, xn_fm, tok_xn_all, sw1T, sw3T, "su",
                preload=_pre if hcp == 0 else None)
        if hcp == 1:
            t_aggd = dma("sync", "LA",
                         agg_in.ap().rearrange("(h p) s -> p h s", p=128),
                         s_agg.ap().rearrange("p (h s) -> p h s", h=H),
                         waits=[agg_prev[H - 1]], tag="aggout")

            def f_cc(eng):
                return eng.collective_compute(
                    "AllGather", OP.bypass,
                    replica_groups=[list(range(NCORES))],
                    ins=[agg_in.ap().opt()], outs=[agg_out.ap().opt()])
            ccbox["cc"] = S.op("gpsimd", f_cc, waits=[t_aggd], inc="CC",
                               tag="cc")
            t_clr = vop(lambda eng: eng.memset(s_agg.ap(), 0.0),
                        waits=[t_aggd], tag="clr_sagg")
            rk["acc"] = [t_clr, t_clr]
            rk["free"] = [t_clr, t_clr]
    plan_rank_batch(2 * NCORES)
    tok_hsh = S.tok("EV")

    acc_half = rk["acc"]
    prev_cast_mm = [None, None]
    prev_num_rd = [None, None]
    prev_den_rd = [None, None]
    prev_dbc = [None, None]
    init_free = [list(bank_free[i]) for i in range(8)]
    for i in range(8):
        bank_free[i] = []
    ones1 = ones_b.ap()[0:1, 0:1]

    def plan_chainA(h):
        p = h % 2
        ca = VChain()
        cast_toks = []
        for c in range(CH):
            w = [acc_half[h // 8], prev_cast_mm[p]] if c == 0 else []
            cast_toks.append(ca.v(
                lambda eng, h=h, c=c, p=p: eng.tensor_copy(
                    sb4[p].ap()[:, c * SW:(c + 1) * SW],
                    s_agg.ap()[:, h * SW:(h + 1) * SW]),
                waits=w, tag=f"cs{h}_{c}"))
            if c < CH - 1:
                ca.v(lambda eng, h=h, c=c: eng.tensor_tensor(
                    s_agg.ap()[:, h * SW:(h + 1) * SW],
                    s_agg.ap()[:, h * SW:(h + 1) * SW],
                    g_raw.ap()[:, (h * CH + c) * SW:(h * CH + c + 1) * SW],
                    OP.add), tag=f"sr{h}_{c}")
        return cast_toks

    all_casts = {0: plan_chainA(0)}
    pipe = {}

    def plan_dbc_ao(h):
        st = pipe.pop(h)
        p = h % 2
        pdD = pd[2 + p].ap()
        dbc_toks = []
        for half in range(2):
            tok = S.op("tensor", lambda eng, half=half, pdD=pdD: eng.matmul(
                pdD[:, half * 512:(half + 1) * 512], ones_r.ap(),
                st["rc_ap"][0:1, half * 512:(half + 1) * 512],
                start=True, stop=True),
                waits=[st["rc"]], inc="MM", tag=f"dbc{h}_{half}")
            dbc_toks.append(tok)
        prev_dbc[p] = dbc_toks[-1]
        t_ao = vop(lambda eng, h=h, p=p, pdD=pdD: eng.tensor_tensor(
            n_acc.ap()[:, h * T:(h + 1) * T], at_b[p].ap(), pdD[:, 0:1024],
            OP.mult), waits=[dbc_toks[-1], st["nt"]], tag=f"ao{h}")
        prev_den_rd[p] = t_ao
        return dbc_toks[-1]

    for h in range(H):
        p = h % 2
        pdN = pd[p].ap()
        pdD = pd[2 + p].ap()
        q_h = q_fm.ap()[:, h * T:(h + 1) * T]
        cast_toks = all_casts[h]
        if h - 2 in pipe:
            plan_dbc_ao(h - 2)
        for c in range(CH):
            wN = [cast_toks[c], prev_num_rd[p]] + (init_free[2 * p] +
                 init_free[2 * p + 1] if c == 0 and h < 2 else [])
            S.op("tensor", lambda eng, h=h, c=c, p=p, pdN=pdN, q_h=q_h:
                 eng.matmul(pdN[:, c * 256:(c + 1) * 256],
                            sb4[p].ap()[:, c * SW:c * SW + 128],
                            q_h[:, c * 256:(c + 1) * 256],
                            start=True, stop=True),
                 waits=wN, inc="MM", tag=f"xn{h}_{c}")
            wD = [prev_den_rd[p], tok_p3_sc] + (init_free[4 + 2 * p] +
                 init_free[5 + 2 * p] if c == 0 and h < 2 else [])
            S.op("tensor", lambda eng, h=h, c=c, pdD=pdD: eng.matmul(
                pdD[0:1, c * 256:(c + 1) * 256], ones1,
                den_i.ap()[0:1, h * T + c * 256: h * T + (c + 1) * 256],
                start=True, stop=False),
                waits=wD, tag=f"xdi{h}_{c}")
            S.op("tensor", lambda eng, h=h, c=c, p=p, pdD=pdD, q_h=q_h:
                 eng.matmul(pdD[0:1, c * 256:(c + 1) * 256],
                            sb4[p].ap()[:, c * SW + 128:c * SW + 129],
                            q_h[:, c * 256:(c + 1) * 256],
                            start=False, stop=True),
                 inc="MM", tag=f"xdz{h}_{c}")
        t_mms = S.tok("MM")
        prev_cast_mm[p] = t_mms
        if h + 1 < H:
            all_casts[h + 1] = plan_chainA(h + 1)
        rc_ap = (rc_b if p == 0 else rc_s2).ap()[0:1, 0:1024]
        dn_ap = dn_b.ap()[0:1, 0:1024]
        t_mx = vop(lambda eng, pdD=pdD, dn_ap=dn_ap: eng.tensor_scalar_max(
            dn_ap, pdD[0:1, 0:1024], 1e-6),
            waits=[t_mms, prev_dbc[p]], tag=f"mx{h}")
        def f_rc(eng, rc_ap=rc_ap, dn_ap=dn_ap):
            with eng.bass.allow_low_precision(
                    reason="bf16 den/recip: within calibrated tolerance"):
                return eng.reciprocal(rc_ap, dn_ap)
        t_rc = vop(f_rc, waits=[t_mx, prev_dbc[p]], tag=f"rc{h}")
        t_nt = vop(lambda eng, h=h, p=p, pdN=pdN: eng.scalar_tensor_tensor(
            at_b[p].ap(), pdN[:, 0:1024], 1.0,
            n_acc.ap()[:, h * T:(h + 1) * T], OP.mult, OP.add),
            waits=[t_mms], tag=f"nt{h}")
        prev_num_rd[p] = t_nt
        pipe[h] = {"rc": t_rc, "rc_ap": rc_ap, "nt": t_nt}
    plan_dbc_ao(H - 2)
    plan_dbc_ao(H - 1)
    for p in range(2):
        bank_free[2 * p].append(prev_num_rd[p])
        bank_free[2 * p + 1].append(prev_num_rd[p])
        bank_free[4 + 2 * p].append(prev_den_rd[p])
        bank_free[5 + 2 * p].append(prev_den_rd[p])
    tok_attn = S.tok("EV")

    if upto < 5:
        return _finish()
    y_banks = [0, 1, 2, 3]
    for oc in range(OC):
        s, wtok = wload(wpanel(woutT, oc * 128, (oc + 1) * 128), DC,
                        tag=f"wo{oc}")
        last_mm = None
        for th in range(TH):
            yb = y_banks[(oc * TH + th) % 4]
            items = [(pb[yb][:, 0:512],
                      wslot[s].ap()[:, hh * 128:(hh + 1) * 128],
                      n_acc.ap()[:, hh * T + th * 512: hh * T + (th + 1) * 512])
                     for hh in range(H)]
            last_mm = mmg(items, yb, waits=[wtok, tok_attn], tag=f"womm{oc}_{th}")
            evac("vector", lambda eng, oc=oc, th=th, yb=yb: eng.tensor_copy(
                y_acc.ap()[:, oc * T + th * 512: oc * T + (th + 1) * 512],
                pb[yb][:, 0:512]), yb, tag=f"evwo{oc}_{th}")
        slot_free[s] = last_mm
    tok_y0 = S.tok("EV")

    if upto < 6:
        return _finish()
    for oc in range(OC):
        s, wtok = wload(wpanel(sw2T, oc * 128, (oc + 1) * 128), HC,
                        tag=f"sw2_{oc}")
        last_mm = None
        for th in range(TH):
            yb = y_banks[(oc * TH + th) % 4]
            items = [(pb[yb][:, 0:512],
                      wslot[s].ap()[:, hc * 128:(hc + 1) * 128],
                      h_e.ap()[:, hc * T + th * 512: hc * T + (th + 1) * 512])
                     for hc in range(HC)]
            last_mm = mmg(items, yb, waits=[wtok, tok_hsh, tok_y0],
                          tag=f"symm{oc}_{th}")
            evac("vector", lambda eng, oc=oc, th=th, yb=yb: eng.tensor_tensor(
                y_acc.ap()[:, oc * T + th * 512: oc * T + (th + 1) * 512],
                y_acc.ap()[:, oc * T + th * 512: oc * T + (th + 1) * 512],
                pb[yb][:, 0:512], OP.add), yb, tag=f"evsy{oc}_{th}")
        slot_free[s] = last_mm

    if upto < 7:
        return _finish()
    prev_gbc_gate = [tok_attn]

    def plan_gbc_xng(e, extra_wait):
        gb_toks = []
        for th in range(TH):
            gband = 6 + th
            for i, tt in enumerate(range(th * 4, th * 4 + 4)):
                trans(pb[gband][0:1, i * 128:(i + 1) * 128],
                      g_tm2.ap()[:, tt * E + e: tt * E + e + 1],
                      identf.ap(), gband, waits=[tok_gtm], tag=f"gtr{e}_{tt}")
            t_gs = evac("vector", lambda eng, gband=gband: eng.tensor_copy(
                g_stage.ap(), pb[gband][0:1, 0:512]), gband,
                waits=[prev_gbc_gate[0]], tag=f"gst{e}_{th}")
            bb = 6 + th
            mmg([(pb[bb][:, 0:512], ones_r.ap(), g_stage.ap())], bb,
                waits=[t_gs], tag=f"gbb{e}_{th}")
            gb_toks.append(evac("vector",
                                lambda eng, th=th, bb=bb: eng.tensor_copy(
                                    gbc.ap()[:, th * 512:(th + 1) * 512],
                                    pb[bb][:, 0:512]), bb,
                                waits=[prev_gbc_gate[0]], tag=f"gbe{e}_{th}"))
        xng_tok = None
        for dc in range(DC):
            xng_tok = vop(lambda eng, dc=dc: eng.tensor_tensor(
                xng.ap()[:, dc * T:(dc + 1) * T],
                xn_fm.ap()[:, dc * T:(dc + 1) * T], gbc.ap(), OP.mult),
                waits=(gb_toks + [extra_wait]) if dc == 0 else (),
                tag=f"xng{e}_{dc}")
        prev_gbc_gate[0] = xng_tok
        return xng_tok

    xng_tok = plan_gbc_xng(0, None)
    for e in range(E):
        for hcp in range(HC // 2):
            u_phase(hcp, xng, xng_tok, w1T[e], w3T[e], f"e{e}")
        tok_umm = S.tok("MM")
        tok_he = S.tok("EV")
        if e + 1 < E:
            xng_tok = plan_gbc_xng(e + 1, tok_umm)
        for oc in range(OC):
            s, wtok = wload(wpanel(w2T[e], oc * 128, (oc + 1) * 128), HC,
                            tag=f"e{e}w2_{oc}")
            last_mm = None
            for th in range(TH):
                yb = y_banks[(oc * TH + th) % 4]
                items = [(pb[yb][:, 0:512],
                          wslot[s].ap()[:, hc * 128:(hc + 1) * 128],
                          h_e.ap()[:, hc * T + th * 512: hc * T + (th + 1) * 512])
                         for hc in range(HC)]
                last_mm = mmg(items, yb, waits=[wtok, tok_he],
                              tag=f"e{e}ymm{oc}_{th}")
                evac("vector", lambda eng, oc=oc, th=th, yb=yb: eng.tensor_tensor(
                    y_acc.ap()[:, oc * T + th * 512: oc * T + (th + 1) * 512],
                    y_acc.ap()[:, oc * T + th * 512: oc * T + (th + 1) * 512],
                    pb[yb][:, 0:512], OP.add), yb, tag=f"e{e}ev{oc}_{th}")
            slot_free[s] = last_mm
    tok_yfull = S.tok("EV")

    if upto < 8:
        return _finish()
    tok_mm_p8 = S.tok("MM")
    st_tok = [None, None]
    xr_tok = [None, None]
    xr_free = [None, None]
    for tt in range(TT):
        b = tt % 2
        xr_tok[b] = dma("scalar", f"LX{b}", x_rd[b].ap(),
                        x_in[tt * 128:(tt + 1) * 128, :],
                        waits=[xr_free[b], tok_mm_p8], tag=f"xrd{tt}")
        last_add = None
        for g in range(4):
            tb = g % 4
            for i in range(4):
                oc = g * 4 + i
                trans(pb[tb][:, i * 128:(i + 1) * 128],
                      y_acc.ap()[:, oc * T + tt * 128: oc * T + (tt + 1) * 128],
                      identf.ap(), tb, waits=[tok_yfull], tag=f"try{tt}_{oc}")
            last_add = evac("vector",
                            lambda eng, g=g, tb=tb, b=b: eng.tensor_tensor(
                                out_st[b].ap()[:, g * 512:(g + 1) * 512],
                                pb[tb][:, 0:512],
                                x_rd[b].ap()[:, g * 512:(g + 1) * 512],
                                OP.add),
                            tb, waits=[xr_tok[b], st_tok[b]],
                            tag=f"oadd{tt}_{g}")
        xr_free[b] = last_add
        st_tok[b] = dma("sync", "ST" if b == 0 else "ST1",
                        out_t[tt * 128:(tt + 1) * 128, :],
                        out_st[b].ap(), waits=[last_add], tag=f"ost{tt}")

    S.op("sync", lambda eng: None, waits=[S.tok("ST"), S.tok("ST1")],
         tag="fin")

    S.check()
    return nc, S, SEMS


_CACHE = {}


def _host_constants():
    inv = 1.0 / (10000.0 ** (np.arange(0, 128, 2, dtype=np.float64) / 128.0))
    invf = np.concatenate([inv, inv])
    ident = np.eye(128, dtype=np.float32)
    protT = np.zeros((128, 128), dtype=np.float32)
    for i in range(64):
        protT[2 * i + 1, 2 * i] = -1.0
        protT[2 * i, 2 * i + 1] = 1.0
    trilm = np.zeros((128, 512), dtype=np.float32)
    for s_ in range(2):
        for p in range(128):
            for l in range(256):
                if s_ * 128 + p <= l:
                    trilm[p, s_ * 256 + l] = 1.0
    ones = np.ones((128, 128), dtype=np.float32)
    return invf, ident, protT, trilm, ones


def kernel(L0_attn_x, L0_attn_W_qkv, L0_attn_W_out, L0_ffn_expert_bias,
           L0_ffn_gate_w, L0_ffn_w1, L0_ffn_w2, L0_ffn_w3, L0_ffn_sw1,
           L0_ffn_sw2, L0_ffn_sw3):
    x = np.ascontiguousarray(np.asarray(L0_attn_x, dtype=np.float32)
                             .reshape(NCORES * T, D))
    wqkvT = np.ascontiguousarray(np.asarray(L0_attn_W_qkv, np.float32).T)
    woutT = np.ascontiguousarray(np.asarray(L0_attn_W_out, np.float32).T)
    gwT = np.ascontiguousarray(np.asarray(L0_ffn_gate_w, np.float32).T)
    w1T = np.ascontiguousarray(np.asarray(L0_ffn_w1, np.float32)
                               .transpose(0, 2, 1))
    w3T = np.ascontiguousarray(np.asarray(L0_ffn_w3, np.float32)
                               .transpose(0, 2, 1))
    w2T = np.ascontiguousarray(np.asarray(L0_ffn_w2, np.float32)
                               .transpose(0, 2, 1))
    sw1T = np.ascontiguousarray(np.asarray(L0_ffn_sw1, np.float32).T)
    sw3T = np.ascontiguousarray(np.asarray(L0_ffn_sw3, np.float32).T)
    sw2T = np.ascontiguousarray(np.asarray(L0_ffn_sw2, np.float32).T)
    ebias = np.ascontiguousarray(
        np.broadcast_to(np.asarray(L0_ffn_expert_bias, np.float32), (128, E)))

    invf, ident, protT, trilm, ones = _host_constants()

    if "nc" not in _CACHE:
        nc, S, SEMS = build_program()
        _CACHE["nc"] = nc
        _CACHE["S"] = S
        _CACHE["SEMS"] = SEMS
    nc, S, SEMS = _CACHE["nc"], _CACHE["S"], _CACHE["SEMS"]

    in_maps = []
    for c in range(NCORES):
        pos = np.arange(c * T, (c + 1) * T, dtype=np.float64)
        ang = pos[None, :] * invf[:, None]
        cosf = np.ascontiguousarray(np.cos(ang).astype(np.float32))
        sinf = np.ascontiguousarray(np.sin(ang).astype(np.float32))
        pmask = np.zeros((128, E), np.float32)
        pmask[:, :c] = 1.0
        in_maps.append({
            "x": x[c * T:(c + 1) * T],
            "wqkvT": wqkvT, "woutT": woutT, "gwT": gwT,
            "w1T": w1T, "w3T": w3T, "w2T": w2T,
            "sw1T": sw1T, "sw3T": sw3T, "sw2T": sw2T,
            "ebias": ebias, "pmask": pmask,
            "cosf": cosf, "sinf": sinf,
            "identf": ident, "protT": protT, "trilm": trilm, "onesin": ones,
        })

    res = _run(nc, S, SEMS, in_maps)
    out = np.concatenate([res[c]["out"] for c in range(NCORES)], axis=0)
    return out.reshape(1, NCORES * T, D).astype(np.float32)


def _run(nc, S, SEMS, in_maps):
    if "emitted" not in _CACHE:
        with ExitStack() as ctx:
            sems = {s: ctx.enter_context(nc.semaphore(s)) for s in SEMS}
            block = ctx.enter_context(nc.Block())
            mk = S.emit(block, sems)
            block.tensor(mk("tensor"))
            block.vector(mk("vector"))
            block.scalar(mk("scalar"))
            block.sync(mk("sync"))
            block.gpsimd(mk("gpsimd"))
        _CACHE["emitted"] = True
    res = run_bass_kernel_spmd(nc, in_maps, core_ids=list(range(NCORES)))
    return res.results
